# version 07
# baseline (speedup 1.0000x reference)
import numpy as np
import ml_dtypes
import concourse.bass as bass
import concourse.mybir as mybir
import bass_rust
from concourse.bass_utils import run_bass_kernel_spmd

F32 = mybir.dt.float32
BF16 = mybir.dt.bfloat16
I32 = mybir.dt.int32
AF = mybir.ActivationFunctionType
OP = mybir.AluOpType
NPBF = ml_dtypes.bfloat16

T = 4096
D = 1024
H = 16
DH = 64
FF = 4096
E = 8
R = 4
NCORES = 8
ND = D // 128
SW = 512
NS = T // SW
NT = T // 128
EPS = 1e-5
CEXP = 20.0
NEGBIG = -1e30
MNEG = -60000.0
MW = 4480

ENGINES = ["sync", "tensor", "vector", "scalar", "gpsimd"]
DMA_QUEUES = ("sync", "gpsimd", "scalar")

DRAM_SPECS = {
    "xT":    ([D, T], F32),
    "yT":    ([D, T], F32),
    "x32a":  ([D, T], F32),
    "x32b":  ([D, T], F32),
    "x32c":  ([D, T], F32),
    "n1":    ([D, T], BF16),
    "n2":    ([D, T], BF16),
    "n3":    ([D, T], BF16),
    "n4":    ([D, T], BF16),
    "h":     ([D, T], BF16),
    "qT":    ([D, T], BF16),
    "kT":    ([D, T], BF16),
    "vtok":  ([T, H * 65], BF16),
    "oT":    ([D, T], BF16),
    "y32":   ([D, T], F32),
    "d32":   ([D, T], F32),
    "moe32": ([D, T], F32),
    "g":     ([FF, T], BF16),
    "qext":  ([H, T], BF16),
    "wrowT": ([E, T], F32),
    "rowsA": ([2, T], F32), "rowsB": ([2, T], F32), "rowsC": ([2, T], F32),
    "rowsD": ([2, T], F32), "rowsE": ([2, T], F32), "rowsF": ([2, T], F32),
    "rowsG": ([2, T], F32), "rowsH": ([2, T], F32), "rowsI": ([2, T], F32),
    "rowsJ": ([2, T], F32), "rowsK": ([2, T], F32),
    "L0_Wq": ([D, D], F32), "L0_Wk": ([D, D], F32), "L0_Wv": ([D, D], F32), "L0_Wo": ([D, D], F32),
    "L0_bq": ([D], F32), "L0_bk": ([D], F32), "L0_bv": ([D], F32), "L0_bo": ([D], F32),
    "L0_gamma": ([D], F32), "L0_beta": ([D], F32),
    "F0_wq": ([D, D], F32), "F0_wk": ([D, D], F32), "F0_wv": ([D, D], F32), "F0_wo": ([D, D], F32),
    "F0_w1": ([D, FF], F32), "F0_w2": ([FF, D], F32), "F0_w3": ([D, FF], F32),
    "F0_attn_nw": ([D], F32), "F0_ff_nw": ([D], F32), "F0_moe_nw": ([D], F32),
    "F0_router": ([D, E], F32),
    "F0_ew1": ([E, D, D], F32), "F0_ew2": ([E, D, D], F32), "F0_ew3": ([E, D, D], F32),
    "L1_WQT": ([D, D], F32), "L1_WKT": ([D, D], F32), "L1_WVT": ([D, D], F32), "L1_WOT": ([D, D], F32),
    "L1_bQ": ([D], F32), "L1_bK": ([D], F32), "L1_bV": ([D], F32), "L1_bO": ([D], F32),
    "L1_bias_stripes": ([H, 128, MW], F32),
    "G1_gateT": ([D, E], F32), "G1_gate_b": ([E], F32),
    "G1_wiT": ([D, FF], F32), "G1_wi_b": ([FF], F32),
    "G1_woT": ([FF, D], F32), "G1_wo_b": ([D], F32),
    "G1_AcatT": ([D, E * R], F32), "G1_BcatT": ([E * R, FF], F32),
}

EXTERNAL_IN = {
    "xT", "L0_Wq", "L0_Wk", "L0_Wv", "L0_Wo", "L0_bq", "L0_bk", "L0_bv", "L0_bo",
    "L0_gamma", "L0_beta",
    "F0_wq", "F0_wk", "F0_wv", "F0_wo", "F0_w1", "F0_w2", "F0_w3",
    "F0_attn_nw", "F0_ff_nw", "F0_moe_nw", "F0_router", "F0_ew1", "F0_ew2", "F0_ew3",
    "L1_WQT", "L1_WKT", "L1_WVT", "L1_WOT", "L1_bQ", "L1_bK", "L1_bV", "L1_bO",
    "L1_bias_stripes",
    "G1_gateT", "G1_gate_b", "G1_wiT", "G1_wi_b", "G1_woT", "G1_wo_b",
    "G1_AcatT", "G1_BcatT",
}
EXTERNAL_OUT = {"yT"}


def sk(name, ic, s):
    return f"D:{name}:{ic}:{s}"


def srd(name, s, n=ND):
    return [sk(name, ic, s) for ic in range(n)]


class Ctx:

    NSEM_DMA = 8

    def __init__(self, nc, external_in, external_out):
        self.nc = nc
        self.ops = []
        self.sb = {}
        self.dram = {}
        self._cms = []
        self.ext_in = external_in
        self.ext_out = external_out
        self.reads = set()
        self.writes = set()
        self.last_writer = {}
        self.part_writers = {}
        self.readers_ = {}
        self.eng_last = {e: None for e in ENGINES}

    def d(self, name):
        if name not in self.dram:
            shape, dt = DRAM_SPECS[name]
            if name in self.ext_in:
                kind = "ExternalInput"
            elif name in self.ext_out:
                kind = "ExternalOutput"
            else:
                kind = "Internal"
            self.dram[name] = self.nc.dram_tensor(name, shape, dt, kind=kind).ap()
        if name in self.ext_in:
            self.reads.add(name)
        if name in self.ext_out:
            self.writes.add(name)
        return self.dram[name]

    def alloc(self, name, shape, dt, psum=False):
        mk = self.nc.psum_tensor if psum else self.nc.sbuf_tensor
        cm = mk(name, shape, dt)
        t = cm.__enter__()
        self._cms.append(cm)
        self.sb[name] = t
        return t

    def close(self):
        for cm in reversed(self._cms):
            cm.__exit__(None, None, None)

    def op(self, eng, fn, inc=1, r=(), w=(), pw=()):
        oid = len(self.ops)
        deps = set()
        for k in r:
            j = self.last_writer.get(k)
            if j is not None:
                deps.add(j)
            deps.update(self.part_writers.get(k, ()))
        for k in w:
            j = self.last_writer.get(k)
            if j is not None:
                deps.add(j)
            deps.update(self.part_writers.get(k, ()))
            deps.update(self.readers_.get(k, ()))
        for k in pw:
            j = self.last_writer.get(k)
            if j is not None:
                deps.add(j)
            deps.update(self.readers_.get(k, ()))
        for k in r:
            self.readers_.setdefault(k, set()).add(oid)
        for k in w:
            self.last_writer[k] = oid
            self.part_writers[k] = []
            self.readers_[k] = set()
        for k in pw:
            self.part_writers.setdefault(k, []).append(oid)
        deps.discard(oid)
        self.ops.append((eng, inc, fn, sorted(deps)))
        self.eng_last[eng] = oid

    def dma(self, out, in_, eng="sync", r=(), w=(), pw=(), **kw):
        self.op(eng, lambda e, o=out, i=in_, k=kw: e.dma_start(out=o, in_=i, **k),
                inc=16, r=r, w=w, pw=pw)

    def mm(self, out, lhsT, rhs, start=True, stop=True, r=(), w=(), pw=(), tp=None):
        self.op("tensor", lambda e, o=out, l=lhsT, rr=rhs, s1=start, s2=stop, t=tp:
                e.matmul(o, l, rr, start=s1, stop=s2, tile_position=t), r=r, w=w, pw=pw)

    def transpose(self, out, in_, ident, r=(), w=(), pw=()):
        self.op("tensor", lambda e, o=out, i=in_, d=ident: e.transpose(o, i, d), r=r, w=w, pw=pw)

    def act(self, out, in_, func, bias=None, scale=None, r=(), w=(), pw=()):
        kw = {}
        if bias is not None:
            kw["bias"] = bias
        if scale is not None:
            kw["scale"] = scale
        self.op("scalar", lambda e, o=out, i=in_, f=func, k=kw:
                e.activation(out=o, in_=i, func=f, **k), r=r, w=w, pw=pw)

    def tt(self, out, a, b, o, r=(), w=(), pw=()):
        self.op("vector", lambda e, oo=out, aa=a, bb=b, op_=o:
                e.tensor_tensor(out=oo, in0=aa, in1=bb, op=op_), r=r, w=w, pw=pw)

    def ts(self, out, in0, s1, op0, s2=None, op1=None, r=(), w=(), pw=()):
        kw = dict(scalar2=s2)
        if op1 is not None:
            kw["op1"] = op1
        self.op("vector", lambda e, o=out, i=in0, a=s1, k=kw, p0=op0:
                e.tensor_scalar(out=o, in0=i, scalar1=a, op0=p0, **k), r=r, w=w, pw=pw)

    def stt(self, out, in0, scalar, in1, op0, op1, r=(), w=(), pw=()):
        self.op("vector", lambda e, o=out, i0=in0, s=scalar, i1=in1, p0=op0, p1=op1:
                e.scalar_tensor_tensor(out=o, in0=i0, scalar=s, in1=i1, op0=p0, op1=p1),
                r=r, w=w, pw=pw)

    def copy(self, out, in_, r=(), w=(), pw=()):
        self.op("vector", lambda e, o=out, i=in_: e.tensor_copy(out=o, in_=i), r=r, w=w, pw=pw)

    def memset(self, ap, v, r=(), w=(), pw=()):
        self.op("vector", lambda e, a=ap, vv=v: e.memset(a, vv), r=r, w=w, pw=pw)

    def rmax(self, out, in_, negate=False, r=(), w=(), pw=()):
        self.op("vector", lambda e, o=out, i=in_, n=negate:
                e.reduce_max(o, i, axis=mybir.AxisListType.X, negate=n), r=r, w=w, pw=pw)

    def rsum(self, out, in_, r=(), w=(), pw=()):
        self.op("vector", lambda e, o=out, i=in_:
                e.reduce_sum(o, i, axis=mybir.AxisListType.X), r=r, w=w, pw=pw)

    def recip(self, out, in_, r=(), w=(), pw=()):
        self.op("vector", lambda e, o=out, i=in_: e.reciprocal(out=o, in_=i), r=r, w=w, pw=pw)

    def iota(self, out, pattern, base, chmul, r=(), w=(), pw=()):
        self.op("gpsimd", lambda e, o=out, p=pattern, b=base, cm=chmul:
                e.iota(o, pattern=p, base=b, channel_multiplier=cm,
                       allow_small_or_imprecise_dtypes=True), r=r, w=w, pw=pw)

    def realize(self):
        nc = self.nc
        n_ops = len(self.ops)
        opinfo = [None] * n_ops
        sem_names = ["c_" + e for e in ENGINES]
        for q in DMA_QUEUES:
            for i in range(self.NSEM_DMA):
                sem_names.append(f"d_{q}_{i}")
        cum = {sn: 0 for sn in sem_names}
        rr = {q: 0 for q in DMA_QUEUES}
        presem = [None] * n_ops
        for oid, (eng, inc, fn, deps) in enumerate(self.ops):
            if inc == 16 and eng in DMA_QUEUES:
                i = rr[eng]
                rr[eng] = (i + 1) % self.NSEM_DMA
                sn = f"d_{eng}_{i}"
                presem[oid] = (sn, cum[sn])
            else:
                sn = "c_" + eng
            cum[sn] += inc
            opinfo[oid] = (sn, cum[sn])
        totals = dict(cum)
        sem_cms = {sn: nc.semaphore("S" + sn) for sn in sem_names}
        sems = {sn: sem_cms[sn].__enter__() for sn in sem_names}
        blk_cm = nc.Block()
        block = blk_cm.__enter__()
        try:
            def mksec(name):
                def body(eng):
                    hw = {sn: 0 for sn in sem_names}
                    for oid, (oeng, inc, fn, deps) in enumerate(self.ops):
                        if oeng != name:
                            continue
                        need = {}
                        for j in deps:
                            js, jcum = opinfo[j]
                            if js == "c_tensor" and name == "tensor":
                                continue
                            if jcum > need.get(js, 0):
                                need[js] = jcum
                        if presem[oid] is not None:
                            sn, cb = presem[oid]
                            if cb > need.get(sn, 0):
                                need[sn] = cb
                        for js, v in sorted(need.items()):
                            if v > hw[js]:
                                eng.wait_ge(sems[js], v)
                                hw[js] = v
                        inst = fn(eng)
                        my_sn = opinfo[oid][0]
                        inst.then_inc(sems[my_sn], inc)
                    if name in ("sync", "gpsimd"):
                        for sn in sem_names:
                            if totals[sn] > 0:
                                eng.wait_ge(sems[sn], totals[sn])
                return body
            for name in ENGINES:
                getattr(block, name)(mksec(name))
        finally:
            blk_cm.__exit__(None, None, None)
            for sn in sem_names:
                sem_cms[sn].__exit__(None, None, None)


def fm(ap_dram, cols, p=128):
    return ap_dram[:, cols].rearrange("(c p) t -> p c t", p=p)


SLOTS = {
    "A0": ("wsb", 0), "A1": ("wsb", 8192),
    "B0": ("wsb2", 0), "B1": ("wsb2", 8192),
    "C0": ("ssb32", 0),
}


def e_prep(c):
    if "wsb" in c.sb:
        return
    c.alloc("wsb", [128, 16384], BF16)
    c.alloc("wsb2", [128, 16384], BF16)
    c.alloc("ssb32", [128, 8192], BF16)
    c.alloc("nsbA", [128, 4096], BF16)
    c.alloc("nsbB", [128, 4096], BF16)
    c.alloc("gsb", [128, 4096], BF16)
    c.alloc("ksbhA", [128, 4096], BF16)
    c.alloc("ksbhB", [128, 4096], BF16)
    c.alloc("vsbhA", [128, NT * 65], BF16)
    c.alloc("vsbhB", [128, NT * 65], BF16)
    c.alloc("Msb", [128, MW], BF16)
    c.alloc("qsbA", [128, SW], BF16)
    c.alloc("qsbB", [128, SW], BF16)
    c.alloc("PtA", [128, SW], BF16)
    c.alloc("PtB", [128, SW], BF16)
    c.alloc("vsb2A", [128, 1040], BF16)
    c.alloc("vsb2B", [128, 1040], BF16)
    c.alloc("xsbA0", [128, SW], F32)
    c.alloc("xsbA1", [128, SW], F32)
    c.alloc("xsbB0", [128, SW], F32)
    c.alloc("xsbB1", [128, SW], F32)
    c.alloc("tf0", [128, SW], F32)
    c.alloc("tf1", [128, SW], F32)
    c.alloc("evB0", [128, SW], BF16)
    c.alloc("evB1", [128, SW], BF16)
    c.alloc("evC0", [128, SW], BF16)
    c.alloc("evC1", [128, SW], BF16)
    c.alloc("xbf0", [128, SW], BF16)
    c.alloc("xbf1", [128, SW], BF16)
    c.alloc("x2bf0", [128, SW], BF16)
    c.alloc("x2bf1", [128, SW], BF16)
    c.alloc("bcmA", [128, SW], F32)
    c.alloc("bcrA", [128, SW], F32)
    c.alloc("bcmB", [128, SW], F32)
    c.alloc("bcrB", [128, SW], F32)
    c.alloc("pcol", [128, 16], F32)
    c.alloc("pcol2", [128, 16], F32)
    c.alloc("gb0", [128, SW], BF16)
    c.alloc("gb1", [128, SW], BF16)
    c.alloc("gc0", [128, SW], BF16)
    c.alloc("gc1", [128, SW], BF16)
    c.alloc("gf0", [128, SW], F32)
    c.alloc("gf1", [128, SW], F32)
    c.alloc("gpc", [128, 16], F32)
    c.alloc("qrw1", [1, SW], F32)
    c.alloc("qrw2", [1, SW], F32)
    c.alloc("rlA1", [1, SW], F32)
    c.alloc("rlA2", [1, SW], F32)
    c.alloc("rbc", [128, SW], F32)
    c.alloc("rtf", [128, 64], F32)
    c.alloc("rtt", [128, 128], F32)
    c.alloc("rpc", [128, 8], F32)
    c.alloc("lsm", [128, 320], BF16)
    c.alloc("bvrow", [1, 1024], BF16)
    c.alloc("gbrow", [1, 8], BF16)
    c.alloc("ones_bf", [128, 1], BF16)
    c.alloc("ones_f", [128, 1], F32)
    c.alloc("onesrow_bf", [1, 128], BF16)
    c.alloc("onesrow_f", [1, 128], F32)
    c.alloc("epscol", [1, 1], F32)
    c.alloc("khat2", [1, H], F32)
    c.alloc("bhat", [1, H], F32)
    c.alloc("identf", [128, 128], F32)
    c.alloc("psA0", [128, SW], F32, psum=True)
    c.alloc("psA1", [128, SW], F32, psum=True)
    c.alloc("psS0", [128, SW], F32, psum=True)
    c.alloc("psS1", [128, SW], F32, psum=True)
    c.alloc("psPV", [128, SW], F32, psum=True)
    c.alloc("psB0", [128, SW], F32, psum=True)
    c.alloc("psB1", [128, SW], F32, psum=True)
    c.alloc("psR", [128, SW], F32, psum=True)

    c.memset(c.sb["ones_bf"][:, :], 1.0, w=["ones"])
    c.memset(c.sb["ones_f"][:, :], 1.0, pw=["ones"])
    c.memset(c.sb["onesrow_bf"][:, :], 1.0, pw=["ones"])
    c.memset(c.sb["onesrow_f"][:, :], 1.0, pw=["ones"])
    c.memset(c.sb["epscol"][:, :], EPS, pw=["ones"])
    c.memset(c.sb["ksbhA"][64:128, :], 0.0, w=["konesA"])
    c.memset(c.sb["ksbhB"][64:128, :], 0.0, w=["konesB"])
    c.memset(c.sb["ksbhA"][64:65, :], 1.0, pw=["konesA"])
    c.memset(c.sb["ksbhB"][64:65, :], 1.0, pw=["konesB"])
    c.memset(c.sb["qsbA"][64:128, :], 0.0, w=["qzA"])
    c.memset(c.sb["qsbB"][64:128, :], 0.0, w=["qzB"])
    ii = c.sb["tf1"]
    c.iota(ii[:, 0:128], [[1, 128]], 0, -1, w=["tf1"])
    c.ts(c.sb["identf"][:, :], ii[:, 0:128], 0.0, OP.is_equal, r=["tf1"], w=["identf"])
    for j in range(4):
        c.iota(ii[:, :], [[1, SW]], -128 * j, -1, w=["tf1"])
        mk = c.sb["Msb"][:, j * SW:(j + 1) * SW]
        tf = c.sb["tf0"]
        c.ts(tf[:, :], ii[:, :], 0.0, OP.is_ge, r=["tf1"], w=["tf0"])
        c.ts(mk, tf[:, :], 1.0, OP.subtract, -NEGBIG, OP.mult, r=["tf0"], pw=["Msb"])


FIN_ROWS = {0: ("rw1", "rw2"), 1: ("rw3", "rw4")}


def _rows_finish(c, s, rms, slot):
    base = 0 if slot == 0 else 64
    skey = f"psR{slot}"
    psR = c.sb["psR"]
    mk, rk = FIN_ROWS[slot]
    rm, rr = c.sb[mk], c.sb[rk]
    if rms:
        c.act(rr[:, :], psR[base + 32:base + 33, :], AF.Ln, bias=c.sb["epscol"][0:1, :],
              scale=1.0 / D, r=[skey, "ones"], w=[rk])
        c.act(rr[:, :], rr[:, :], AF.Exp, scale=-0.5, r=[rk], w=[rk])
        c.memset(rm[:, :], 0.0, w=[mk])
    else:
        c.ts(rm[:, :], psR[base:base + 1, :], -1.0 / D, OP.mult, r=[skey], w=[mk])
        c.tt(rr[:, :], rm[:, :], rm[:, :], OP.mult, r=[mk], w=[rk])
        c.ts(psR[base + 32:base + 33, :], psR[base + 32:base + 33, :], 1.0 / D,
             OP.mult, pw=[skey])
        c.tt(psR[base + 32:base + 33, :], psR[base + 32:base + 33, :],
             rr[:, :], OP.subtract, r=[rk], pw=[skey])
        c.act(rr[:, :], psR[base + 32:base + 33, :], AF.Ln,
              bias=c.sb["epscol"][0:1, :], r=[skey, "ones"], w=[rk])
        c.act(rr[:, :], rr[:, :], AF.Exp, scale=-0.5, r=[rk], w=[rk])


def _rows_store(c, rows_name, s, slot):
    rows = c.d(rows_name)
    cols = slice(s * SW, (s + 1) * SW)
    mk, rk = FIN_ROWS[slot]
    c.dma(rows[0:1, cols], c.sb[mk][:, :], eng="scalar", r=[mk],
          pw=[f"D:{rows_name}:{s}"])
    c.dma(rows[1:2, cols], c.sb[rk][:, :], eng="scalar", r=[rk],
          pw=[f"D:{rows_name}:{s}"])


def collect_chunk(c, src_ap, src_key, p, first, last, slot, rms=False, claim=False):
    base = 0 if slot == 0 else 64
    skey = f"psR{slot}"
    ob = c.sb["ones_bf"]
    x2 = c.sb[f"x2bf{p}"]
    if src_ap.dtype != BF16:
        xb = c.sb[f"xbf{p}"]
        c.copy(xb[:, :], src_ap, r=[src_key], w=[f"xbf{p}"])
        src_ap, src_key = xb[:, :], f"xbf{p}"
    wk = dict(w=[skey]) if claim else dict(pw=[skey])
    if not rms:
        c.mm(c.sb["psR"][base:base + 1, :], ob[:, :], src_ap,
             start=first, stop=last, r=[src_key, "ones"], **wk)
        wk = dict(pw=[skey])
    c.act(x2[:, :], src_ap, AF.Square, r=[src_key], w=[f"x2bf{p}"])
    c.mm(c.sb["psR"][base + 32:base + 33, :], ob[:, :], x2[:, :],
         start=first, stop=last, r=[f"x2bf{p}", "ones"],
         tp=(0, 96) if base == 64 else None, **wk)


def e_stats(c, src_name, rows_name, rms=False, slot=0):
    src = c.d(src_name)
    for s in range(NS):
        cols = slice(s * SW, (s + 1) * SW)
        for ic in range(ND):
            p = ic % 2
            xb = c.sb[f"xbf{p}"]
            c.dma(xb[:, :], src[ic * 128:(ic + 1) * 128, cols], eng="gpsimd",
                  r=[sk(src_name, ic, s)], w=[f"xbf{p}"])
            collect_chunk(c, xb[:, :], f"xbf{p}", p, ic == 0, ic == ND - 1, slot, rms,
                          claim=(s == 0 and ic == 0))
        _rows_finish(c, s, rms, slot)
        _rows_store(c, rows_name, s, slot)


BC_ROWS = {"A": ("rlA1", "rlA2"), "B": ("qrw1", "qrw2")}


def bcast_rows(c, rows_name, s, pair):
    rows = c.d(rows_name)
    cols = slice(s * SW, (s + 1) * SW)
    k1, k2 = BC_ROWS[pair]
    r1, r2 = c.sb[k1], c.sb[k2]
    c.dma(r1[:, :], rows[0:1, cols], eng="scalar", r=[f"D:{rows_name}:{s}"], w=[k1])
    c.dma(r2[:, :], rows[1:2, cols], eng="scalar", r=[f"D:{rows_name}:{s}"], w=[k2])
    orf = c.sb["onesrow_f"]
    bm, br = c.sb["bcm" + pair], c.sb["bcr" + pair]
    c.mm(c.sb["psB0"][:, :], orf[0:1, :], r1[:, :], r=[k1, "ones"], w=["psB0"])
    c.copy(bm[:, :], c.sb["psB0"][:, :], r=["psB0"], w=["bcm" + pair])
    c.mm(c.sb["psB1"][:, :], orf[0:1, :], r2[:, :], r=[k2, "ones"], w=["psB1"])
    c.copy(br[:, :], c.sb["psB1"][:, :], r=["psB1"], w=["bcr" + pair])


def load_pcols(c, bufname, dram_name, scale=None, n=ND, off=0):
    p = c.d(dram_name)
    buf = c.sb[bufname]
    for ic in range(n):
        col = buf[:, ic:ic + 1]
        src = p.rearrange("(c p) -> c p", p=128)[off + ic:off + ic + 1, :].transpose([1, 0])
        c.dma(col, src, eng="scalar", r=["D:" + dram_name], pw=[bufname])
    if scale is not None:
        c.ts(buf[:, 0:n], buf[:, 0:n], scale, OP.mult, r=[bufname], w=[bufname])


def e_apply(c, src_name, rows_name, out_name, out_dt=BF16, gamma=None, beta=None,
            rmsw=None, resid_name=None, resid_dt=F32, collect_rows=None, slot=0,
            pair="A", pbuf="pcol"):
    src, out = c.d(src_name), c.d(out_name)
    resid = c.d(resid_name) if resid_name else None
    if gamma is not None:
        load_pcols(c, pbuf, gamma)
        load_pcols(c, "pcol2", beta)
    if rmsw is not None:
        load_pcols(c, pbuf, rmsw)
    bm, br = c.sb["bcm" + pair], c.sb["bcr" + pair]
    bmk, brk = "bcm" + pair, "bcr" + pair
    for s in range(NS):
        cols = slice(s * SW, (s + 1) * SW)
        bcast_rows(c, rows_name, s, pair)
        for ic in range(ND):
            p = ic % 2
            rr = slice(ic * 128, (ic + 1) * 128)
            xA, tA = c.sb[f"xsbA{p}"], c.sb[f"tf{p}"]
            kx, kt = f"xsbA{p}", f"tf{p}"
            c.dma(xA[:, :], src[rr, cols], r=[sk(src_name, ic, s)], w=[kx])
            c.tt(tA[:, :], xA[:, :], bm[:, :], OP.add, r=[kx, bmk], w=[kt])
            if gamma is not None:
                c.stt(tA[:, :], tA[:, :], c.sb[pbuf][:, ic:ic + 1], br[:, :],
                      OP.mult, OP.mult, r=[kt, pbuf, brk], w=[kt])
                c.ts(tA[:, :], tA[:, :], c.sb["pcol2"][:, ic:ic + 1], OP.add,
                     r=[kt, "pcol2"], w=[kt])
            elif rmsw is not None:
                c.stt(tA[:, :], tA[:, :], c.sb[pbuf][:, ic:ic + 1], br[:, :],
                      OP.mult, OP.mult, r=[kt, pbuf, brk], w=[kt])
            else:
                c.tt(tA[:, :], tA[:, :], br[:, :], OP.mult, r=[kt, brk], w=[kt])
            if resid is not None:
                if resid_dt == F32:
                    rb, rbk = c.sb[f"xsbB{p}"], f"xsbB{p}"
                else:
                    rb, rbk = c.sb[f"evC{p}"], f"evC{p}"
                c.dma(rb[:, :], resid[rr, cols], r=[sk(resid_name, ic, s)], w=[rbk])
                c.tt(tA[:, :], tA[:, :], rb[:, :], OP.add, r=[kt, rbk], w=[kt])
            if out_dt == F32:
                c.dma(out[rr, cols], tA[:, :], r=[kt], w=[sk(out_name, ic, s)])
            else:
                ev = c.sb[f"evB{p}"]
                c.copy(ev[:, :], tA[:, :], r=[kt], w=[f"evB{p}"])
                c.dma(out[rr, cols], ev[:, :], r=[f"evB{p}"], w=[sk(out_name, ic, s)])
            if collect_rows is not None:
                collect_chunk(c, tA[:, :], kt, p, ic == 0, ic == ND - 1, slot,
                              claim=(s == 0 and ic == 0))
        if collect_rows is not None:
            _rows_finish(c, s, False, slot)
            _rows_store(c, collect_rows, s, slot)


def slot_ap(c, slot):
    buf, off = SLOTS[slot]
    return c.sb[buf].ap()[:, off:off + 8192]


def load_w_slot(c, w_name, slot, w_slice=None):
    w = c.d(w_name)
    if w_slice is not None:
        w = w_slice
    c.dma(slot_ap(c, slot).rearrange("p (c j) -> p c j", c=ND),
          w.rearrange("(c p) j -> p c j", p=128), eng="gpsimd",
          r=["D:" + w_name], w=[slot])


def e_proj(c, w_name, src_name, dst_name, slot, bias=None, scale=None, out_dt=BF16,
           src_reader=None, collect_rows=None, collect_rms=False, cslot=0):
    src, dst = c.d(src_name), c.d(dst_name)
    wap = slot_ap(c, slot)
    load_w_slot(c, w_name, slot)
    if bias is not None:
        load_pcols(c, "gpc", bias, scale=scale)
    for s in range(NS):
        sp = s % 2
        nsb = c.sb["nsbA" if sp == 0 else "nsbB"]
        nkey = "nsbA" if sp == 0 else "nsbB"
        cols = slice(s * SW, (s + 1) * SW)
        rkeys = src_reader(s) if src_reader else srd(src_name, s)
        c.dma(nsb[:, :].rearrange("p (c t) -> p c t", c=ND),
              fm(src, cols), r=rkeys, w=[nkey])
        for oc in range(ND):
            p = oc % 2
            ps = c.sb[f"psA{p}"]
            for ic in range(ND):
                c.mm(ps[:, :], wap[:, ic * 1024 + oc * 128: ic * 1024 + (oc + 1) * 128],
                     nsb[:, ic * SW:(ic + 1) * SW], start=(ic == 0), stop=(ic == ND - 1),
                     r=[slot, nkey], w=[f"psA{p}"] if ic == 0 else (),
                     pw=[f"psA{p}"] if ic > 0 else ())
            if out_dt == BF16:
                ev, ekey = c.sb[f"gb{p}"], f"gb{p}"
            else:
                ev, ekey = c.sb[f"gf{p}"], f"gf{p}"
            if bias is not None:
                c.ts(ev[:, :], ps[:, :], scale if scale is not None else 1.0, OP.mult,
                     c.sb["gpc"][:, oc:oc + 1], OP.add, r=[f"psA{p}", "gpc"], w=[ekey])
            elif scale is not None:
                c.ts(ev[:, :], ps[:, :], scale, OP.mult, r=[f"psA{p}"], w=[ekey])
            else:
                c.copy(ev[:, :], ps[:, :], r=[f"psA{p}"], w=[ekey])
            c.dma(dst[oc * 128:(oc + 1) * 128, cols], ev[:, :],
                  r=[ekey], w=[sk(dst_name, oc, s)])
            if collect_rows is not None:
                collect_chunk(c, ev[:, :], ekey, p, oc == 0, oc == ND - 1, cslot,
                              rms=collect_rms, claim=(s == 0 and oc == 0))
        if collect_rows is not None:
            _rows_finish(c, s, collect_rms, cslot)
            _rows_store(c, collect_rows, s, cslot)


def e_qk(c, w_name, src_name, dst_name, role, slot, bias=None, use_bhat=False):
    src, dst = c.d(src_name), c.d(dst_name)
    wap = slot_ap(c, slot)
    scale = 0.125 if role == "q" else None
    qext = c.d("qext") if role == "q" else None
    load_w_slot(c, w_name, slot)
    if bias is not None:
        load_pcols(c, "gpc", bias, scale=scale)
    if role == "k":
        c.memset(c.sb["khat2"][:, :], 0.0, w=["khat2"])
    ob = c.sb["ones_bf"]
    for s in range(NS):
        sp = s % 2
        nsb = c.sb["nsbA" if sp == 0 else "nsbB"]
        nkey = "nsbA" if sp == 0 else "nsbB"
        cols = slice(s * SW, (s + 1) * SW)
        c.dma(nsb[:, :].rearrange("p (c t) -> p c t", c=ND), fm(src, cols),
              r=srd(src_name, s), w=[nkey])
        for oc in range(ND):
            p = oc % 2
            ps = c.sb[f"psA{p}"]
            for ic in range(ND):
                c.mm(ps[:, :], wap[:, ic * 1024 + oc * 128: ic * 1024 + (oc + 1) * 128],
                     nsb[:, ic * SW:(ic + 1) * SW], start=(ic == 0), stop=(ic == ND - 1),
                     r=[slot, nkey], w=[f"psA{p}"] if ic == 0 else (),
                     pw=[f"psA{p}"] if ic > 0 else ())
            kb = c.sb[f"gb{p}"]
            if bias is not None:
                c.ts(kb[:, :], ps[:, :], scale if scale else 1.0, OP.mult,
                     c.sb["gpc"][:, oc:oc + 1], OP.add, r=[f"psA{p}", "gpc"], w=[f"gb{p}"])
            elif scale is not None:
                c.ts(kb[:, :], ps[:, :], scale, OP.mult, r=[f"psA{p}"], w=[f"gb{p}"])
            else:
                c.copy(kb[:, :], ps[:, :], r=[f"psA{p}"], w=[f"gb{p}"])
            c.dma(dst[oc * 128:(oc + 1) * 128, cols], kb[:, :], r=[f"gb{p}"],
                  w=[sk(dst_name, oc, s)])
            k2 = c.sb["PtA" if p == 0 else "PtB"]
            k2key = "PtA" if p == 0 else "PtB"
            c.act(k2[:, :], kb[:, :], AF.Square, r=[f"gb{p}"], w=[k2key])
            for g in range(2):
                hh = 2 * oc + g
                psr = c.sb["psS0" if g == 0 else "psS1"]
                pk = "psS0" if g == 0 else "psS1"
                c.mm(psr[0:1, :], ob[64 * g:64 * g + 64, :], k2[64 * g:64 * g + 64, :],
                     r=[k2key, "ones"], w=[pk])
                if role == "k":
                    c.rmax(c.sb["qrw2"][0:1, 0:1], psr[0:1, :], r=[pk], w=["qrw2"])
                    c.tt(c.sb["khat2"][0:1, hh:hh + 1], c.sb["khat2"][0:1, hh:hh + 1],
                         c.sb["qrw2"][0:1, 0:1], OP.max, r=["khat2", "qrw2"], w=["khat2"])
                else:
                    r1 = c.sb["qrw1"]
                    c.ts(r1[:, :], psr[0:1, :], c.sb["khat2"][0:1, hh:hh + 1], OP.mult,
                         r=[pk, "khat2"], w=["qrw1"])
                    c.act(r1[:, :], r1[:, :], AF.Sqrt, r=["qrw1"], w=["qrw1"])
                    if use_bhat:
                        c.ts(r1[:, :], r1[:, :], c.sb["bhat"][0:1, hh:hh + 1], OP.add,
                             r=["qrw1", "bhat"], w=["qrw1"])
                    c.ts(r1[:, :], r1[:, :], -1.0, OP.mult, CEXP, OP.add,
                         r=["qrw1"], w=["qrw1"])
                    c.dma(qext[hh:hh + 1, cols], r1[:, :], eng="gpsimd", r=["qrw1"],
                          w=[f"D:qext:{hh}:{s}"])


def e_v(c, w_name, src_name, slot, bias=None):
    src, vtok = c.d(src_name), c.d("vtok")
    wap = slot_ap(c, slot)
    load_w_slot(c, w_name, slot)
    if bias is not None:
        b = c.d(bias)
        row = bass_rust.AP(tensor=b.tensor, offset=0, ap=[[D, 1], [1, D]])
        c.dma(c.sb["bvrow"][0:1, :], row, eng="gpsimd", r=["D:" + bias], w=["bvrow"])
    orb = c.sb["onesrow_bf"]
    for nm in ("vsb2A", "vsb2B"):
        vv = c.sb[nm].ap().rearrange("p (h m) -> p h m", m=65)
        c.memset(vv[:, :, 64:65], 1.0, w=[nm + ":ones"])
    for s in range(NS):
        sp = s % 2
        nsb = c.sb["nsbA" if sp == 0 else "nsbB"]
        nkey = "nsbA" if sp == 0 else "nsbB"
        cols = slice(s * SW, (s + 1) * SW)
        c.dma(nsb[:, :].rearrange("p (c t) -> p c t", c=ND), fm(src, cols),
              r=srd(src_name, s), w=[nkey])
        for tt in range(SW // 128):
            vp = tt % 2
            vsb2 = c.sb["vsb2A" if vp == 0 else "vsb2B"]
            vkey = "vsb2A" if vp == 0 else "vsb2B"
            vv = vsb2.ap().rearrange("p (h m) -> p h m", m=65)
            for jh in range(2):
                ps = c.sb[f"psA{jh}"]
                for ic in range(ND):
                    c.mm(ps[:, :], nsb[:, ic * SW + tt * 128: ic * SW + (tt + 1) * 128],
                         wap[:, ic * 1024 + jh * 512: ic * 1024 + (jh + 1) * 512],
                         start=(ic == 0), stop=(ic == ND - 1) and bias is None,
                         r=[nkey, slot], w=[f"psA{jh}"] if ic == 0 else (),
                         pw=[f"psA{jh}"] if ic > 0 else ())
                if bias is not None:
                    c.mm(ps[:, :], orb[0:1, :],
                         c.sb["bvrow"][0:1, jh * 512:(jh + 1) * 512],
                         start=False, stop=True,
                         r=["ones", "bvrow"], pw=[f"psA{jh}"])
                dstv = vv[:, jh * 8:(jh + 1) * 8, 0:64]
                srcv = ps.ap().rearrange("p (hh m) -> p hh m", m=64)
                c.copy(dstv, srcv, r=[f"psA{jh}"], pw=[vkey])
            t0 = s * SW + tt * 128
            c.dma(vtok[t0:t0 + 128, :], vsb2[:, :], r=[vkey, vkey + ":ones"],
                  w=[f"D:vtok:{s * 4 + tt}"])


def e_attn(c, use_bias=False):
    qT, kT, vtok, qext, oT = c.d("qT"), c.d("kT"), c.d("vtok"), c.d("qext"), c.d("oT")
    masks = c.sb["Msb"]
    if use_bias:
        stripes = c.d("L1_bias_stripes")
        Msb = c.sb["Msb"]
    for h in range(H):
        hp = h % 2
        ksbh = c.sb["ksbhA" if hp == 0 else "ksbhB"]
        vsbh = c.sb["vsbhA" if hp == 0 else "vsbhB"]
        kk = "ksbhA" if hp == 0 else "ksbhB"
        vk = "vsbhA" if hp == 0 else "vsbhB"
        c.dma(ksbh[0:64, :], kT[64 * h:64 * h + 64, :], eng="gpsimd",
              r=[sk("kT", h // 2, s) for s in range(NS)], w=[kk])
        c.dma(vsbh[:, :].rearrange("p (k m) -> p k m", m=65),
              vtok.rearrange("(k p) m -> p k m", p=128)[:, :, 65 * h:65 * h + 65],
              eng="gpsimd", r=[f"D:vtok:{t}" for t in range(NT)], w=[vk])
        if use_bias:
            c.dma(Msb[:, :], stripes[h, :, :], eng="gpsimd",
                  r=["D:L1_bias_stripes"], w=["Msb"])
        konk = "konesA" if hp == 0 else "konesB"
        for s in range(NS):
            q0 = s * SW
            cols = slice(q0, q0 + SW)
            sp = s % 2
            qsb = c.sb["qsbA" if sp == 0 else "qsbB"]
            qk = "qsbA" if sp == 0 else "qsbB"
            qz = "qzA" if sp == 0 else "qzB"
            c.dma(qsb[0:64, :], qT[64 * h:64 * h + 64, cols],
                  r=[sk("qT", h // 2, s)], pw=[qk + ":q"])
            c.dma(qsb[64:65, :], qext[h:h + 1, cols],
                  r=[f"D:qext:{h}:{s}", qz], pw=[qk + ":x"])
            nk = 4 * (s + 1)
            SB = ["psS0", "psS1", "psA0", "psA1"]
            PT = ["PtA", "PtB", "gc0", "gc1"]

            def emit_S(k):
                bk = SB[k % 4]
                c.mm(c.sb[bk][:, :], ksbh[0:128, 128 * k:128 * (k + 1)], qsb[0:128, :],
                     start=True, stop=True,
                     r=[kk, konk, qk + ":q", qk + ":x", qz], w=[bk])
            for k in range(min(4, nk)):
                emit_S(k)
            for k in range(nk):
                bk = SB[k % 4]
                ps = c.sb[bk]
                if use_bias:
                    y0 = q0 - 128 * k + 384
                    c.tt(ps[:, :], ps[:, :], Msb[:, y0:y0 + SW], OP.add,
                         r=[bk, "Msb"], w=[bk])
                elif k >= 4 * s:
                    j = k - 4 * s
                    c.tt(ps[:, :], ps[:, :], masks[:, j * SW:(j + 1) * SW], OP.add,
                         r=[bk, "Msb"], w=[bk])
                pk = PT[k % 4]
                Pt = c.sb[pk]
                c.act(Pt[:, :], ps[:, :], AF.Exp, r=[bk], w=[pk])
                if k + 4 < nk:
                    emit_S(k + 4)
                c.mm(c.sb["psPV"][0:65, :], vsbh[:, 65 * k:65 * (k + 1)], Pt[:, :],
                     start=(k == 0), stop=(k == nk - 1),
                     r=[vk, pk], w=["psPV"] if k == 0 else (),
                     pw=["psPV"] if k > 0 else ())
            c.recip(c.sb["qrw1"][:, :], c.sb["psPV"][64:65, :], r=["psPV"], w=["qrw1"])
            c.mm(c.sb["psB0"][0:64, :], c.sb["onesrow_f"][0:1, 0:64], c.sb["qrw1"][:, :],
                 r=["qrw1", "ones"], w=["psB0"])
            c.copy(c.sb["rbc"][0:64, :], c.sb["psB0"][0:64, :], r=["psB0"], w=["rbc"])
            ev = c.sb["gb0" if sp == 0 else "gb1"]
            ek = "gb0" if sp == 0 else "gb1"
            c.tt(ev[0:64, :], c.sb["psPV"][0:64, :], c.sb["rbc"][0:64, :], OP.mult,
                 r=["psPV", "rbc"], w=[ek])
            c.dma(oT[64 * h:64 * h + 64, cols], ev[0:64, :], r=[ek],
                  w=[f"D:oT:{h}:{s}"])


def oT_reader(s):
    return [f"D:oT:{h}:{s}" for h in range(H)]


def e_router(c, src_name):
    n, rw, wrowT = c.d(src_name), c.d("F0_router"), c.d("wrowT")
    rsb = c.sb["gsb"]
    c.dma(rsb[:, 0:64].rearrange("p (c j) -> p c j", c=ND),
          rw.rearrange("(c p) j -> p c j", p=128), eng="gpsimd",
          r=["D:F0_router"], w=["gsb"])
    lg = c.sb["rtf"]
    pc = c.sb["rpc"]
    for s in range(NS):
        sp = s % 2
        nsb = c.sb["nsbA" if sp == 0 else "nsbB"]
        nkey = "nsbA" if sp == 0 else "nsbB"
        cols = slice(s * SW, (s + 1) * SW)
        c.dma(nsb[:, :].rearrange("p (c t) -> p c t", c=ND), fm(n, cols),
              r=srd(src_name, s), w=[nkey])
        for tt in range(SW // 128):
            tp = tt % 2
            ps = c.sb[f"psA{tp}"]
            for ic in range(ND):
                c.mm(ps[:, 0:8], nsb[:, ic * SW + tt * 128: ic * SW + (tt + 1) * 128],
                     rsb[:, ic * 8:(ic + 1) * 8], start=(ic == 0), stop=(ic == ND - 1),
                     r=[nkey, "gsb"], w=[f"psA{tp}"] if ic == 0 else (),
                     pw=[f"psA{tp}"] if ic > 0 else ())
            c.rmax(pc[:, 0:1], ps[:, 0:8], negate=True, r=[f"psA{tp}"], w=["rpc"])
            c.act(lg[:, 0:8], ps[:, 0:8], AF.Exp, bias=pc[:, 0:1],
                  r=[f"psA{tp}", "rpc"], w=["rtf"])
            c.rsum(pc[:, 1:2], lg[:, 0:8], r=["rtf"], pw=["rpc"])
            c.recip(pc[:, 2:3], pc[:, 1:2], r=["rpc"], pw=["rpc"])
            c.ts(lg[:, 0:8], lg[:, 0:8], pc[:, 2:3], OP.mult, r=["rtf", "rpc"], w=["rtf"])
            c.rmax(pc[:, 3:4], lg[:, 0:8], r=["rtf"], pw=["rpc"])
            c.ts(lg[:, 8:16], lg[:, 0:8], pc[:, 3:4], OP.is_equal, r=["rtf", "rpc"], w=["rtf:m1"])
            c.ts(lg[:, 16:24], lg[:, 8:16], -1.0, OP.mult, 1.0, OP.add, r=["rtf:m1"], w=["rtf:nm"])
            c.tt(lg[:, 16:24], lg[:, 0:8], lg[:, 16:24], OP.mult, r=["rtf", "rtf:nm"], w=["rtf:nm"])
            c.rmax(pc[:, 4:5], lg[:, 16:24], r=["rtf:nm"], pw=["rpc"])
            c.ts(lg[:, 16:24], lg[:, 16:24], pc[:, 4:5], OP.is_equal, r=["rtf:nm", "rpc"], w=["rtf:nm"])
            c.ts(lg[:, 8:16], lg[:, 8:16], pc[:, 3:4], OP.mult, r=["rtf:m1", "rpc"], w=["rtf:m1"])
            c.ts(lg[:, 16:24], lg[:, 16:24], pc[:, 4:5], OP.mult, r=["rtf:nm", "rpc"], w=["rtf:nm"])
            c.tt(lg[:, 8:16], lg[:, 8:16], lg[:, 16:24], OP.add, r=["rtf:m1", "rtf:nm"], w=["rtf:m1"])
            c.transpose(c.sb["psB1"][0:8, 0:128], lg[:, 8:16], c.sb["identf"][:, :],
                        r=["rtf:m1", "identf"], w=["psB1"])
            c.copy(c.sb["rtt"][0:8, 0:128], c.sb["psB1"][0:8, 0:128], r=["psB1"], w=["rtt"])
            t0 = s * SW + tt * 128
            c.dma(wrowT[0:8, t0:t0 + 128], c.sb["rtt"][0:8, 0:128], eng="scalar",
                  r=["rtt"], pw=[f"D:wrowT:{s}"])


def e_moe(c, src_name, collect_rows=None, cslot=0):
    n, moe32, wrowT = c.d(src_name), c.d("moe32"), c.d("wrowT")
    ew1, ew2, ew3 = c.d("F0_ew1"), c.d("F0_ew2"), c.d("F0_ew3")
    for e_i in range(E):
        s1slot = "A0" if e_i % 2 == 0 else "B1"
        s3slot = "A1" if e_i % 2 == 0 else "C0"
        s2slot = "B0" if e_i % 2 == 0 else "A0"
        load_w_slot(c, "F0_ew1", s1slot, w_slice=ew1[e_i])
        load_w_slot(c, "F0_ew3", s3slot, w_slice=ew3[e_i])
        load_w_slot(c, "F0_ew2", s2slot, w_slice=ew2[e_i])
        w1b, w3b, w2b = slot_ap(c, s1slot), slot_ap(c, s3slot), slot_ap(c, s2slot)
        for s in range(NS):
            sp = s % 2
            nsb = c.sb["nsbA" if sp == 0 else "nsbB"]
            nkey = "nsbA" if sp == 0 else "nsbB"
            cols = slice(s * SW, (s + 1) * SW)
            c.dma(nsb[:, :].rearrange("p (c t) -> p c t", c=ND), fm(n, cols),
                  r=srd(src_name, s), w=[nkey])
            c.dma(c.sb["qrw2"][:, :], wrowT[e_i:e_i + 1, cols], eng="scalar",
                  r=[f"D:wrowT:{s}"], w=["qrw2"])
            c.mm(c.sb["psB0"][:, :], c.sb["onesrow_f"][0:1, :], c.sb["qrw2"][:, :],
                 r=["qrw2", "ones"], w=["psB0"])
            c.copy(c.sb["rbc"][:, :], c.sb["psB0"][:, :], r=["psB0"], w=["rbc"])
            for eh in range(ND):
                p = eh % 2
                psu = c.sb[f"psA{p}"]
                for ic in range(ND):
                    c.mm(psu[:, :], w1b[:, ic * 1024 + eh * 128: ic * 1024 + (eh + 1) * 128],
                         nsb[:, ic * SW:(ic + 1) * SW], start=(ic == 0), stop=(ic == ND - 1),
                         r=[s1slot, nkey], w=[f"psA{p}"] if ic == 0 else (),
                         pw=[f"psA{p}"] if ic > 0 else ())
                s1 = c.sb[f"gb{p}"]
                c.act(s1[:, :], psu[:, :], AF.Silu, r=[f"psA{p}"], w=[f"gb{p}"])
                psv = c.sb[f"psB{p}"]
                for ic in range(ND):
                    c.mm(psv[:, :], w3b[:, ic * 1024 + eh * 128: ic * 1024 + (eh + 1) * 128],
                         nsb[:, ic * SW:(ic + 1) * SW], start=(ic == 0), stop=(ic == ND - 1),
                         r=[s3slot, nkey], w=[f"psB{p}"] if ic == 0 else (),
                         pw=[f"psB{p}"] if ic > 0 else ())
                gslice = c.sb["gsb"][:, eh * SW:(eh + 1) * SW]
                c.tt(gslice, s1[:, :], psv[:, :], OP.mult,
                     r=[f"gb{p}", f"psB{p}"], pw=["gsb"])
                c.tt(gslice, gslice, c.sb["rbc"][:, :], OP.mult,
                     r=["gsb", "rbc"], pw=["gsb"])
            for dc in range(ND):
                p = dc % 2
                psu = c.sb[f"psA{p}"]
                for eh in range(ND):
                    c.mm(psu[:, :], w2b[:, eh * 1024 + dc * 128: eh * 1024 + (dc + 1) * 128],
                         c.sb["gsb"][:, eh * SW:(eh + 1) * SW], start=(eh == 0),
                         stop=(eh == ND - 1), r=[s2slot, "gsb"],
                         w=[f"psA{p}"] if eh == 0 else (), pw=[f"psA{p}"] if eh > 0 else ())
                ev = c.sb[f"gf{p}"]
                mk = sk("moe32", dc, s)
                if e_i == 0:
                    c.copy(ev[:, :], psu[:, :], r=[f"psA{p}"], w=[f"gf{p}"])
                else:
                    xb = c.sb[f"xsbB{p}"]
                    c.dma(xb[:, :], moe32[dc * 128:(dc + 1) * 128, cols],
                          r=[mk], w=[f"xsbB{p}"])
                    c.tt(ev[:, :], psu[:, :], xb[:, :], OP.add,
                         r=[f"psA{p}", f"xsbB{p}"], w=[f"gf{p}"])
                c.dma(moe32[dc * 128:(dc + 1) * 128, cols], ev[:, :],
                      r=[f"gf{p}"], w=[mk])
                if collect_rows is not None and e_i == E - 1:
                    collect_chunk(c, ev[:, :], f"gf{p}", p, dc == 0, dc == ND - 1,
                                  cslot, rms=True, claim=(s == 0 and dc == 0))
            if collect_rows is not None and e_i == E - 1:
                _rows_finish(c, s, True, cslot)
                _rows_store(c, collect_rows, s, cslot)


def e_glu(c):
    h, w1, w3, g = c.d("h"), c.d("F0_w1"), c.d("F0_w3"), c.d("g")
    slot_rot = [("A0", "A1"), ("B0", "B1"), ("C0", "A0"), ("A1", "B0")]
    for it, og in enumerate(range(0, FF, 1024)):
        s1slot, s3slot = slot_rot[it]
        load_w_slot(c, "F0_w1", s1slot, w_slice=w1[:, og:og + 1024])
        load_w_slot(c, "F0_w3", s3slot, w_slice=w3[:, og:og + 1024])
        wap1, wap3 = slot_ap(c, s1slot), slot_ap(c, s3slot)
        for s in range(NS):
            sp = s % 2
            nsb = c.sb["nsbA" if sp == 0 else "nsbB"]
            nkey = "nsbA" if sp == 0 else "nsbB"
            cols = slice(s * SW, (s + 1) * SW)
            c.dma(nsb[:, :].rearrange("p (c t) -> p c t", c=ND), fm(h, cols),
                  r=srd("h", s), w=[nkey])
            for j in range(8):
                p = j % 2
                psu, psv = c.sb[f"psA{p}"], c.sb[f"psB{p}"]
                for ic in range(ND):
                    c.mm(psu[:, :], wap1[:, ic * 1024 + j * 128: ic * 1024 + (j + 1) * 128],
                         nsb[:, ic * SW:(ic + 1) * SW], start=(ic == 0), stop=(ic == ND - 1),
                         r=[s1slot, nkey], w=[f"psA{p}"] if ic == 0 else (),
                         pw=[f"psA{p}"] if ic > 0 else ())
                s1 = c.sb[f"gb{p}"]
                c.act(s1[:, :], psu[:, :], AF.Silu, r=[f"psA{p}"], w=[f"gb{p}"])
                for ic in range(ND):
                    c.mm(psv[:, :], wap3[:, ic * 1024 + j * 128: ic * 1024 + (j + 1) * 128],
                         nsb[:, ic * SW:(ic + 1) * SW], start=(ic == 0), stop=(ic == ND - 1),
                         r=[s3slot, nkey], w=[f"psB{p}"] if ic == 0 else (),
                         pw=[f"psB{p}"] if ic > 0 else ())
                ev = c.sb[f"gc{p}"]
                c.tt(ev[:, :], s1[:, :], psv[:, :], OP.mult,
                     r=[f"gb{p}", f"psB{p}"], w=[f"gc{p}"])
                jg = og // 128 + j
                c.dma(g[og + j * 128: og + (j + 1) * 128, cols], ev[:, :],
                      r=[f"gc{p}"], w=[f"D:g:{jg}:{s}"])


def e_projK32(c, w_name, src_name, dst_name, bias=None, collect_rows=None,
              collect_rms=False, cslot=0):
    w, src, dst = c.d(w_name), c.d(src_name), c.d(dst_name)
    ssb = c.sb["ssb32"]
    if bias is not None:
        load_pcols(c, "gpc", bias)
    c.dma(c.sb["wsb"][:, :].rearrange("p (c j) -> p c j", c=16),
          w[0:2048, :].rearrange("(c p) j -> p c j", p=128), eng="gpsimd",
          r=["D:" + w_name], w=["A0", "A1"])
    c.dma(c.sb["wsb2"][:, :].rearrange("p (c j) -> p c j", c=16),
          w[2048:4096, :].rearrange("(c p) j -> p c j", p=128), eng="gpsimd",
          r=["D:" + w_name], w=["B0", "B1"])
    for s in range(NS):
        cols = slice(s * SW, (s + 1) * SW)
        c.dma(ssb[:, :].rearrange("p (c t) -> p c t", c=16),
              fm(src[0:2048, :], cols),
              r=[f"D:{src_name}:{jg}:{s}" for jg in range(16)], w=["C0"])
        c.dma(c.sb["nsbA"][:, :].rearrange("p (c t) -> p c t", c=8),
              fm(src[2048:3072, :], cols),
              r=[f"D:{src_name}:{jg}:{s}" for jg in range(16, 24)], w=["nsbA"])
        c.dma(c.sb["nsbB"][:, :].rearrange("p (c t) -> p c t", c=8),
              fm(src[3072:4096, :], cols),
              r=[f"D:{src_name}:{jg}:{s}" for jg in range(24, 32)], w=["nsbB"])
        for s_ in [s]:
            pass
        for oc in range(ND):
            p = oc % 2
            ps = c.sb[f"psA{p}"]
            for ic in range(32):
                wt = (c.sb["wsb"] if ic < 16 else c.sb["wsb2"])[
                    :, (ic % 16) * 1024 + oc * 128:(ic % 16) * 1024 + (oc + 1) * 128]
                rslot = ("A0" if ic < 8 else "A1") if ic < 16 else ("B0" if ic < 24 else "B1")
                if ic < 16:
                    act = ssb[:, ic * SW:(ic + 1) * SW]
                    akey = "C0"
                elif ic < 24:
                    act = c.sb["nsbA"][:, (ic - 16) * SW:(ic - 15) * SW]
                    akey = "nsbA"
                else:
                    act = c.sb["nsbB"][:, (ic - 24) * SW:(ic - 23) * SW]
                    akey = "nsbB"
                c.mm(ps[:, :], wt, act,
                     start=(ic == 0), stop=(ic == 31),
                     r=[rslot, akey],
                     w=[f"psA{p}"] if ic == 0 else (), pw=[f"psA{p}"] if ic > 0 else ())
            ev = c.sb[f"gf{p}"]
            if bias is not None:
                c.ts(ev[:, :], ps[:, :], 1.0, OP.mult, c.sb["gpc"][:, oc:oc + 1], OP.add,
                     r=[f"psA{p}", "gpc"], w=[f"gf{p}"])
            else:
                c.copy(ev[:, :], ps[:, :], r=[f"psA{p}"], w=[f"gf{p}"])
            c.dma(dst[oc * 128:(oc + 1) * 128, cols], ev[:, :], r=[f"gf{p}"],
                  w=[sk(dst_name, oc, s)])
            if collect_rows is not None:
                collect_chunk(c, ev[:, :], f"gf{p}", p, oc == 0, oc == ND - 1, cslot,
                              rms=collect_rms, claim=(s == 0 and oc == 0))
        if collect_rows is not None:
            _rows_finish(c, s, collect_rms, cslot)
            _rows_store(c, collect_rows, s, cslot)


def e_l1ffn(c, src_name):
    n, g = c.d(src_name), c.d("g")
    gateT, gate_b = c.d("G1_gateT"), c.d("G1_gate_b")
    AcatT, BcatT = c.d("G1_AcatT"), c.d("G1_BcatT")
    wiT = c.d("G1_wiT")
    small = c.sb["lsm"]
    c.dma(small[:, 0:64].rearrange("p (c j) -> p c j", c=ND),
          gateT.rearrange("(c p) j -> p c j", p=128), eng="gpsimd",
          r=["D:G1_gateT"], pw=["lsm"])
    c.dma(small[:, 64:64 + 256].rearrange("p (c j) -> p c j", c=ND),
          AcatT.rearrange("(c p) j -> p c j", p=128), eng="gpsimd",
          r=["D:G1_AcatT"], pw=["lsm"])
    gb_row = bass_rust.AP(tensor=gate_b.tensor, offset=0, ap=[[E, 1], [1, E]])
    c.dma(c.sb["gbrow"][0:1, 0:8], gb_row, eng="gpsimd", r=["D:G1_gate_b"], w=["gbrow"])
    lg = c.sb["rtf"]
    pc = c.sb["rpc"]
    rmt = c.sb["gsb"]
    orb = c.sb["onesrow_bf"]
    for s in range(NS):
        sp = s % 2
        nsb = c.sb["nsbA" if sp == 0 else "nsbB"]
        nkey = "nsbA" if sp == 0 else "nsbB"
        cols = slice(s * SW, (s + 1) * SW)
        c.dma(nsb[:, :].rearrange("p (c t) -> p c t", c=ND), fm(n, cols),
              r=srd(src_name, s), w=[nkey])
        for tt in range(SW // 128):
            tp = tt % 2
            ps = c.sb[f"psA{tp}"]
            for ic in range(ND):
                c.mm(ps[:, 0:8], nsb[:, ic * SW + tt * 128: ic * SW + (tt + 1) * 128],
                     small[:, ic * 8:(ic + 1) * 8], start=(ic == 0), stop=False,
                     r=[nkey, "lsm"], w=[f"psA{tp}"] if ic == 0 else (),
                     pw=[f"psA{tp}"] if ic > 0 else ())
            c.mm(ps[:, 0:8], orb[0:1, :], c.sb["gbrow"][0:1, 0:8],
                 start=False, stop=True, r=["ones", "gbrow"], pw=[f"psA{tp}"])
            c.copy(lg[:, 0:8], ps[:, 0:8], r=[f"psA{tp}"], w=["rtf"])
            c.rmax(pc[:, 0:1], lg[:, 0:8], r=["rtf"], w=["rpc"])
            c.ts(lg[:, 8:16], lg[:, 0:8], pc[:, 0:1], OP.is_equal,
                 r=["rtf", "rpc"], w=["rtf:m1"])
            m32 = lg.ap()[:, 16:48].rearrange("p (e r2) -> p e r2", r2=R)
            m1v = lg.ap()[:, 8:16].rearrange("p (e r2) -> p e r2", r2=1)
            for r_ in range(R):
                c.copy(m32[:, :, r_:r_ + 1], m1v, r=["rtf:m1"], pw=["rtf:nm"])
            psb = c.sb[f"psB{tp}"]
            for ic in range(ND):
                c.mm(psb[:, 0:32], nsb[:, ic * SW + tt * 128: ic * SW + (tt + 1) * 128],
                     small[:, 64 + ic * 32:64 + (ic + 1) * 32], start=(ic == 0),
                     stop=(ic == ND - 1), r=[nkey, "lsm"],
                     w=[f"psB{tp}"] if ic == 0 else (), pw=[f"psB{tp}"] if ic > 0 else ())
            c.tt(c.sb["rtt"][:, 0:32], psb[:, 0:32], lg[:, 16:48], OP.mult,
                 r=[f"psB{tp}", "rtf:nm"], w=["rtt"])
            c.transpose(c.sb["psB1"][0:32, 0:128], c.sb["rtt"][:, 0:32],
                        c.sb["identf"][:, :], r=["rtt", "identf"], w=["psB1"])
            c.copy(rmt[0:32, s * SW + tt * 128: s * SW + (tt + 1) * 128],
                   c.sb["psB1"][0:32, 0:128], r=["psB1"], pw=["gsb"])
    bc_sb = c.sb["Msb"]
    c.dma(bc_sb[0:32, 0:FF], BcatT[:, :], eng="gpsimd", r=["D:G1_BcatT"], w=["Msb"])
    wrot = ["A0", "B0", "C0", "A1"]
    for it, og in enumerate(range(0, FF, 1024)):
        wslot = wrot[it]
        wi = c.d("G1_wiT")
        c.dma(slot_ap(c, wslot).rearrange("p (c j) -> p c j", c=ND),
              wi[:, og:og + 1024].rearrange("(c p) j -> p c j", p=128), eng="gpsimd",
              r=["D:G1_wiT"], w=[wslot])
        wap = slot_ap(c, wslot)
        load_pcols(c, "gpc", "G1_wi_b", n=8, off=og // 128)
        for s in range(NS):
            sp = s % 2
            nsb = c.sb["nsbA" if sp == 0 else "nsbB"]
            nkey = "nsbA" if sp == 0 else "nsbB"
            cols = slice(s * SW, (s + 1) * SW)
            c.dma(nsb[:, :].rearrange("p (c t) -> p c t", c=ND), fm(n, cols),
                  r=srd(src_name, s), w=[nkey])
            for j in range(8):
                p = j % 2
                ps = c.sb[f"psA{p}"]
                c.mm(ps[:, :], bc_sb[0:32, og + j * 128: og + (j + 1) * 128],
                     rmt[0:32, cols], start=True, stop=False,
                     r=["Msb", "gsb"], w=[f"psA{p}"])
                for ic in range(ND):
                    c.mm(ps[:, :], wap[:, ic * 1024 + j * 128: ic * 1024 + (j + 1) * 128],
                         nsb[:, ic * SW:(ic + 1) * SW], start=False, stop=(ic == ND - 1),
                         r=[wslot, nkey], pw=[f"psA{p}"])
                ev = c.sb[f"gb{p}"]
                c.act(ev[:, :], ps[:, :], AF.Relu, bias=c.sb["gpc"][:, j:j + 1],
                      r=[f"psA{p}", "gpc"], w=[f"gb{p}"])
                jg = og // 128 + j
                c.dma(g[og + j * 128: og + (j + 1) * 128, cols], ev[:, :],
                      r=[f"gb{p}"], w=[f"D:g:{jg}:{s}"])
    e_projK32(c, "G1_woT", "g", "y32", bias="G1_wo_b", collect_rows="rowsA", cslot=1)


def e_bhat(c):
    stripes = c.d("L1_bias_stripes")
    c.memset(c.sb["bhat"][:, :], NEGBIG, w=["bhat"])
    r = c.sb["qrw1"]
    for h in range(H):
        for ch in range(5):
            w = min(SW, 2559 - ch * SW)
            c.dma(r[0:1, 0:w], stripes[h, 127:128, ch * SW:ch * SW + w],
                  eng="scalar", r=["D:L1_bias_stripes"], w=["qrw1"])
            c.rmax(c.sb["qrw2"][0:1, 0:1], r[0:1, 0:w], r=["qrw1"], w=["qrw2"])
            c.tt(c.sb["bhat"][0:1, h:h + 1], c.sb["bhat"][0:1, h:h + 1],
                 c.sb["qrw2"][0:1, 0:1], OP.max, r=["bhat", "qrw2"], w=["bhat"])


def ph_all(c):
    e_prep(c)
    c.alloc("rw1", [1, SW], F32)
    c.alloc("rw2", [1, SW], F32)
    c.alloc("rw3", [1, SW], F32)
    c.alloc("rw4", [1, SW], F32)
    e_stats(c, "xT", "rowsA", slot=0)
    e_apply(c, "xT", "rowsA", "n1", pair="A")
    e_qk(c, "L0_Wk", "n1", "kT", "k", "A0", bias="L0_bk")
    e_qk(c, "L0_Wq", "n1", "qT", "q", "A1", bias="L0_bq")
    e_v(c, "L0_Wv", "n1", "B0", bias="L0_bv")
    e_attn(c, use_bias=False)
    e_proj(c, "L0_Wo", "oT", "y32", "B1", bias="L0_bo", out_dt=F32,
           src_reader=oT_reader, collect_rows="rowsB", cslot=1)
    e_apply(c, "y32", "rowsB", "d32", out_dt=F32, gamma="L0_gamma", beta="L0_beta",
            collect_rows="rowsC", slot=0, pair="B")
    e_apply(c, "d32", "rowsC", "x32a", out_dt=F32, resid_name="xT",
            collect_rows="rowsD", slot=1, pair="A")
    e_apply(c, "x32a", "rowsD", "n2", pair="B")
    e_router(c, "n2")
    e_qk(c, "F0_wk", "n2", "kT", "k", "A0")
    e_qk(c, "F0_wq", "n2", "qT", "q", "A1")
    e_v(c, "F0_wv", "n2", "B0")
    e_attn(c, use_bias=False)
    e_proj(c, "F0_wo", "oT", "y32", "B1", out_dt=F32, src_reader=oT_reader,
           collect_rows="rowsE", collect_rms=True, cslot=0)
    e_apply(c, "y32", "rowsE", "h", out_dt=BF16, rmsw="F0_attn_nw",
            resid_name="n2", resid_dt=BF16, pair="A")
    e_glu(c)
    e_projK32(c, "F0_w2", "g", "y32", collect_rows="rowsF", collect_rms=True, cslot=1)
    e_apply(c, "y32", "rowsF", "d32", out_dt=F32, rmsw="F0_ff_nw",
            resid_name="h", resid_dt=BF16, pair="B")
    e_moe(c, "n2", collect_rows="rowsG", cslot=0)
    e_apply(c, "moe32", "rowsG", "y32", out_dt=F32, rmsw="F0_moe_nw",
            resid_name="d32", resid_dt=F32, collect_rows="rowsH", slot=1, pair="A")
    e_apply(c, "y32", "rowsH", "x32b", out_dt=F32, resid_name="x32a",
            collect_rows="rowsI", slot=0, pair="B")
    e_apply(c, "x32b", "rowsI", "n3", pair="A")
    e_bhat(c)
    e_qk(c, "L1_WKT", "n3", "kT", "k", "A0", bias="L1_bK")
    e_qk(c, "L1_WQT", "n3", "qT", "q", "A1", bias="L1_bQ", use_bhat=True)
    e_v(c, "L1_WVT", "n3", "B0", bias="L1_bV")
    e_attn(c, use_bias=True)
    e_proj(c, "L1_WOT", "oT", "y32", "B1", bias="L1_bO", out_dt=F32,
           src_reader=oT_reader, collect_rows="rowsJ", cslot=1)
    e_apply(c, "y32", "rowsJ", "x32c", out_dt=F32, resid_name="x32b",
            collect_rows="rowsK", slot=0, pair="B")
    e_apply(c, "x32c", "rowsK", "n4", pair="A")
    e_l1ffn(c, "n4")
    e_apply(c, "y32", "rowsA", "yT", out_dt=F32, resid_name="x32c", pair="B")


def _host_layout(inputs):
    g = {}
    g["xT"] = [np.ascontiguousarray(np.asarray(inputs["L0_attn_x"][i]).T) for i in range(NCORES)]
    def A(name):
        return np.ascontiguousarray(np.asarray(inputs[name], dtype=np.float32))
    shared = {
        "L0_Wq": A("L0_attn_Wq"), "L0_Wk": A("L0_attn_Wk"), "L0_Wv": A("L0_attn_Wv"),
        "L0_Wo": A("L0_attn_Wo"),
        "L0_bq": A("L0_attn_bq"), "L0_bk": A("L0_attn_bk"), "L0_bv": A("L0_attn_bv"),
        "L0_bo": A("L0_attn_bo"), "L0_gamma": A("L0_attn_gamma"), "L0_beta": A("L0_attn_beta"),
        "F0_wq": A("L0_ffn_wq"), "F0_wk": A("L0_ffn_wk"), "F0_wv": A("L0_ffn_wv"),
        "F0_wo": A("L0_ffn_wo"), "F0_w1": A("L0_ffn_w1"), "F0_w2": A("L0_ffn_w2"),
        "F0_w3": A("L0_ffn_w3"),
        "F0_attn_nw": A("L0_ffn_attn_norm_w"), "F0_ff_nw": A("L0_ffn_ff_norm_w"),
        "F0_moe_nw": A("L0_ffn_moe_norm_w"), "F0_router": A("L0_ffn_router_w"),
        "F0_ew1": A("L0_ffn_ew1"), "F0_ew2": A("L0_ffn_ew2"), "F0_ew3": A("L0_ffn_ew3"),
        "L1_WQT": np.ascontiguousarray(A("L1_attn_WQ").T),
        "L1_WKT": np.ascontiguousarray(A("L1_attn_WK").T),
        "L1_WVT": np.ascontiguousarray(A("L1_attn_WV").T),
        "L1_WOT": np.ascontiguousarray(A("L1_attn_WO").T),
        "L1_bQ": A("L1_attn_bQ"), "L1_bK": A("L1_attn_bK"), "L1_bV": A("L1_attn_bV"),
        "L1_bO": A("L1_attn_bO"),
        "G1_gateT": np.ascontiguousarray(A("L1_ffn_gate_W").T),
        "G1_gate_b": A("L1_ffn_gate_b"),
        "G1_wiT": np.ascontiguousarray(A("L1_ffn_wi_W").T),
        "G1_wi_b": A("L1_ffn_wi_b"),
        "G1_woT": np.ascontiguousarray(A("L1_ffn_wo_W").T),
        "G1_wo_b": A("L1_ffn_wo_b"),
        "G1_AcatT": np.ascontiguousarray(A("L1_ffn_lora_A").reshape(E * R, D).T),
        "G1_BcatT": np.ascontiguousarray(
            A("L1_ffn_lora_B").transpose(0, 2, 1).reshape(E * R, FF)),
    }
    tbl = A("L1_attn_rel_pos_table")
    yy = np.arange(MW)[None, :]
    pp = np.arange(128)[:, None]
    idx = np.minimum(1663 + yy - pp, 4094)
    stripes = tbl[idx, :]
    masked = (yy - 384 - pp) < 0
    stripes = np.where(masked[:, :, None], np.float32(MNEG), stripes)
    shared["L1_bias_stripes"] = np.ascontiguousarray(stripes.transpose(2, 0, 1))
    return g, shared


def kernel(**inputs):
    percore, shared = _host_layout(inputs)
    nc = bass.Bass("TRN2", target_bir_lowering=False)
    c = Ctx(nc, set(EXTERNAL_IN), set(EXTERNAL_OUT))
    try:
        ph_all(c)
        c.realize()
    finally:
        c.close()
    in_maps = []
    for i in range(NCORES):
        m = {}
        for rn in c.reads:
            a = percore["xT"][i] if rn == "xT" else shared[rn]
            m[rn] = _as_np(a, rn)
        in_maps.append(m)
    res = run_bass_kernel_spmd(nc, in_maps, core_ids=list(range(NCORES)))
    out = np.stack([np.ascontiguousarray(np.asarray(res.results[i]["yT"]).T)
                    for i in range(NCORES)])
    return out.astype(np.float32)


def _as_np(a, name):
    shape, dt = DRAM_SPECS[name]
    npdt = {F32: np.float32, BF16: NPBF, I32: np.int32}[dt]
    a = np.asarray(a)
    if a.dtype != npdt:
        a = a.astype(npdt)
    return np.ascontiguousarray(a.reshape(shape))


# version 08
# speedup vs baseline: 1.0001x; 1.0001x over previous
import numpy as np
import ml_dtypes
import concourse.bass as bass
import concourse.mybir as mybir
import bass_rust
from concourse.bass_utils import run_bass_kernel_spmd

F32 = mybir.dt.float32
BF16 = mybir.dt.bfloat16
I32 = mybir.dt.int32
AF = mybir.ActivationFunctionType
OP = mybir.AluOpType
NPBF = ml_dtypes.bfloat16

T = 4096
D = 1024
H = 16
DH = 64
FF = 4096
E = 8
R = 4
NCORES = 8
ND = D // 128
SW = 512
NS = T // SW
NT = T // 128
EPS = 1e-5
CEXP = 20.0
NEGBIG = -1e30
MNEG = -60000.0
MW = 4480

ENGINES = ["sync", "tensor", "vector", "scalar", "gpsimd"]
DMA_QUEUES = ("sync", "gpsimd", "scalar")

DRAM_SPECS = {
    "xT":    ([D, T], F32),
    "yT":    ([D, T], F32),
    "x32a":  ([D, T], F32),
    "x32b":  ([D, T], F32),
    "x32c":  ([D, T], F32),
    "n1":    ([D, T], BF16),
    "n2":    ([D, T], BF16),
    "n3":    ([D, T], BF16),
    "n4":    ([D, T], BF16),
    "h":     ([D, T], BF16),
    "qT":    ([D, T], BF16),
    "kT":    ([D, T], BF16),
    "vtok":  ([T, H * 65], BF16),
    "oT":    ([D, T], BF16),
    "y32":   ([D, T], F32),
    "d32":   ([D, T], F32),
    "moe32": ([D, T], F32),
    "g":     ([FF, T], BF16),
    "qext":  ([H, T], BF16),
    "wrowT": ([E, T], F32),
    "rowsA": ([2, T], F32), "rowsB": ([2, T], F32), "rowsC": ([2, T], F32),
    "rowsD": ([2, T], F32), "rowsE": ([2, T], F32), "rowsF": ([2, T], F32),
    "rowsG": ([2, T], F32), "rowsH": ([2, T], F32), "rowsI": ([2, T], F32),
    "rowsJ": ([2, T], F32), "rowsK": ([2, T], F32),
    "L0_Wq": ([D, D], F32), "L0_Wk": ([D, D], F32), "L0_Wv": ([D, D], F32), "L0_Wo": ([D, D], F32),
    "L0_bq": ([D], F32), "L0_bk": ([D], F32), "L0_bv": ([D], F32), "L0_bo": ([D], F32),
    "L0_gamma": ([D], F32), "L0_beta": ([D], F32),
    "F0_wq": ([D, D], F32), "F0_wk": ([D, D], F32), "F0_wv": ([D, D], F32), "F0_wo": ([D, D], F32),
    "F0_w1": ([D, FF], F32), "F0_w2": ([FF, D], F32), "F0_w3": ([D, FF], F32),
    "F0_attn_nw": ([D], F32), "F0_ff_nw": ([D], F32), "F0_moe_nw": ([D], F32),
    "F0_router": ([D, E], F32),
    "F0_ew1": ([E, D, D], F32), "F0_ew2": ([E, D, D], F32), "F0_ew3": ([E, D, D], F32),
    "L1_WQT": ([D, D], F32), "L1_WKT": ([D, D], F32), "L1_WVT": ([D, D], F32), "L1_WOT": ([D, D], F32),
    "L1_bQ": ([D], F32), "L1_bK": ([D], F32), "L1_bV": ([D], F32), "L1_bO": ([D], F32),
    "L1_bias_stripes": ([H, 128, MW], F32),
    "G1_gateT": ([D, E], F32), "G1_gate_b": ([E], F32),
    "G1_wiT": ([D, FF], F32), "G1_wi_b": ([FF], F32),
    "G1_woT": ([FF, D], F32), "G1_wo_b": ([D], F32),
    "G1_AcatT": ([D, E * R], F32), "G1_BcatT": ([E * R, FF], F32),
}

EXTERNAL_IN = {
    "xT", "L0_Wq", "L0_Wk", "L0_Wv", "L0_Wo", "L0_bq", "L0_bk", "L0_bv", "L0_bo",
    "L0_gamma", "L0_beta",
    "F0_wq", "F0_wk", "F0_wv", "F0_wo", "F0_w1", "F0_w2", "F0_w3",
    "F0_attn_nw", "F0_ff_nw", "F0_moe_nw", "F0_router", "F0_ew1", "F0_ew2", "F0_ew3",
    "L1_WQT", "L1_WKT", "L1_WVT", "L1_WOT", "L1_bQ", "L1_bK", "L1_bV", "L1_bO",
    "L1_bias_stripes",
    "G1_gateT", "G1_gate_b", "G1_wiT", "G1_wi_b", "G1_woT", "G1_wo_b",
    "G1_AcatT", "G1_BcatT",
}
EXTERNAL_OUT = {"yT"}


def sk(name, ic, s):
    return f"D:{name}:{ic}:{s}"


def srd(name, s, n=ND):
    return [sk(name, ic, s) for ic in range(n)]


class Ctx:

    NSEM_DMA = 8

    def __init__(self, nc, external_in, external_out):
        self.nc = nc
        self.ops = []
        self.sb = {}
        self.dram = {}
        self._cms = []
        self.ext_in = external_in
        self.ext_out = external_out
        self.reads = set()
        self.writes = set()
        self.last_writer = {}
        self.part_writers = {}
        self.readers_ = {}
        self.eng_last = {e: None for e in ENGINES}

    def d(self, name):
        if name not in self.dram:
            shape, dt = DRAM_SPECS[name]
            if name in self.ext_in:
                kind = "ExternalInput"
            elif name in self.ext_out:
                kind = "ExternalOutput"
            else:
                kind = "Internal"
            self.dram[name] = self.nc.dram_tensor(name, shape, dt, kind=kind).ap()
        if name in self.ext_in:
            self.reads.add(name)
        if name in self.ext_out:
            self.writes.add(name)
        return self.dram[name]

    def alloc(self, name, shape, dt, psum=False):
        mk = self.nc.psum_tensor if psum else self.nc.sbuf_tensor
        cm = mk(name, shape, dt)
        t = cm.__enter__()
        self._cms.append(cm)
        self.sb[name] = t
        return t

    def close(self):
        for cm in reversed(self._cms):
            cm.__exit__(None, None, None)

    def op(self, eng, fn, inc=1, r=(), w=(), pw=()):
        oid = len(self.ops)
        deps = set()
        for k in r:
            j = self.last_writer.get(k)
            if j is not None:
                deps.add(j)
            deps.update(self.part_writers.get(k, ()))
        for k in w:
            j = self.last_writer.get(k)
            if j is not None:
                deps.add(j)
            deps.update(self.part_writers.get(k, ()))
            deps.update(self.readers_.get(k, ()))
        for k in pw:
            j = self.last_writer.get(k)
            if j is not None:
                deps.add(j)
            deps.update(self.readers_.get(k, ()))
        for k in r:
            self.readers_.setdefault(k, set()).add(oid)
        for k in w:
            self.last_writer[k] = oid
            self.part_writers[k] = []
            self.readers_[k] = set()
        for k in pw:
            self.part_writers.setdefault(k, []).append(oid)
        deps.discard(oid)
        self.ops.append((eng, inc, fn, sorted(deps)))
        self.eng_last[eng] = oid

    def dma(self, out, in_, eng="sync", r=(), w=(), pw=(), **kw):
        self.op(eng, lambda e, o=out, i=in_, k=kw: e.dma_start(out=o, in_=i, **k),
                inc=16, r=r, w=w, pw=pw)

    def mm(self, out, lhsT, rhs, start=True, stop=True, r=(), w=(), pw=(), tp=None):
        self.op("tensor", lambda e, o=out, l=lhsT, rr=rhs, s1=start, s2=stop, t=tp:
                e.matmul(o, l, rr, start=s1, stop=s2, tile_position=t), r=r, w=w, pw=pw)

    def transpose(self, out, in_, ident, r=(), w=(), pw=()):
        self.op("tensor", lambda e, o=out, i=in_, d=ident: e.transpose(o, i, d), r=r, w=w, pw=pw)

    def act(self, out, in_, func, bias=None, scale=None, r=(), w=(), pw=()):
        kw = {}
        if bias is not None:
            kw["bias"] = bias
        if scale is not None:
            kw["scale"] = scale
        self.op("scalar", lambda e, o=out, i=in_, f=func, k=kw:
                e.activation(out=o, in_=i, func=f, **k), r=r, w=w, pw=pw)

    def tt(self, out, a, b, o, r=(), w=(), pw=()):
        self.op("vector", lambda e, oo=out, aa=a, bb=b, op_=o:
                e.tensor_tensor(out=oo, in0=aa, in1=bb, op=op_), r=r, w=w, pw=pw)

    def ts(self, out, in0, s1, op0, s2=None, op1=None, r=(), w=(), pw=()):
        kw = dict(scalar2=s2)
        if op1 is not None:
            kw["op1"] = op1
        self.op("vector", lambda e, o=out, i=in0, a=s1, k=kw, p0=op0:
                e.tensor_scalar(out=o, in0=i, scalar1=a, op0=p0, **k), r=r, w=w, pw=pw)

    def stt(self, out, in0, scalar, in1, op0, op1, r=(), w=(), pw=()):
        self.op("vector", lambda e, o=out, i0=in0, s=scalar, i1=in1, p0=op0, p1=op1:
                e.scalar_tensor_tensor(out=o, in0=i0, scalar=s, in1=i1, op0=p0, op1=p1),
                r=r, w=w, pw=pw)

    def copy(self, out, in_, r=(), w=(), pw=()):
        self.op("vector", lambda e, o=out, i=in_: e.tensor_copy(out=o, in_=i), r=r, w=w, pw=pw)

    def memset(self, ap, v, r=(), w=(), pw=()):
        self.op("vector", lambda e, a=ap, vv=v: e.memset(a, vv), r=r, w=w, pw=pw)

    def rmax(self, out, in_, negate=False, r=(), w=(), pw=()):
        self.op("vector", lambda e, o=out, i=in_, n=negate:
                e.reduce_max(o, i, axis=mybir.AxisListType.X, negate=n), r=r, w=w, pw=pw)

    def rsum(self, out, in_, r=(), w=(), pw=()):
        self.op("vector", lambda e, o=out, i=in_:
                e.reduce_sum(o, i, axis=mybir.AxisListType.X), r=r, w=w, pw=pw)

    def recip(self, out, in_, r=(), w=(), pw=()):
        self.op("vector", lambda e, o=out, i=in_: e.reciprocal(out=o, in_=i), r=r, w=w, pw=pw)

    def iota(self, out, pattern, base, chmul, r=(), w=(), pw=()):
        self.op("gpsimd", lambda e, o=out, p=pattern, b=base, cm=chmul:
                e.iota(o, pattern=p, base=b, channel_multiplier=cm,
                       allow_small_or_imprecise_dtypes=True), r=r, w=w, pw=pw)

    def realize(self):
        nc = self.nc
        n_ops = len(self.ops)
        opinfo = [None] * n_ops
        sem_names = ["c_" + e for e in ENGINES]
        for q in DMA_QUEUES:
            for i in range(self.NSEM_DMA):
                sem_names.append(f"d_{q}_{i}")
        cum = {sn: 0 for sn in sem_names}
        rr = {q: 0 for q in DMA_QUEUES}
        presem = [None] * n_ops
        for oid, (eng, inc, fn, deps) in enumerate(self.ops):
            if inc == 16 and eng in DMA_QUEUES:
                i = rr[eng]
                rr[eng] = (i + 1) % self.NSEM_DMA
                sn = f"d_{eng}_{i}"
                presem[oid] = (sn, cum[sn])
            else:
                sn = "c_" + eng
            cum[sn] += inc
            opinfo[oid] = (sn, cum[sn])
        totals = dict(cum)
        sem_cms = {sn: nc.semaphore("S" + sn) for sn in sem_names}
        sems = {sn: sem_cms[sn].__enter__() for sn in sem_names}
        blk_cm = nc.Block()
        block = blk_cm.__enter__()
        try:
            def mksec(name):
                def body(eng):
                    hw = {sn: 0 for sn in sem_names}
                    for oid, (oeng, inc, fn, deps) in enumerate(self.ops):
                        if oeng != name:
                            continue
                        need = {}
                        for j in deps:
                            js, jcum = opinfo[j]
                            if js == "c_tensor" and name == "tensor":
                                continue
                            if jcum > need.get(js, 0):
                                need[js] = jcum
                        if presem[oid] is not None:
                            sn, cb = presem[oid]
                            if cb > need.get(sn, 0):
                                need[sn] = cb
                        for js, v in sorted(need.items()):
                            if v > hw[js]:
                                eng.wait_ge(sems[js], v)
                                hw[js] = v
                        inst = fn(eng)
                        my_sn = opinfo[oid][0]
                        inst.then_inc(sems[my_sn], inc)
                    if name in ("sync", "gpsimd"):
                        for sn in sem_names:
                            if totals[sn] > 0:
                                eng.wait_ge(sems[sn], totals[sn])
                return body
            for name in ENGINES:
                getattr(block, name)(mksec(name))
        finally:
            blk_cm.__exit__(None, None, None)
            for sn in sem_names:
                sem_cms[sn].__exit__(None, None, None)


def fm(ap_dram, cols, p=128):
    return ap_dram[:, cols].rearrange("(c p) t -> p c t", p=p)


SLOTS = {
    "A0": ("wsb", 0), "A1": ("wsb", 8192),
    "B0": ("wsb2", 0), "B1": ("wsb2", 8192),
    "C0": ("ssb32", 0),
}


def e_prep(c):
    if "wsb" in c.sb:
        return
    c.alloc("wsb", [128, 16384], BF16)
    c.alloc("wsb2", [128, 16384], BF16)
    c.alloc("ssb32", [128, 8192], BF16)
    c.alloc("nsbA", [128, 4096], BF16)
    c.alloc("nsbB", [128, 4096], BF16)
    c.alloc("gsb", [128, 4096], BF16)
    c.alloc("ksbhA", [128, 4096], BF16)
    c.alloc("ksbhB", [128, 4096], BF16)
    c.alloc("vsbhA", [128, NT * 65], BF16)
    c.alloc("vsbhB", [128, NT * 65], BF16)
    c.alloc("Msb", [128, MW], BF16)
    c.alloc("qsbA", [128, SW], BF16)
    c.alloc("qsbB", [128, SW], BF16)
    c.alloc("PtA", [128, SW], BF16)
    c.alloc("PtB", [128, SW], BF16)
    c.alloc("vsb2A", [128, 1040], BF16)
    c.alloc("vsb2B", [128, 1040], BF16)
    c.alloc("xsbA0", [128, SW], F32)
    c.alloc("xsbA1", [128, SW], F32)
    c.alloc("xsbB0", [128, SW], F32)
    c.alloc("xsbB1", [128, SW], F32)
    c.alloc("tf0", [128, SW], F32)
    c.alloc("tf1", [128, SW], F32)
    c.alloc("evB0", [128, SW], BF16)
    c.alloc("evB1", [128, SW], BF16)
    c.alloc("evC0", [128, SW], BF16)
    c.alloc("evC1", [128, SW], BF16)
    c.alloc("xbf0", [128, SW], BF16)
    c.alloc("xbf1", [128, SW], BF16)
    c.alloc("x2bf0", [128, SW], BF16)
    c.alloc("x2bf1", [128, SW], BF16)
    c.alloc("bcmA", [128, SW], F32)
    c.alloc("bcrA", [128, SW], F32)
    c.alloc("bcmB", [128, SW], F32)
    c.alloc("bcrB", [128, SW], F32)
    c.alloc("pcol", [128, 16], F32)
    c.alloc("pcol2", [128, 16], F32)
    c.alloc("gb0", [128, SW], BF16)
    c.alloc("gb1", [128, SW], BF16)
    c.alloc("gc0", [128, SW], BF16)
    c.alloc("gc1", [128, SW], BF16)
    c.alloc("gf0", [128, SW], F32)
    c.alloc("gf1", [128, SW], F32)
    c.alloc("gpc", [128, 16], F32)
    c.alloc("qrw1", [1, SW], F32)
    c.alloc("qrw2", [1, SW], F32)
    c.alloc("rlA1", [1, SW], F32)
    c.alloc("rlA2", [1, SW], F32)
    c.alloc("rbc", [128, SW], F32)
    c.alloc("rtf", [128, 64], F32)
    c.alloc("rtt", [128, 128], F32)
    c.alloc("rpc", [128, 8], F32)
    c.alloc("lsm", [128, 320], BF16)
    c.alloc("bvrow", [1, 1024], BF16)
    c.alloc("gbrow", [1, 8], BF16)
    c.alloc("ones_bf", [128, 1], BF16)
    c.alloc("ones_f", [128, 1], F32)
    c.alloc("onesrow_bf", [1, 128], BF16)
    c.alloc("onesrow_f", [1, 128], F32)
    c.alloc("epscol", [1, 1], F32)
    c.alloc("khat2", [1, H], F32)
    c.alloc("bhat", [1, H], F32)
    c.alloc("identf", [128, 128], F32)
    c.alloc("psA0", [128, SW], F32, psum=True)
    c.alloc("psA1", [128, SW], F32, psum=True)
    c.alloc("psS0", [128, SW], F32, psum=True)
    c.alloc("psS1", [128, SW], F32, psum=True)
    c.alloc("psPV", [128, SW], F32, psum=True)
    c.alloc("psB0", [128, SW], F32, psum=True)
    c.alloc("psB1", [128, SW], F32, psum=True)
    c.alloc("psR", [128, SW], F32, psum=True)

    c.memset(c.sb["ones_bf"][:, :], 1.0, w=["ones"])
    c.memset(c.sb["ones_f"][:, :], 1.0, pw=["ones"])
    c.memset(c.sb["onesrow_bf"][:, :], 1.0, pw=["ones"])
    c.memset(c.sb["onesrow_f"][:, :], 1.0, pw=["ones"])
    c.memset(c.sb["epscol"][:, :], EPS, pw=["ones"])
    c.memset(c.sb["ksbhA"][64:128, :], 0.0, w=["konesA"])
    c.memset(c.sb["ksbhB"][64:128, :], 0.0, w=["konesB"])
    c.memset(c.sb["ksbhA"][64:65, :], 1.0, pw=["konesA"])
    c.memset(c.sb["ksbhB"][64:65, :], 1.0, pw=["konesB"])
    c.memset(c.sb["qsbA"][64:128, :], 0.0, w=["qzA"])
    c.memset(c.sb["qsbB"][64:128, :], 0.0, w=["qzB"])
    ii = c.sb["tf1"]
    c.iota(ii[:, 0:128], [[1, 128]], 0, -1, w=["tf1"])
    c.ts(c.sb["identf"][:, :], ii[:, 0:128], 0.0, OP.is_equal, r=["tf1"], w=["identf"])
    for j in range(4):
        c.iota(ii[:, :], [[1, SW]], -128 * j, -1, w=["tf1"])
        mk = c.sb["Msb"][:, j * SW:(j + 1) * SW]
        tf = c.sb["tf0"]
        c.ts(tf[:, :], ii[:, :], 0.0, OP.is_ge, r=["tf1"], w=["tf0"])
        c.ts(mk, tf[:, :], 1.0, OP.subtract, -NEGBIG, OP.mult, r=["tf0"], pw=["Msb"])


FIN_ROWS = {0: ("rw1", "rw2"), 1: ("rw3", "rw4")}


def _rows_finish(c, s, rms, slot):
    base = 0 if slot == 0 else 64
    skey = f"psR{slot}"
    psR = c.sb["psR"]
    mk, rk = FIN_ROWS[slot]
    rm, rr = c.sb[mk], c.sb[rk]
    if rms:
        c.act(rr[:, :], psR[base + 32:base + 33, :], AF.Ln, bias=c.sb["epscol"][0:1, :],
              scale=1.0 / D, r=[skey, "ones"], w=[rk])
        c.act(rr[:, :], rr[:, :], AF.Exp, scale=-0.5, r=[rk], w=[rk])
        c.memset(rm[:, :], 0.0, w=[mk])
    else:
        c.ts(rm[:, :], psR[base:base + 1, :], -1.0 / D, OP.mult, r=[skey], w=[mk])
        c.tt(rr[:, :], rm[:, :], rm[:, :], OP.mult, r=[mk], w=[rk])
        c.ts(psR[base + 32:base + 33, :], psR[base + 32:base + 33, :], 1.0 / D,
             OP.mult, pw=[skey])
        c.tt(psR[base + 32:base + 33, :], psR[base + 32:base + 33, :],
             rr[:, :], OP.subtract, r=[rk], pw=[skey])
        c.act(rr[:, :], psR[base + 32:base + 33, :], AF.Ln,
              bias=c.sb["epscol"][0:1, :], r=[skey, "ones"], w=[rk])
        c.act(rr[:, :], rr[:, :], AF.Exp, scale=-0.5, r=[rk], w=[rk])


def _rows_store(c, rows_name, s, slot):
    rows = c.d(rows_name)
    cols = slice(s * SW, (s + 1) * SW)
    mk, rk = FIN_ROWS[slot]
    c.dma(rows[0:1, cols], c.sb[mk][:, :], eng="scalar", r=[mk],
          pw=[f"D:{rows_name}:{s}"])
    c.dma(rows[1:2, cols], c.sb[rk][:, :], eng="scalar", r=[rk],
          pw=[f"D:{rows_name}:{s}"])


def collect_chunk(c, src_ap, src_key, p, first, last, slot, rms=False, claim=False):
    base = 0 if slot == 0 else 64
    skey = f"psR{slot}"
    ob = c.sb["ones_bf"]
    x2 = c.sb[f"x2bf{p}"]
    if src_ap.dtype != BF16:
        xb = c.sb[f"xbf{p}"]
        c.copy(xb[:, :], src_ap, r=[src_key], w=[f"xbf{p}"])
        src_ap, src_key = xb[:, :], f"xbf{p}"
    wk = dict(w=[skey]) if claim else dict(pw=[skey])
    if not rms:
        c.mm(c.sb["psR"][base:base + 1, :], ob[:, :], src_ap,
             start=first, stop=last, r=[src_key, "ones"], **wk)
        wk = dict(pw=[skey])
    c.act(x2[:, :], src_ap, AF.Square, r=[src_key], w=[f"x2bf{p}"])
    c.mm(c.sb["psR"][base + 32:base + 33, :], ob[:, :], x2[:, :],
         start=first, stop=last, r=[f"x2bf{p}", "ones"],
         tp=(0, 96) if base == 64 else None, **wk)


def e_stats(c, src_name, rows_name, rms=False, slot=0):
    src = c.d(src_name)
    for s in range(NS):
        cols = slice(s * SW, (s + 1) * SW)
        for ic in range(ND):
            p = ic % 2
            xb = c.sb[f"xbf{p}"]
            c.dma(xb[:, :], src[ic * 128:(ic + 1) * 128, cols], eng="gpsimd",
                  r=[sk(src_name, ic, s)], w=[f"xbf{p}"])
            collect_chunk(c, xb[:, :], f"xbf{p}", p, ic == 0, ic == ND - 1, slot, rms,
                          claim=(s == 0 and ic == 0))
        _rows_finish(c, s, rms, slot)
        _rows_store(c, rows_name, s, slot)


BC_ROWS = {"A": ("rlA1", "rlA2"), "B": ("qrw1", "qrw2")}


def bcast_rows(c, rows_name, s, pair):
    rows = c.d(rows_name)
    cols = slice(s * SW, (s + 1) * SW)
    k1, k2 = BC_ROWS[pair]
    r1, r2 = c.sb[k1], c.sb[k2]
    c.dma(r1[:, :], rows[0:1, cols], eng="scalar", r=[f"D:{rows_name}:{s}"], w=[k1])
    c.dma(r2[:, :], rows[1:2, cols], eng="scalar", r=[f"D:{rows_name}:{s}"], w=[k2])
    orf = c.sb["onesrow_f"]
    bm, br = c.sb["bcm" + pair], c.sb["bcr" + pair]
    pb = "psB0" if pair == "A" else "psB1"
    c.mm(c.sb[pb][:, :], orf[0:1, :], r1[:, :], r=[k1, "ones"], w=[pb])
    c.copy(bm[:, :], c.sb[pb][:, :], r=[pb], w=["bcm" + pair])
    c.mm(c.sb[pb][:, :], orf[0:1, :], r2[:, :], r=[k2, "ones"], w=[pb])
    c.copy(br[:, :], c.sb[pb][:, :], r=[pb], w=["bcr" + pair])


def load_pcols(c, bufname, dram_name, scale=None, n=ND, off=0):
    p = c.d(dram_name)
    buf = c.sb[bufname]
    for ic in range(n):
        col = buf[:, ic:ic + 1]
        src = p.rearrange("(c p) -> c p", p=128)[off + ic:off + ic + 1, :].transpose([1, 0])
        c.dma(col, src, eng="scalar", r=["D:" + dram_name], pw=[bufname])
    if scale is not None:
        c.ts(buf[:, 0:n], buf[:, 0:n], scale, OP.mult, r=[bufname], w=[bufname])


def e_apply(c, src_name, rows_name, out_name, out_dt=BF16, gamma=None, beta=None,
            rmsw=None, resid_name=None, resid_dt=F32, collect_rows=None, slot=0,
            pair="A", pbuf="pcol"):
    src, out = c.d(src_name), c.d(out_name)
    resid = c.d(resid_name) if resid_name else None
    if gamma is not None:
        load_pcols(c, pbuf, gamma)
        load_pcols(c, "pcol2", beta)
    if rmsw is not None:
        load_pcols(c, pbuf, rmsw)
    bm, br = c.sb["bcm" + pair], c.sb["bcr" + pair]
    bmk, brk = "bcm" + pair, "bcr" + pair
    for s in range(NS):
        cols = slice(s * SW, (s + 1) * SW)
        bcast_rows(c, rows_name, s, pair)
        for ic in range(ND):
            p = ic % 2
            rr = slice(ic * 128, (ic + 1) * 128)
            xA, tA = c.sb[f"xsbA{p}"], c.sb[f"tf{p}"]
            kx, kt = f"xsbA{p}", f"tf{p}"
            c.dma(xA[:, :], src[rr, cols], r=[sk(src_name, ic, s)], w=[kx])
            c.tt(tA[:, :], xA[:, :], bm[:, :], OP.add, r=[kx, bmk], w=[kt])
            if gamma is not None:
                c.stt(tA[:, :], tA[:, :], c.sb[pbuf][:, ic:ic + 1], br[:, :],
                      OP.mult, OP.mult, r=[kt, pbuf, brk], w=[kt])
                c.ts(tA[:, :], tA[:, :], c.sb["pcol2"][:, ic:ic + 1], OP.add,
                     r=[kt, "pcol2"], w=[kt])
            elif rmsw is not None:
                c.stt(tA[:, :], tA[:, :], c.sb[pbuf][:, ic:ic + 1], br[:, :],
                      OP.mult, OP.mult, r=[kt, pbuf, brk], w=[kt])
            else:
                c.tt(tA[:, :], tA[:, :], br[:, :], OP.mult, r=[kt, brk], w=[kt])
            if resid is not None:
                if resid_dt == F32:
                    rb, rbk = c.sb[f"xsbB{p}"], f"xsbB{p}"
                else:
                    rb, rbk = c.sb[f"evC{p}"], f"evC{p}"
                c.dma(rb[:, :], resid[rr, cols], r=[sk(resid_name, ic, s)], w=[rbk])
                c.tt(tA[:, :], tA[:, :], rb[:, :], OP.add, r=[kt, rbk], w=[kt])
            if out_dt == F32:
                c.dma(out[rr, cols], tA[:, :], r=[kt], w=[sk(out_name, ic, s)])
            else:
                ev = c.sb[f"evB{p}"]
                c.copy(ev[:, :], tA[:, :], r=[kt], w=[f"evB{p}"])
                c.dma(out[rr, cols], ev[:, :], r=[f"evB{p}"], w=[sk(out_name, ic, s)])
            if collect_rows is not None:
                collect_chunk(c, tA[:, :], kt, p, ic == 0, ic == ND - 1, slot,
                              claim=(s == 0 and ic == 0))
        if collect_rows is not None:
            _rows_finish(c, s, False, slot)
            _rows_store(c, collect_rows, s, slot)


def slot_ap(c, slot):
    buf, off = SLOTS[slot]
    return c.sb[buf].ap()[:, off:off + 8192]


def load_w_slot(c, w_name, slot, w_slice=None):
    w = c.d(w_name)
    if w_slice is not None:
        w = w_slice
    c.dma(slot_ap(c, slot).rearrange("p (c j) -> p c j", c=ND),
          w.rearrange("(c p) j -> p c j", p=128), eng="gpsimd",
          r=["D:" + w_name], w=[slot])


def e_proj(c, w_name, src_name, dst_name, slot, bias=None, scale=None, out_dt=BF16,
           src_reader=None, collect_rows=None, collect_rms=False, cslot=0):
    src, dst = c.d(src_name), c.d(dst_name)
    wap = slot_ap(c, slot)
    load_w_slot(c, w_name, slot)
    if bias is not None:
        load_pcols(c, "gpc", bias, scale=scale)
    for s in range(NS):
        sp = s % 2
        nsb = c.sb["nsbA" if sp == 0 else "nsbB"]
        nkey = "nsbA" if sp == 0 else "nsbB"
        cols = slice(s * SW, (s + 1) * SW)
        rkeys = src_reader(s) if src_reader else srd(src_name, s)
        c.dma(nsb[:, :].rearrange("p (c t) -> p c t", c=ND),
              fm(src, cols), r=rkeys, w=[nkey])
        for oc in range(ND):
            p = oc % 2
            ps = c.sb[f"psA{p}"]
            for ic in range(ND):
                c.mm(ps[:, :], wap[:, ic * 1024 + oc * 128: ic * 1024 + (oc + 1) * 128],
                     nsb[:, ic * SW:(ic + 1) * SW], start=(ic == 0), stop=(ic == ND - 1),
                     r=[slot, nkey], w=[f"psA{p}"] if ic == 0 else (),
                     pw=[f"psA{p}"] if ic > 0 else ())
            if out_dt == BF16:
                ev, ekey = c.sb[f"gb{p}"], f"gb{p}"
            else:
                ev, ekey = c.sb[f"gf{p}"], f"gf{p}"
            if bias is not None:
                c.ts(ev[:, :], ps[:, :], scale if scale is not None else 1.0, OP.mult,
                     c.sb["gpc"][:, oc:oc + 1], OP.add, r=[f"psA{p}", "gpc"], w=[ekey])
            elif scale is not None:
                c.ts(ev[:, :], ps[:, :], scale, OP.mult, r=[f"psA{p}"], w=[ekey])
            else:
                c.copy(ev[:, :], ps[:, :], r=[f"psA{p}"], w=[ekey])
            c.dma(dst[oc * 128:(oc + 1) * 128, cols], ev[:, :],
                  r=[ekey], w=[sk(dst_name, oc, s)])
            if collect_rows is not None:
                collect_chunk(c, ev[:, :], ekey, p, oc == 0, oc == ND - 1, cslot,
                              rms=collect_rms, claim=(s == 0 and oc == 0))
        if collect_rows is not None:
            _rows_finish(c, s, collect_rms, cslot)
            _rows_store(c, collect_rows, s, cslot)


def e_qk(c, w_name, src_name, dst_name, role, slot, bias=None, use_bhat=False):
    src, dst = c.d(src_name), c.d(dst_name)
    wap = slot_ap(c, slot)
    scale = 0.125 if role == "q" else None
    qext = c.d("qext") if role == "q" else None
    load_w_slot(c, w_name, slot)
    if bias is not None:
        load_pcols(c, "gpc", bias, scale=scale)
    if role == "k":
        c.memset(c.sb["khat2"][:, :], 0.0, w=["khat2"])
    ob = c.sb["ones_bf"]
    for s in range(NS):
        sp = s % 2
        nsb = c.sb["nsbA" if sp == 0 else "nsbB"]
        nkey = "nsbA" if sp == 0 else "nsbB"
        cols = slice(s * SW, (s + 1) * SW)
        c.dma(nsb[:, :].rearrange("p (c t) -> p c t", c=ND), fm(src, cols),
              r=srd(src_name, s), w=[nkey])
        for oc in range(ND):
            p = oc % 2
            ps = c.sb[f"psA{p}"]
            for ic in range(ND):
                c.mm(ps[:, :], wap[:, ic * 1024 + oc * 128: ic * 1024 + (oc + 1) * 128],
                     nsb[:, ic * SW:(ic + 1) * SW], start=(ic == 0), stop=(ic == ND - 1),
                     r=[slot, nkey], w=[f"psA{p}"] if ic == 0 else (),
                     pw=[f"psA{p}"] if ic > 0 else ())
            kb = c.sb[f"gb{p}"]
            if bias is not None:
                c.ts(kb[:, :], ps[:, :], scale if scale else 1.0, OP.mult,
                     c.sb["gpc"][:, oc:oc + 1], OP.add, r=[f"psA{p}", "gpc"], w=[f"gb{p}"])
            elif scale is not None:
                c.ts(kb[:, :], ps[:, :], scale, OP.mult, r=[f"psA{p}"], w=[f"gb{p}"])
            else:
                c.copy(kb[:, :], ps[:, :], r=[f"psA{p}"], w=[f"gb{p}"])
            c.dma(dst[oc * 128:(oc + 1) * 128, cols], kb[:, :], r=[f"gb{p}"],
                  w=[sk(dst_name, oc, s)])
            k2 = c.sb["PtA" if p == 0 else "PtB"]
            k2key = "PtA" if p == 0 else "PtB"
            c.act(k2[:, :], kb[:, :], AF.Square, r=[f"gb{p}"], w=[k2key])
            for g in range(2):
                hh = 2 * oc + g
                psr = c.sb["psS0" if g == 0 else "psS1"]
                pk = "psS0" if g == 0 else "psS1"
                c.mm(psr[0:1, :], ob[64 * g:64 * g + 64, :], k2[64 * g:64 * g + 64, :],
                     r=[k2key, "ones"], w=[pk])
                if role == "k":
                    c.rmax(c.sb["qrw2"][0:1, 0:1], psr[0:1, :], r=[pk], w=["qrw2"])
                    c.tt(c.sb["khat2"][0:1, hh:hh + 1], c.sb["khat2"][0:1, hh:hh + 1],
                         c.sb["qrw2"][0:1, 0:1], OP.max, r=["khat2", "qrw2"], w=["khat2"])
                else:
                    r1 = c.sb["qrw1"]
                    c.ts(r1[:, :], psr[0:1, :], c.sb["khat2"][0:1, hh:hh + 1], OP.mult,
                         r=[pk, "khat2"], w=["qrw1"])
                    c.act(r1[:, :], r1[:, :], AF.Sqrt, r=["qrw1"], w=["qrw1"])
                    if use_bhat:
                        c.ts(r1[:, :], r1[:, :], c.sb["bhat"][0:1, hh:hh + 1], OP.add,
                             r=["qrw1", "bhat"], w=["qrw1"])
                    c.ts(r1[:, :], r1[:, :], -1.0, OP.mult, CEXP, OP.add,
                         r=["qrw1"], w=["qrw1"])
                    c.dma(qext[hh:hh + 1, cols], r1[:, :], eng="gpsimd", r=["qrw1"],
                          w=[f"D:qext:{hh}:{s}"])


def e_v(c, w_name, src_name, slot, bias=None):
    src, vtok = c.d(src_name), c.d("vtok")
    wap = slot_ap(c, slot)
    load_w_slot(c, w_name, slot)
    if bias is not None:
        b = c.d(bias)
        row = bass_rust.AP(tensor=b.tensor, offset=0, ap=[[D, 1], [1, D]])
        c.dma(c.sb["bvrow"][0:1, :], row, eng="gpsimd", r=["D:" + bias], w=["bvrow"])
    orb = c.sb["onesrow_bf"]
    for nm in ("vsb2A", "vsb2B"):
        vv = c.sb[nm].ap().rearrange("p (h m) -> p h m", m=65)
        c.memset(vv[:, :, 64:65], 1.0, w=[nm + ":ones"])
    for s in range(NS):
        sp = s % 2
        nsb = c.sb["nsbA" if sp == 0 else "nsbB"]
        nkey = "nsbA" if sp == 0 else "nsbB"
        cols = slice(s * SW, (s + 1) * SW)
        c.dma(nsb[:, :].rearrange("p (c t) -> p c t", c=ND), fm(src, cols),
              r=srd(src_name, s), w=[nkey])
        for tt in range(SW // 128):
            vp = tt % 2
            vsb2 = c.sb["vsb2A" if vp == 0 else "vsb2B"]
            vkey = "vsb2A" if vp == 0 else "vsb2B"
            vv = vsb2.ap().rearrange("p (h m) -> p h m", m=65)
            for jh in range(2):
                ps = c.sb[f"psA{jh}"]
                for ic in range(ND):
                    c.mm(ps[:, :], nsb[:, ic * SW + tt * 128: ic * SW + (tt + 1) * 128],
                         wap[:, ic * 1024 + jh * 512: ic * 1024 + (jh + 1) * 512],
                         start=(ic == 0), stop=(ic == ND - 1) and bias is None,
                         r=[nkey, slot], w=[f"psA{jh}"] if ic == 0 else (),
                         pw=[f"psA{jh}"] if ic > 0 else ())
                if bias is not None:
                    c.mm(ps[:, :], orb[0:1, :],
                         c.sb["bvrow"][0:1, jh * 512:(jh + 1) * 512],
                         start=False, stop=True,
                         r=["ones", "bvrow"], pw=[f"psA{jh}"])
                dstv = vv[:, jh * 8:(jh + 1) * 8, 0:64]
                srcv = ps.ap().rearrange("p (hh m) -> p hh m", m=64)
                c.copy(dstv, srcv, r=[f"psA{jh}"], pw=[vkey])
            t0 = s * SW + tt * 128
            c.dma(vtok[t0:t0 + 128, :], vsb2[:, :], r=[vkey, vkey + ":ones"],
                  w=[f"D:vtok:{s * 4 + tt}"])


def e_attn(c, use_bias=False):
    qT, kT, vtok, qext, oT = c.d("qT"), c.d("kT"), c.d("vtok"), c.d("qext"), c.d("oT")
    masks = c.sb["Msb"]
    if use_bias:
        stripes = c.d("L1_bias_stripes")
        Msb = c.sb["Msb"]
    for h in range(H):
        hp = h % 2
        ksbh = c.sb["ksbhA" if hp == 0 else "ksbhB"]
        vsbh = c.sb["vsbhA" if hp == 0 else "vsbhB"]
        kk = "ksbhA" if hp == 0 else "ksbhB"
        vk = "vsbhA" if hp == 0 else "vsbhB"
        c.dma(ksbh[0:64, :], kT[64 * h:64 * h + 64, :], eng="gpsimd",
              r=[sk("kT", h // 2, s) for s in range(NS)], w=[kk])
        c.dma(vsbh[:, :].rearrange("p (k m) -> p k m", m=65),
              vtok.rearrange("(k p) m -> p k m", p=128)[:, :, 65 * h:65 * h + 65],
              eng="gpsimd", r=[f"D:vtok:{t}" for t in range(NT)], w=[vk])
        if use_bias:
            c.dma(Msb[:, :], stripes[h, :, :], eng="gpsimd",
                  r=["D:L1_bias_stripes"], w=["Msb"])
        konk = "konesA" if hp == 0 else "konesB"
        for s in range(NS):
            q0 = s * SW
            cols = slice(q0, q0 + SW)
            sp = s % 2
            qsb = c.sb["qsbA" if sp == 0 else "qsbB"]
            qk = "qsbA" if sp == 0 else "qsbB"
            qz = "qzA" if sp == 0 else "qzB"
            c.dma(qsb[0:64, :], qT[64 * h:64 * h + 64, cols],
                  r=[sk("qT", h // 2, s)], pw=[qk + ":q"])
            c.dma(qsb[64:65, :], qext[h:h + 1, cols],
                  r=[f"D:qext:{h}:{s}", qz], pw=[qk + ":x"])
            nk = 4 * (s + 1)
            SB = ["psS0", "psS1", "psA0", "psA1"]
            PT = ["PtA", "PtB", "gc0", "gc1"]

            def emit_S(k):
                bk = SB[k % 4]
                c.mm(c.sb[bk][:, :], ksbh[0:128, 128 * k:128 * (k + 1)], qsb[0:128, :],
                     start=True, stop=True,
                     r=[kk, konk, qk + ":q", qk + ":x", qz], w=[bk])
            for k in range(min(4, nk)):
                emit_S(k)
            for k in range(nk):
                bk = SB[k % 4]
                ps = c.sb[bk]
                if use_bias:
                    y0 = q0 - 128 * k + 384
                    c.tt(ps[:, :], ps[:, :], Msb[:, y0:y0 + SW], OP.add,
                         r=[bk, "Msb"], w=[bk])
                elif k >= 4 * s:
                    j = k - 4 * s
                    c.tt(ps[:, :], ps[:, :], masks[:, j * SW:(j + 1) * SW], OP.add,
                         r=[bk, "Msb"], w=[bk])
                pk = PT[k % 4]
                Pt = c.sb[pk]
                c.act(Pt[:, :], ps[:, :], AF.Exp, r=[bk], w=[pk])
                if k + 4 < nk:
                    emit_S(k + 4)
                c.mm(c.sb["psPV"][0:65, :], vsbh[:, 65 * k:65 * (k + 1)], Pt[:, :],
                     start=(k == 0), stop=(k == nk - 1),
                     r=[vk, pk], w=["psPV"] if k == 0 else (),
                     pw=["psPV"] if k > 0 else ())
            c.recip(c.sb["qrw1"][:, :], c.sb["psPV"][64:65, :], r=["psPV"], w=["qrw1"])
            c.mm(c.sb["psB0"][0:64, :], c.sb["onesrow_f"][0:1, 0:64], c.sb["qrw1"][:, :],
                 r=["qrw1", "ones"], w=["psB0"])
            c.copy(c.sb["rbc"][0:64, :], c.sb["psB0"][0:64, :], r=["psB0"], w=["rbc"])
            ev = c.sb["gb0" if sp == 0 else "gb1"]
            ek = "gb0" if sp == 0 else "gb1"
            c.tt(ev[0:64, :], c.sb["psPV"][0:64, :], c.sb["rbc"][0:64, :], OP.mult,
                 r=["psPV", "rbc"], w=[ek])
            c.dma(oT[64 * h:64 * h + 64, cols], ev[0:64, :], r=[ek],
                  w=[f"D:oT:{h}:{s}"])


def oT_reader(s):
    return [f"D:oT:{h}:{s}" for h in range(H)]


def e_router(c, src_name):
    n, rw, wrowT = c.d(src_name), c.d("F0_router"), c.d("wrowT")
    rsb = c.sb["gsb"]
    c.dma(rsb[:, 0:64].rearrange("p (c j) -> p c j", c=ND),
          rw.rearrange("(c p) j -> p c j", p=128), eng="gpsimd",
          r=["D:F0_router"], w=["gsb"])
    lg = c.sb["rtf"]
    pc = c.sb["rpc"]
    for s in range(NS):
        sp = s % 2
        nsb = c.sb["nsbA" if sp == 0 else "nsbB"]
        nkey = "nsbA" if sp == 0 else "nsbB"
        cols = slice(s * SW, (s + 1) * SW)
        c.dma(nsb[:, :].rearrange("p (c t) -> p c t", c=ND), fm(n, cols),
              r=srd(src_name, s), w=[nkey])
        for tt in range(SW // 128):
            tp = tt % 2
            ps = c.sb[f"psA{tp}"]
            for ic in range(ND):
                c.mm(ps[:, 0:8], nsb[:, ic * SW + tt * 128: ic * SW + (tt + 1) * 128],
                     rsb[:, ic * 8:(ic + 1) * 8], start=(ic == 0), stop=(ic == ND - 1),
                     r=[nkey, "gsb"], w=[f"psA{tp}"] if ic == 0 else (),
                     pw=[f"psA{tp}"] if ic > 0 else ())
            c.rmax(pc[:, 0:1], ps[:, 0:8], negate=True, r=[f"psA{tp}"], w=["rpc"])
            c.act(lg[:, 0:8], ps[:, 0:8], AF.Exp, bias=pc[:, 0:1],
                  r=[f"psA{tp}", "rpc"], w=["rtf"])
            c.rsum(pc[:, 1:2], lg[:, 0:8], r=["rtf"], pw=["rpc"])
            c.recip(pc[:, 2:3], pc[:, 1:2], r=["rpc"], pw=["rpc"])
            c.ts(lg[:, 0:8], lg[:, 0:8], pc[:, 2:3], OP.mult, r=["rtf", "rpc"], w=["rtf"])
            c.rmax(pc[:, 3:4], lg[:, 0:8], r=["rtf"], pw=["rpc"])
            c.ts(lg[:, 8:16], lg[:, 0:8], pc[:, 3:4], OP.is_equal, r=["rtf", "rpc"], w=["rtf:m1"])
            c.ts(lg[:, 16:24], lg[:, 8:16], -1.0, OP.mult, 1.0, OP.add, r=["rtf:m1"], w=["rtf:nm"])
            c.tt(lg[:, 16:24], lg[:, 0:8], lg[:, 16:24], OP.mult, r=["rtf", "rtf:nm"], w=["rtf:nm"])
            c.rmax(pc[:, 4:5], lg[:, 16:24], r=["rtf:nm"], pw=["rpc"])
            c.ts(lg[:, 16:24], lg[:, 16:24], pc[:, 4:5], OP.is_equal, r=["rtf:nm", "rpc"], w=["rtf:nm"])
            c.ts(lg[:, 8:16], lg[:, 8:16], pc[:, 3:4], OP.mult, r=["rtf:m1", "rpc"], w=["rtf:m1"])
            c.ts(lg[:, 16:24], lg[:, 16:24], pc[:, 4:5], OP.mult, r=["rtf:nm", "rpc"], w=["rtf:nm"])
            c.tt(lg[:, 8:16], lg[:, 8:16], lg[:, 16:24], OP.add, r=["rtf:m1", "rtf:nm"], w=["rtf:m1"])
            c.transpose(c.sb["psPV"][0:8, 0:128], lg[:, 8:16], c.sb["identf"][:, :],
                        r=["rtf:m1", "identf"], w=["psPV"])
            c.copy(c.sb["rtt"][0:8, 0:128], c.sb["psPV"][0:8, 0:128], r=["psPV"], w=["rtt"])
            t0 = s * SW + tt * 128
            c.dma(wrowT[0:8, t0:t0 + 128], c.sb["rtt"][0:8, 0:128], eng="scalar",
                  r=["rtt"], pw=[f"D:wrowT:{s}"])


def e_moe(c, src_name, collect_rows=None, cslot=0):
    n, moe32, wrowT = c.d(src_name), c.d("moe32"), c.d("wrowT")
    ew1, ew2, ew3 = c.d("F0_ew1"), c.d("F0_ew2"), c.d("F0_ew3")
    for e_i in range(E):
        s1slot = "A0" if e_i % 2 == 0 else "B1"
        s3slot = "A1" if e_i % 2 == 0 else "C0"
        s2slot = "B0" if e_i % 2 == 0 else "A0"
        load_w_slot(c, "F0_ew1", s1slot, w_slice=ew1[e_i])
        load_w_slot(c, "F0_ew3", s3slot, w_slice=ew3[e_i])
        load_w_slot(c, "F0_ew2", s2slot, w_slice=ew2[e_i])
        w1b, w3b, w2b = slot_ap(c, s1slot), slot_ap(c, s3slot), slot_ap(c, s2slot)
        for s in range(NS):
            sp = s % 2
            nsb = c.sb["nsbA" if sp == 0 else "nsbB"]
            nkey = "nsbA" if sp == 0 else "nsbB"
            cols = slice(s * SW, (s + 1) * SW)
            c.dma(nsb[:, :].rearrange("p (c t) -> p c t", c=ND), fm(n, cols),
                  r=srd(src_name, s), w=[nkey])
            c.dma(c.sb["qrw2"][:, :], wrowT[e_i:e_i + 1, cols], eng="scalar",
                  r=[f"D:wrowT:{s}"], w=["qrw2"])
            c.mm(c.sb["psB0"][:, :], c.sb["onesrow_f"][0:1, :], c.sb["qrw2"][:, :],
                 r=["qrw2", "ones"], w=["psB0"])
            c.copy(c.sb["rbc"][:, :], c.sb["psB0"][:, :], r=["psB0"], w=["rbc"])
            for eh in range(ND):
                p = eh % 2
                psu = c.sb[f"psA{p}"]
                for ic in range(ND):
                    c.mm(psu[:, :], w1b[:, ic * 1024 + eh * 128: ic * 1024 + (eh + 1) * 128],
                         nsb[:, ic * SW:(ic + 1) * SW], start=(ic == 0), stop=(ic == ND - 1),
                         r=[s1slot, nkey], w=[f"psA{p}"] if ic == 0 else (),
                         pw=[f"psA{p}"] if ic > 0 else ())
                s1 = c.sb[f"gb{p}"]
                c.act(s1[:, :], psu[:, :], AF.Silu, r=[f"psA{p}"], w=[f"gb{p}"])
                psv = c.sb[f"psB{p}"]
                for ic in range(ND):
                    c.mm(psv[:, :], w3b[:, ic * 1024 + eh * 128: ic * 1024 + (eh + 1) * 128],
                         nsb[:, ic * SW:(ic + 1) * SW], start=(ic == 0), stop=(ic == ND - 1),
                         r=[s3slot, nkey], w=[f"psB{p}"] if ic == 0 else (),
                         pw=[f"psB{p}"] if ic > 0 else ())
                gslice = c.sb["gsb"][:, eh * SW:(eh + 1) * SW]
                c.tt(gslice, s1[:, :], psv[:, :], OP.mult,
                     r=[f"gb{p}", f"psB{p}"], pw=["gsb"])
                c.tt(gslice, gslice, c.sb["rbc"][:, :], OP.mult,
                     r=["gsb", "rbc"], pw=["gsb"])
            for dc in range(ND):
                p = dc % 2
                psu = c.sb[f"psA{p}"]
                for eh in range(ND):
                    c.mm(psu[:, :], w2b[:, eh * 1024 + dc * 128: eh * 1024 + (dc + 1) * 128],
                         c.sb["gsb"][:, eh * SW:(eh + 1) * SW], start=(eh == 0),
                         stop=(eh == ND - 1), r=[s2slot, "gsb"],
                         w=[f"psA{p}"] if eh == 0 else (), pw=[f"psA{p}"] if eh > 0 else ())
                ev = c.sb[f"gf{p}"]
                mk = sk("moe32", dc, s)
                if e_i == 0:
                    c.copy(ev[:, :], psu[:, :], r=[f"psA{p}"], w=[f"gf{p}"])
                else:
                    xb = c.sb[f"xsbB{p}"]
                    c.dma(xb[:, :], moe32[dc * 128:(dc + 1) * 128, cols],
                          r=[mk], w=[f"xsbB{p}"])
                    c.tt(ev[:, :], psu[:, :], xb[:, :], OP.add,
                         r=[f"psA{p}", f"xsbB{p}"], w=[f"gf{p}"])
                c.dma(moe32[dc * 128:(dc + 1) * 128, cols], ev[:, :],
                      r=[f"gf{p}"], w=[mk])
                if collect_rows is not None and e_i == E - 1:
                    collect_chunk(c, ev[:, :], f"gf{p}", p, dc == 0, dc == ND - 1,
                                  cslot, rms=True, claim=(s == 0 and dc == 0))
            if collect_rows is not None and e_i == E - 1:
                _rows_finish(c, s, True, cslot)
                _rows_store(c, collect_rows, s, cslot)


def e_glu(c):
    h, w1, w3, g = c.d("h"), c.d("F0_w1"), c.d("F0_w3"), c.d("g")
    slot_rot = [("A0", "A1"), ("B0", "B1"), ("C0", "A0"), ("A1", "B0")]
    for it, og in enumerate(range(0, FF, 1024)):
        s1slot, s3slot = slot_rot[it]
        load_w_slot(c, "F0_w1", s1slot, w_slice=w1[:, og:og + 1024])
        load_w_slot(c, "F0_w3", s3slot, w_slice=w3[:, og:og + 1024])
        wap1, wap3 = slot_ap(c, s1slot), slot_ap(c, s3slot)
        for s in range(NS):
            sp = s % 2
            nsb = c.sb["nsbA" if sp == 0 else "nsbB"]
            nkey = "nsbA" if sp == 0 else "nsbB"
            cols = slice(s * SW, (s + 1) * SW)
            c.dma(nsb[:, :].rearrange("p (c t) -> p c t", c=ND), fm(h, cols),
                  r=srd("h", s), w=[nkey])
            for j in range(8):
                p = j % 2
                psu, psv = c.sb[f"psA{p}"], c.sb[f"psB{p}"]
                for ic in range(ND):
                    c.mm(psu[:, :], wap1[:, ic * 1024 + j * 128: ic * 1024 + (j + 1) * 128],
                         nsb[:, ic * SW:(ic + 1) * SW], start=(ic == 0), stop=(ic == ND - 1),
                         r=[s1slot, nkey], w=[f"psA{p}"] if ic == 0 else (),
                         pw=[f"psA{p}"] if ic > 0 else ())
                s1 = c.sb[f"gb{p}"]
                c.act(s1[:, :], psu[:, :], AF.Silu, r=[f"psA{p}"], w=[f"gb{p}"])
                for ic in range(ND):
                    c.mm(psv[:, :], wap3[:, ic * 1024 + j * 128: ic * 1024 + (j + 1) * 128],
                         nsb[:, ic * SW:(ic + 1) * SW], start=(ic == 0), stop=(ic == ND - 1),
                         r=[s3slot, nkey], w=[f"psB{p}"] if ic == 0 else (),
                         pw=[f"psB{p}"] if ic > 0 else ())
                ev = c.sb[f"gc{p}"]
                c.tt(ev[:, :], s1[:, :], psv[:, :], OP.mult,
                     r=[f"gb{p}", f"psB{p}"], w=[f"gc{p}"])
                jg = og // 128 + j
                c.dma(g[og + j * 128: og + (j + 1) * 128, cols], ev[:, :],
                      r=[f"gc{p}"], w=[f"D:g:{jg}:{s}"])


def e_projK32(c, w_name, src_name, dst_name, bias=None, collect_rows=None,
              collect_rms=False, cslot=0):
    w, src, dst = c.d(w_name), c.d(src_name), c.d(dst_name)
    ssb = c.sb["ssb32"]
    if bias is not None:
        load_pcols(c, "gpc", bias)
    c.dma(c.sb["wsb"][:, :].rearrange("p (c j) -> p c j", c=16),
          w[0:2048, :].rearrange("(c p) j -> p c j", p=128), eng="gpsimd",
          r=["D:" + w_name], w=["A0", "A1"])
    c.dma(c.sb["wsb2"][:, :].rearrange("p (c j) -> p c j", c=16),
          w[2048:4096, :].rearrange("(c p) j -> p c j", p=128), eng="gpsimd",
          r=["D:" + w_name], w=["B0", "B1"])
    for s in range(NS):
        cols = slice(s * SW, (s + 1) * SW)
        c.dma(ssb[:, :].rearrange("p (c t) -> p c t", c=16),
              fm(src[0:2048, :], cols),
              r=[f"D:{src_name}:{jg}:{s}" for jg in range(16)], w=["C0"])
        c.dma(c.sb["nsbA"][:, :].rearrange("p (c t) -> p c t", c=8),
              fm(src[2048:3072, :], cols),
              r=[f"D:{src_name}:{jg}:{s}" for jg in range(16, 24)], w=["nsbA"])
        c.dma(c.sb["nsbB"][:, :].rearrange("p (c t) -> p c t", c=8),
              fm(src[3072:4096, :], cols),
              r=[f"D:{src_name}:{jg}:{s}" for jg in range(24, 32)], w=["nsbB"])
        for s_ in [s]:
            pass
        for oc in range(ND):
            p = oc % 2
            ps = c.sb[f"psA{p}"]
            for ic in range(32):
                wt = (c.sb["wsb"] if ic < 16 else c.sb["wsb2"])[
                    :, (ic % 16) * 1024 + oc * 128:(ic % 16) * 1024 + (oc + 1) * 128]
                rslot = ("A0" if ic < 8 else "A1") if ic < 16 else ("B0" if ic < 24 else "B1")
                if ic < 16:
                    act = ssb[:, ic * SW:(ic + 1) * SW]
                    akey = "C0"
                elif ic < 24:
                    act = c.sb["nsbA"][:, (ic - 16) * SW:(ic - 15) * SW]
                    akey = "nsbA"
                else:
                    act = c.sb["nsbB"][:, (ic - 24) * SW:(ic - 23) * SW]
                    akey = "nsbB"
                c.mm(ps[:, :], wt, act,
                     start=(ic == 0), stop=(ic == 31),
                     r=[rslot, akey],
                     w=[f"psA{p}"] if ic == 0 else (), pw=[f"psA{p}"] if ic > 0 else ())
            ev = c.sb[f"gf{p}"]
            if bias is not None:
                c.ts(ev[:, :], ps[:, :], 1.0, OP.mult, c.sb["gpc"][:, oc:oc + 1], OP.add,
                     r=[f"psA{p}", "gpc"], w=[f"gf{p}"])
            else:
                c.copy(ev[:, :], ps[:, :], r=[f"psA{p}"], w=[f"gf{p}"])
            c.dma(dst[oc * 128:(oc + 1) * 128, cols], ev[:, :], r=[f"gf{p}"],
                  w=[sk(dst_name, oc, s)])
            if collect_rows is not None:
                collect_chunk(c, ev[:, :], f"gf{p}", p, oc == 0, oc == ND - 1, cslot,
                              rms=collect_rms, claim=(s == 0 and oc == 0))
        if collect_rows is not None:
            _rows_finish(c, s, collect_rms, cslot)
            _rows_store(c, collect_rows, s, cslot)


def e_l1ffn(c, src_name):
    n, g = c.d(src_name), c.d("g")
    gateT, gate_b = c.d("G1_gateT"), c.d("G1_gate_b")
    AcatT, BcatT = c.d("G1_AcatT"), c.d("G1_BcatT")
    wiT = c.d("G1_wiT")
    small = c.sb["lsm"]
    c.dma(small[:, 0:64].rearrange("p (c j) -> p c j", c=ND),
          gateT.rearrange("(c p) j -> p c j", p=128), eng="gpsimd",
          r=["D:G1_gateT"], pw=["lsm"])
    c.dma(small[:, 64:64 + 256].rearrange("p (c j) -> p c j", c=ND),
          AcatT.rearrange("(c p) j -> p c j", p=128), eng="gpsimd",
          r=["D:G1_AcatT"], pw=["lsm"])
    gb_row = bass_rust.AP(tensor=gate_b.tensor, offset=0, ap=[[E, 1], [1, E]])
    c.dma(c.sb["gbrow"][0:1, 0:8], gb_row, eng="gpsimd", r=["D:G1_gate_b"], w=["gbrow"])
    lg = c.sb["rtf"]
    pc = c.sb["rpc"]
    rmt = c.sb["gsb"]
    orb = c.sb["onesrow_bf"]
    for s in range(NS):
        sp = s % 2
        nsb = c.sb["nsbA" if sp == 0 else "nsbB"]
        nkey = "nsbA" if sp == 0 else "nsbB"
        cols = slice(s * SW, (s + 1) * SW)
        c.dma(nsb[:, :].rearrange("p (c t) -> p c t", c=ND), fm(n, cols),
              r=srd(src_name, s), w=[nkey])
        for tt in range(SW // 128):
            tp = tt % 2
            ps = c.sb[f"psA{tp}"]
            for ic in range(ND):
                c.mm(ps[:, 0:8], nsb[:, ic * SW + tt * 128: ic * SW + (tt + 1) * 128],
                     small[:, ic * 8:(ic + 1) * 8], start=(ic == 0), stop=False,
                     r=[nkey, "lsm"], w=[f"psA{tp}"] if ic == 0 else (),
                     pw=[f"psA{tp}"] if ic > 0 else ())
            c.mm(ps[:, 0:8], orb[0:1, :], c.sb["gbrow"][0:1, 0:8],
                 start=False, stop=True, r=["ones", "gbrow"], pw=[f"psA{tp}"])
            c.copy(lg[:, 0:8], ps[:, 0:8], r=[f"psA{tp}"], w=["rtf"])
            c.rmax(pc[:, 0:1], lg[:, 0:8], r=["rtf"], w=["rpc"])
            c.ts(lg[:, 8:16], lg[:, 0:8], pc[:, 0:1], OP.is_equal,
                 r=["rtf", "rpc"], w=["rtf:m1"])
            m32 = lg.ap()[:, 16:48].rearrange("p (e r2) -> p e r2", r2=R)
            m1v = lg.ap()[:, 8:16].rearrange("p (e r2) -> p e r2", r2=1)
            for r_ in range(R):
                c.copy(m32[:, :, r_:r_ + 1], m1v, r=["rtf:m1"], pw=["rtf:nm"])
            psb = c.sb[f"psS{tp}"]
            for ic in range(ND):
                c.mm(psb[:, 0:32], nsb[:, ic * SW + tt * 128: ic * SW + (tt + 1) * 128],
                     small[:, 64 + ic * 32:64 + (ic + 1) * 32], start=(ic == 0),
                     stop=(ic == ND - 1), r=[nkey, "lsm"],
                     w=[f"psS{tp}"] if ic == 0 else (), pw=[f"psS{tp}"] if ic > 0 else ())
            c.tt(c.sb["rtt"][:, 0:32], psb[:, 0:32], lg[:, 16:48], OP.mult,
                 r=[f"psS{tp}", "rtf:nm"], w=["rtt"])
            c.transpose(c.sb["psPV"][0:32, 0:128], c.sb["rtt"][:, 0:32],
                        c.sb["identf"][:, :], r=["rtt", "identf"], w=["psPV"])
            c.copy(rmt[0:32, s * SW + tt * 128: s * SW + (tt + 1) * 128],
                   c.sb["psPV"][0:32, 0:128], r=["psPV"], pw=["gsb"])
    bc_sb = c.sb["Msb"]
    c.dma(bc_sb[0:32, 0:FF], BcatT[:, :], eng="gpsimd", r=["D:G1_BcatT"], w=["Msb"])
    wrot = ["A0", "B0", "C0", "A1"]
    for it, og in enumerate(range(0, FF, 1024)):
        wslot = wrot[it]
        wi = c.d("G1_wiT")
        c.dma(slot_ap(c, wslot).rearrange("p (c j) -> p c j", c=ND),
              wi[:, og:og + 1024].rearrange("(c p) j -> p c j", p=128), eng="gpsimd",
              r=["D:G1_wiT"], w=[wslot])
        wap = slot_ap(c, wslot)
        load_pcols(c, "gpc", "G1_wi_b", n=8, off=og // 128)
        for s in range(NS):
            sp = s % 2
            nsb = c.sb["nsbA" if sp == 0 else "nsbB"]
            nkey = "nsbA" if sp == 0 else "nsbB"
            cols = slice(s * SW, (s + 1) * SW)
            c.dma(nsb[:, :].rearrange("p (c t) -> p c t", c=ND), fm(n, cols),
                  r=srd(src_name, s), w=[nkey])
            for j in range(8):
                p = j % 2
                ps = c.sb[f"psA{p}"]
                c.mm(ps[:, :], bc_sb[0:32, og + j * 128: og + (j + 1) * 128],
                     rmt[0:32, cols], start=True, stop=False,
                     r=["Msb", "gsb"], w=[f"psA{p}"])
                for ic in range(ND):
                    c.mm(ps[:, :], wap[:, ic * 1024 + j * 128: ic * 1024 + (j + 1) * 128],
                         nsb[:, ic * SW:(ic + 1) * SW], start=False, stop=(ic == ND - 1),
                         r=[wslot, nkey], pw=[f"psA{p}"])
                ev = c.sb[f"gb{p}"]
                c.act(ev[:, :], ps[:, :], AF.Relu, bias=c.sb["gpc"][:, j:j + 1],
                      r=[f"psA{p}", "gpc"], w=[f"gb{p}"])
                jg = og // 128 + j
                c.dma(g[og + j * 128: og + (j + 1) * 128, cols], ev[:, :],
                      r=[f"gb{p}"], w=[f"D:g:{jg}:{s}"])
    e_projK32(c, "G1_woT", "g", "y32", bias="G1_wo_b", collect_rows="rowsA", cslot=1)


def e_bhat(c):
    stripes = c.d("L1_bias_stripes")
    c.memset(c.sb["bhat"][:, :], NEGBIG, w=["bhat"])
    r = c.sb["qrw1"]
    for h in range(H):
        for ch in range(5):
            w = min(SW, 2559 - ch * SW)
            c.dma(r[0:1, 0:w], stripes[h, 127:128, ch * SW:ch * SW + w],
                  eng="scalar", r=["D:L1_bias_stripes"], w=["qrw1"])
            c.rmax(c.sb["qrw2"][0:1, 0:1], r[0:1, 0:w], r=["qrw1"], w=["qrw2"])
            c.tt(c.sb["bhat"][0:1, h:h + 1], c.sb["bhat"][0:1, h:h + 1],
                 c.sb["qrw2"][0:1, 0:1], OP.max, r=["bhat", "qrw2"], w=["bhat"])


def ph_all(c):
    e_prep(c)
    c.alloc("rw1", [1, SW], F32)
    c.alloc("rw2", [1, SW], F32)
    c.alloc("rw3", [1, SW], F32)
    c.alloc("rw4", [1, SW], F32)
    e_stats(c, "xT", "rowsA", slot=0)
    e_apply(c, "xT", "rowsA", "n1", pair="A")
    e_qk(c, "L0_Wk", "n1", "kT", "k", "A0", bias="L0_bk")
    e_qk(c, "L0_Wq", "n1", "qT", "q", "A1", bias="L0_bq")
    e_v(c, "L0_Wv", "n1", "B0", bias="L0_bv")
    e_attn(c, use_bias=False)
    e_proj(c, "L0_Wo", "oT", "y32", "B1", bias="L0_bo", out_dt=F32,
           src_reader=oT_reader, collect_rows="rowsB", cslot=1)
    e_apply(c, "y32", "rowsB", "d32", out_dt=F32, gamma="L0_gamma", beta="L0_beta",
            collect_rows="rowsC", slot=0, pair="B")
    e_apply(c, "d32", "rowsC", "x32a", out_dt=F32, resid_name="xT",
            collect_rows="rowsD", slot=1, pair="A")
    e_apply(c, "x32a", "rowsD", "n2", pair="B")
    e_router(c, "n2")
    e_qk(c, "F0_wk", "n2", "kT", "k", "A0")
    e_qk(c, "F0_wq", "n2", "qT", "q", "A1")
    e_v(c, "F0_wv", "n2", "B0")
    e_attn(c, use_bias=False)
    e_proj(c, "F0_wo", "oT", "y32", "B1", out_dt=F32, src_reader=oT_reader,
           collect_rows="rowsE", collect_rms=True, cslot=0)
    e_apply(c, "y32", "rowsE", "h", out_dt=BF16, rmsw="F0_attn_nw",
            resid_name="n2", resid_dt=BF16, pair="A")
    e_glu(c)
    e_projK32(c, "F0_w2", "g", "y32", collect_rows="rowsF", collect_rms=True, cslot=1)
    e_apply(c, "y32", "rowsF", "d32", out_dt=F32, rmsw="F0_ff_nw",
            resid_name="h", resid_dt=BF16, pair="B")
    e_moe(c, "n2", collect_rows="rowsG", cslot=0)
    e_apply(c, "moe32", "rowsG", "y32", out_dt=F32, rmsw="F0_moe_nw",
            resid_name="d32", resid_dt=F32, collect_rows="rowsH", slot=1, pair="A")
    e_apply(c, "y32", "rowsH", "x32b", out_dt=F32, resid_name="x32a",
            collect_rows="rowsI", slot=0, pair="B")
    e_apply(c, "x32b", "rowsI", "n3", pair="A")
    e_bhat(c)
    e_qk(c, "L1_WKT", "n3", "kT", "k", "A0", bias="L1_bK")
    e_qk(c, "L1_WQT", "n3", "qT", "q", "A1", bias="L1_bQ", use_bhat=True)
    e_v(c, "L1_WVT", "n3", "B0", bias="L1_bV")
    e_attn(c, use_bias=True)
    e_proj(c, "L1_WOT", "oT", "y32", "B1", bias="L1_bO", out_dt=F32,
           src_reader=oT_reader, collect_rows="rowsJ", cslot=1)
    e_apply(c, "y32", "rowsJ", "x32c", out_dt=F32, resid_name="x32b",
            collect_rows="rowsK", slot=0, pair="B")
    e_apply(c, "x32c", "rowsK", "n4", pair="A")
    e_l1ffn(c, "n4")
    e_apply(c, "y32", "rowsA", "yT", out_dt=F32, resid_name="x32c", pair="B")


def _host_layout(inputs):
    g = {}
    g["xT"] = [np.ascontiguousarray(np.asarray(inputs["L0_attn_x"][i]).T) for i in range(NCORES)]
    def A(name):
        return np.ascontiguousarray(np.asarray(inputs[name], dtype=np.float32))
    shared = {
        "L0_Wq": A("L0_attn_Wq"), "L0_Wk": A("L0_attn_Wk"), "L0_Wv": A("L0_attn_Wv"),
        "L0_Wo": A("L0_attn_Wo"),
        "L0_bq": A("L0_attn_bq"), "L0_bk": A("L0_attn_bk"), "L0_bv": A("L0_attn_bv"),
        "L0_bo": A("L0_attn_bo"), "L0_gamma": A("L0_attn_gamma"), "L0_beta": A("L0_attn_beta"),
        "F0_wq": A("L0_ffn_wq"), "F0_wk": A("L0_ffn_wk"), "F0_wv": A("L0_ffn_wv"),
        "F0_wo": A("L0_ffn_wo"), "F0_w1": A("L0_ffn_w1"), "F0_w2": A("L0_ffn_w2"),
        "F0_w3": A("L0_ffn_w3"),
        "F0_attn_nw": A("L0_ffn_attn_norm_w"), "F0_ff_nw": A("L0_ffn_ff_norm_w"),
        "F0_moe_nw": A("L0_ffn_moe_norm_w"), "F0_router": A("L0_ffn_router_w"),
        "F0_ew1": A("L0_ffn_ew1"), "F0_ew2": A("L0_ffn_ew2"), "F0_ew3": A("L0_ffn_ew3"),
        "L1_WQT": np.ascontiguousarray(A("L1_attn_WQ").T),
        "L1_WKT": np.ascontiguousarray(A("L1_attn_WK").T),
        "L1_WVT": np.ascontiguousarray(A("L1_attn_WV").T),
        "L1_WOT": np.ascontiguousarray(A("L1_attn_WO").T),
        "L1_bQ": A("L1_attn_bQ"), "L1_bK": A("L1_attn_bK"), "L1_bV": A("L1_attn_bV"),
        "L1_bO": A("L1_attn_bO"),
        "G1_gateT": np.ascontiguousarray(A("L1_ffn_gate_W").T),
        "G1_gate_b": A("L1_ffn_gate_b"),
        "G1_wiT": np.ascontiguousarray(A("L1_ffn_wi_W").T),
        "G1_wi_b": A("L1_ffn_wi_b"),
        "G1_woT": np.ascontiguousarray(A("L1_ffn_wo_W").T),
        "G1_wo_b": A("L1_ffn_wo_b"),
        "G1_AcatT": np.ascontiguousarray(A("L1_ffn_lora_A").reshape(E * R, D).T),
        "G1_BcatT": np.ascontiguousarray(
            A("L1_ffn_lora_B").transpose(0, 2, 1).reshape(E * R, FF)),
    }
    tbl = A("L1_attn_rel_pos_table")
    yy = np.arange(MW)[None, :]
    pp = np.arange(128)[:, None]
    idx = np.minimum(1663 + yy - pp, 4094)
    stripes = tbl[idx, :]
    masked = (yy - 384 - pp) < 0
    stripes = np.where(masked[:, :, None], np.float32(MNEG), stripes)
    shared["L1_bias_stripes"] = np.ascontiguousarray(stripes.transpose(2, 0, 1))
    return g, shared


def kernel(**inputs):
    percore, shared = _host_layout(inputs)
    nc = bass.Bass("TRN2", target_bir_lowering=False)
    c = Ctx(nc, set(EXTERNAL_IN), set(EXTERNAL_OUT))
    try:
        ph_all(c)
        c.realize()
    finally:
        c.close()
    in_maps = []
    for i in range(NCORES):
        m = {}
        for rn in c.reads:
            a = percore["xT"][i] if rn == "xT" else shared[rn]
            m[rn] = _as_np(a, rn)
        in_maps.append(m)
    res = run_bass_kernel_spmd(nc, in_maps, core_ids=list(range(NCORES)))
    out = np.stack([np.ascontiguousarray(np.asarray(res.results[i]["yT"]).T)
                    for i in range(NCORES)])
    return out.astype(np.float32)


def _as_np(a, name):
    shape, dt = DRAM_SPECS[name]
    npdt = {F32: np.float32, BF16: NPBF, I32: np.int32}[dt]
    a = np.asarray(a)
    if a.dtype != npdt:
        a = a.astype(npdt)
    return np.ascontiguousarray(a.reshape(shape))


# version 10
# speedup vs baseline: 1.1829x; 1.1829x over previous
import numpy as np
import ml_dtypes
import concourse.bass as bass
import concourse.mybir as mybir
import bass_rust
from concourse.bass_utils import run_bass_kernel_spmd

F32 = mybir.dt.float32
BF16 = mybir.dt.bfloat16
I32 = mybir.dt.int32
AF = mybir.ActivationFunctionType
OP = mybir.AluOpType
NPBF = ml_dtypes.bfloat16

T = 4096
D = 1024
H = 16
DH = 64
FF = 4096
E = 8
R = 4
NCORES = 8
ND = D // 128
SW = 512
NS = T // SW
NT = T // 128
EPS = 1e-5
CEXP = 20.0
NEGBIG = -1e30
MNEG = -60000.0
MW = 4480

ENGINES = ["sync", "tensor", "vector", "scalar", "gpsimd"]
DMA_QUEUES = ("sync", "gpsimd", "scalar")

DRAM_SPECS = {
    "xT":    ([D, T], F32),
    "yT":    ([D, T], F32),
    "x32a":  ([D, T], F32),
    "x32b":  ([D, T], F32),
    "x32c":  ([D, T], F32),
    "n1":    ([D, T], BF16),
    "n2":    ([D, T], BF16),
    "n3":    ([D, T], BF16),
    "n4":    ([D, T], BF16),
    "h":     ([D, T], BF16),
    "qT":    ([D, T], BF16),
    "kT":    ([D, T], BF16),
    "vtok":  ([T, H * 65], BF16),
    "oT":    ([D, T], BF16),
    "y32":   ([D, T], F32),
    "d32":   ([D, T], F32),
    "moe32": ([D, T], F32),
    "g":     ([FF, T], BF16),
    "qext":  ([H, T], BF16),
    "wrowT": ([E, T], F32),
    "rowsA": ([2, T], F32), "rowsB": ([2, T], F32), "rowsC": ([2, T], F32),
    "rowsD": ([2, T], F32), "rowsE": ([2, T], F32), "rowsF": ([2, T], F32),
    "rowsG": ([2, T], F32), "rowsH": ([2, T], F32), "rowsI": ([2, T], F32),
    "rowsJ": ([2, T], F32), "rowsK": ([2, T], F32),
    "L0_Wq": ([D, D], F32), "L0_Wk": ([D, D], F32), "L0_Wv": ([D, D], F32), "L0_Wo": ([D, D], F32),
    "L0_bq": ([D], F32), "L0_bk": ([D], F32), "L0_bv": ([D], F32), "L0_bo": ([D], F32),
    "L0_gamma": ([D], F32), "L0_beta": ([D], F32),
    "F0_wq": ([D, D], F32), "F0_wk": ([D, D], F32), "F0_wv": ([D, D], F32), "F0_wo": ([D, D], F32),
    "F0_w1": ([D, FF], F32), "F0_w2": ([FF, D], F32), "F0_w3": ([D, FF], F32),
    "F0_attn_nw": ([D], F32), "F0_ff_nw": ([D], F32), "F0_moe_nw": ([D], F32),
    "F0_router": ([D, E], F32),
    "F0_ew1": ([E, D, D], F32), "F0_ew2": ([E, D, D], F32), "F0_ew3": ([E, D, D], F32),
    "L1_WQT": ([D, D], F32), "L1_WKT": ([D, D], F32), "L1_WVT": ([D, D], F32), "L1_WOT": ([D, D], F32),
    "L1_bQ": ([D], F32), "L1_bK": ([D], F32), "L1_bV": ([D], F32), "L1_bO": ([D], F32),
    "L1_bias_stripes": ([H, 128, MW], F32),
    "G1_gateT": ([D, E], F32), "G1_gate_b": ([E], F32),
    "G1_wiT": ([D, FF], F32), "G1_wi_b": ([FF], F32),
    "G1_woT": ([FF, D], F32), "G1_wo_b": ([D], F32),
    "G1_AcatT": ([D, E * R], F32), "G1_BcatT": ([E * R, FF], F32),
}

EXTERNAL_IN = {
    "xT", "L0_Wq", "L0_Wk", "L0_Wv", "L0_Wo", "L0_bq", "L0_bk", "L0_bv", "L0_bo",
    "L0_gamma", "L0_beta",
    "F0_wq", "F0_wk", "F0_wv", "F0_wo", "F0_w1", "F0_w2", "F0_w3",
    "F0_attn_nw", "F0_ff_nw", "F0_moe_nw", "F0_router", "F0_ew1", "F0_ew2", "F0_ew3",
    "L1_WQT", "L1_WKT", "L1_WVT", "L1_WOT", "L1_bQ", "L1_bK", "L1_bV", "L1_bO",
    "L1_bias_stripes",
    "G1_gateT", "G1_gate_b", "G1_wiT", "G1_wi_b", "G1_woT", "G1_wo_b",
    "G1_AcatT", "G1_BcatT",
}
EXTERNAL_OUT = {"yT"}


def sk(name, ic, s):
    return f"D:{name}:{ic}:{s}"


def srd(name, s, n=ND):
    return [sk(name, ic, s) for ic in range(n)]


class Ctx:

    NSEM_DMA = 8

    def __init__(self, nc, external_in, external_out):
        self.nc = nc
        self.ops = []
        self.sb = {}
        self.dram = {}
        self._cms = []
        self.ext_in = external_in
        self.ext_out = external_out
        self.reads = set()
        self.writes = set()
        self.last_writer = {}
        self.part_writers = {}
        self.readers_ = {}
        self.eng_last = {e: None for e in ENGINES}

    def d(self, name):
        if name not in self.dram:
            shape, dt = DRAM_SPECS[name]
            if name in self.ext_in:
                kind = "ExternalInput"
            elif name in self.ext_out:
                kind = "ExternalOutput"
            else:
                kind = "Internal"
            self.dram[name] = self.nc.dram_tensor(name, shape, dt, kind=kind).ap()
        if name in self.ext_in:
            self.reads.add(name)
        if name in self.ext_out:
            self.writes.add(name)
        return self.dram[name]

    def alloc(self, name, shape, dt, psum=False):
        mk = self.nc.psum_tensor if psum else self.nc.sbuf_tensor
        cm = mk(name, shape, dt)
        t = cm.__enter__()
        self._cms.append(cm)
        self.sb[name] = t
        return t

    def close(self):
        for cm in reversed(self._cms):
            cm.__exit__(None, None, None)

    def op(self, eng, fn, inc=1, r=(), w=(), pw=()):
        oid = len(self.ops)
        deps = set()
        for k in r:
            j = self.last_writer.get(k)
            if j is not None:
                deps.add(j)
            deps.update(self.part_writers.get(k, ()))
        for k in w:
            j = self.last_writer.get(k)
            if j is not None:
                deps.add(j)
            deps.update(self.part_writers.get(k, ()))
            deps.update(self.readers_.get(k, ()))
        for k in pw:
            j = self.last_writer.get(k)
            if j is not None:
                deps.add(j)
            deps.update(self.readers_.get(k, ()))
        for k in r:
            self.readers_.setdefault(k, set()).add(oid)
        for k in w:
            self.last_writer[k] = oid
            self.part_writers[k] = []
            self.readers_[k] = set()
        for k in pw:
            self.part_writers.setdefault(k, []).append(oid)
        deps.discard(oid)
        self.ops.append((eng, inc, fn, sorted(deps)))
        self.eng_last[eng] = oid

    def dma(self, out, in_, eng="sync", r=(), w=(), pw=(), **kw):
        self.op(eng, lambda e, o=out, i=in_, k=kw: e.dma_start(out=o, in_=i, **k),
                inc=16, r=r, w=w, pw=pw)

    def mm(self, out, lhsT, rhs, start=True, stop=True, r=(), w=(), pw=(), tp=None):
        self.op("tensor", lambda e, o=out, l=lhsT, rr=rhs, s1=start, s2=stop, t=tp:
                e.matmul(o, l, rr, start=s1, stop=s2, tile_position=t), r=r, w=w, pw=pw)

    def transpose(self, out, in_, ident, r=(), w=(), pw=()):
        self.op("tensor", lambda e, o=out, i=in_, d=ident: e.transpose(o, i, d), r=r, w=w, pw=pw)

    def act(self, out, in_, func, bias=None, scale=None, r=(), w=(), pw=()):
        kw = {}
        if bias is not None:
            kw["bias"] = bias
        if scale is not None:
            kw["scale"] = scale
        self.op("scalar", lambda e, o=out, i=in_, f=func, k=kw:
                e.activation(out=o, in_=i, func=f, **k), r=r, w=w, pw=pw)

    def tt(self, out, a, b, o, r=(), w=(), pw=()):
        self.op("vector", lambda e, oo=out, aa=a, bb=b, op_=o:
                e.tensor_tensor(out=oo, in0=aa, in1=bb, op=op_), r=r, w=w, pw=pw)

    def ts(self, out, in0, s1, op0, s2=None, op1=None, r=(), w=(), pw=()):
        kw = dict(scalar2=s2)
        if op1 is not None:
            kw["op1"] = op1
        self.op("vector", lambda e, o=out, i=in0, a=s1, k=kw, p0=op0:
                e.tensor_scalar(out=o, in0=i, scalar1=a, op0=p0, **k), r=r, w=w, pw=pw)

    def stt(self, out, in0, scalar, in1, op0, op1, r=(), w=(), pw=()):
        self.op("vector", lambda e, o=out, i0=in0, s=scalar, i1=in1, p0=op0, p1=op1:
                e.scalar_tensor_tensor(out=o, in0=i0, scalar=s, in1=i1, op0=p0, op1=p1),
                r=r, w=w, pw=pw)

    def copy(self, out, in_, r=(), w=(), pw=()):
        self.op("vector", lambda e, o=out, i=in_: e.tensor_copy(out=o, in_=i), r=r, w=w, pw=pw)

    def memset(self, ap, v, r=(), w=(), pw=()):
        self.op("vector", lambda e, a=ap, vv=v: e.memset(a, vv), r=r, w=w, pw=pw)

    def rmax(self, out, in_, negate=False, r=(), w=(), pw=()):
        self.op("vector", lambda e, o=out, i=in_, n=negate:
                e.reduce_max(o, i, axis=mybir.AxisListType.X, negate=n), r=r, w=w, pw=pw)

    def rsum(self, out, in_, r=(), w=(), pw=()):
        self.op("vector", lambda e, o=out, i=in_:
                e.reduce_sum(o, i, axis=mybir.AxisListType.X), r=r, w=w, pw=pw)

    def recip(self, out, in_, r=(), w=(), pw=()):
        self.op("vector", lambda e, o=out, i=in_: e.reciprocal(out=o, in_=i), r=r, w=w, pw=pw)

    def iota(self, out, pattern, base, chmul, r=(), w=(), pw=()):
        self.op("gpsimd", lambda e, o=out, p=pattern, b=base, cm=chmul:
                e.iota(o, pattern=p, base=b, channel_multiplier=cm,
                       allow_small_or_imprecise_dtypes=True), r=r, w=w, pw=pw)

    def realize(self):
        nc = self.nc
        n_ops = len(self.ops)
        opinfo = [None] * n_ops
        sem_names = ["c_" + e for e in ENGINES]
        for q in DMA_QUEUES:
            for i in range(self.NSEM_DMA):
                sem_names.append(f"d_{q}_{i}")
        cum = {sn: 0 for sn in sem_names}
        rr = {q: 0 for q in DMA_QUEUES}
        presem = [None] * n_ops
        for oid, (eng, inc, fn, deps) in enumerate(self.ops):
            if inc == 16 and eng in DMA_QUEUES:
                i = rr[eng]
                rr[eng] = (i + 1) % self.NSEM_DMA
                sn = f"d_{eng}_{i}"
                presem[oid] = (sn, cum[sn])
            else:
                sn = "c_" + eng
            cum[sn] += inc
            opinfo[oid] = (sn, cum[sn])
        totals = dict(cum)
        sem_cms = {sn: nc.semaphore("S" + sn) for sn in sem_names}
        sems = {sn: sem_cms[sn].__enter__() for sn in sem_names}
        blk_cm = nc.Block()
        block = blk_cm.__enter__()
        try:
            def mksec(name):
                def body(eng):
                    hw = {sn: 0 for sn in sem_names}
                    for oid, (oeng, inc, fn, deps) in enumerate(self.ops):
                        if oeng != name:
                            continue
                        need = {}
                        for j in deps:
                            js, jcum = opinfo[j]
                            if js == "c_tensor" and name == "tensor":
                                continue
                            if jcum > need.get(js, 0):
                                need[js] = jcum
                        if presem[oid] is not None:
                            sn, cb = presem[oid]
                            if cb > need.get(sn, 0):
                                need[sn] = cb
                        for js, v in sorted(need.items()):
                            if v > hw[js]:
                                eng.wait_ge(sems[js], v)
                                hw[js] = v
                        inst = fn(eng)
                        my_sn = opinfo[oid][0]
                        inst.then_inc(sems[my_sn], inc)
                    if name in ("sync", "gpsimd"):
                        for sn in sem_names:
                            if totals[sn] > 0:
                                eng.wait_ge(sems[sn], totals[sn])
                return body
            for name in ENGINES:
                getattr(block, name)(mksec(name))
        finally:
            blk_cm.__exit__(None, None, None)
            for sn in sem_names:
                sem_cms[sn].__exit__(None, None, None)


def fm(ap_dram, cols, p=128):
    return ap_dram[:, cols].rearrange("(c p) t -> p c t", p=p)


SLOTS = {
    "A0": ("wsb", 0), "A1": ("wsb", 8192),
    "B0": ("wsb2", 0), "B1": ("wsb2", 8192),
    "C0": ("ssb32", 0),
}


def e_prep(c):
    if "wsb" in c.sb:
        return
    c.alloc("wsb", [128, 16384], BF16)
    c.alloc("wsb2", [128, 16384], BF16)
    c.alloc("ssb32", [128, 8192], BF16)
    c.alloc("nsbA", [128, 4096], BF16)
    c.alloc("nsbB", [128, 4096], BF16)
    c.alloc("gsb", [128, 4096], BF16)
    c.alloc("ksbhA", [128, 4096], BF16)
    c.alloc("ksbhB", [128, 4096], BF16)
    c.alloc("vsbhA", [128, NT * 65], BF16)
    c.alloc("vsbhB", [128, NT * 65], BF16)
    c.alloc("Msb", [128, MW], BF16)
    c.alloc("qsbA", [128, SW], BF16)
    c.alloc("qsbB", [128, SW], BF16)
    c.alloc("PtA", [128, SW], BF16)
    c.alloc("PtB", [128, SW], BF16)
    c.alloc("vsb2A", [128, 1040], BF16)
    c.alloc("vsb2B", [128, 1040], BF16)
    c.alloc("xsbA0", [128, SW], F32)
    c.alloc("xsbA1", [128, SW], F32)
    c.alloc("xsbB0", [128, SW], F32)
    c.alloc("xsbB1", [128, SW], F32)
    c.alloc("tf0", [128, SW], F32)
    c.alloc("tf1", [128, SW], F32)
    c.alloc("evB0", [128, SW], BF16)
    c.alloc("evB1", [128, SW], BF16)
    c.alloc("evC0", [128, SW], BF16)
    c.alloc("evC1", [128, SW], BF16)
    c.alloc("xbf0", [128, SW], BF16)
    c.alloc("xbf1", [128, SW], BF16)
    c.alloc("x2bf0", [128, SW], BF16)
    c.alloc("x2bf1", [128, SW], BF16)
    c.alloc("bcmA", [128, SW], F32)
    c.alloc("bcrA", [128, SW], F32)
    c.alloc("bcmB", [128, SW], F32)
    c.alloc("bcrB", [128, SW], F32)
    c.alloc("pcol", [128, 16], F32)
    c.alloc("pcol2", [128, 16], F32)
    c.alloc("gb0", [128, SW], BF16)
    c.alloc("gb1", [128, SW], BF16)
    c.alloc("gc0", [128, SW], BF16)
    c.alloc("gc1", [128, SW], BF16)
    c.alloc("gf0", [128, SW], F32)
    c.alloc("gf1", [128, SW], F32)
    c.alloc("gpc", [128, 16], F32)
    c.alloc("qrw1", [1, SW], F32)
    c.alloc("qrw2", [1, SW], F32)
    c.alloc("rlA1", [1, SW], F32)
    c.alloc("rlA2", [1, SW], F32)
    c.alloc("rbc", [128, SW], F32)
    c.alloc("rtf", [128, 64], F32)
    c.alloc("rtt", [128, 128], F32)
    c.alloc("rpc", [128, 8], F32)
    c.alloc("lsm", [128, 320], BF16)
    c.alloc("bvrow", [1, 1024], BF16)
    c.alloc("gbrow", [1, 8], BF16)
    c.alloc("ones_bf", [128, 1], BF16)
    c.alloc("ones_f", [128, 1], F32)
    c.alloc("onesrow_bf", [1, 128], BF16)
    c.alloc("onesrow_f", [1, 128], F32)
    c.alloc("epscol", [1, 1], F32)
    c.alloc("khat2", [1, H], F32)
    c.alloc("bhat", [1, H], F32)
    c.alloc("identf", [128, 128], F32)
    c.alloc("psA0", [128, SW], F32, psum=True)
    c.alloc("psA1", [128, SW], F32, psum=True)
    c.alloc("psS0", [128, SW], F32, psum=True)
    c.alloc("psS1", [128, SW], F32, psum=True)
    c.alloc("psPV", [128, SW], F32, psum=True)
    c.alloc("psB0", [128, SW], F32, psum=True)
    c.alloc("psB1", [128, SW], F32, psum=True)
    c.alloc("psR", [128, SW], F32, psum=True)

    c.memset(c.sb["ones_bf"][:, :], 1.0, w=["ones"])
    c.memset(c.sb["ones_f"][:, :], 1.0, pw=["ones"])
    c.memset(c.sb["onesrow_bf"][:, :], 1.0, pw=["ones"])
    c.memset(c.sb["onesrow_f"][:, :], 1.0, pw=["ones"])
    c.memset(c.sb["epscol"][:, :], EPS, pw=["ones"])
    c.memset(c.sb["ksbhA"][64:128, :], 0.0, w=["konesA"])
    c.memset(c.sb["ksbhB"][64:128, :], 0.0, w=["konesB"])
    c.memset(c.sb["ksbhA"][64:65, :], 1.0, pw=["konesA"])
    c.memset(c.sb["ksbhB"][64:65, :], 1.0, pw=["konesB"])
    c.memset(c.sb["qsbA"][64:128, :], 0.0, w=["qzA"])
    c.memset(c.sb["qsbB"][64:128, :], 0.0, w=["qzB"])
    ii = c.sb["tf1"]
    c.iota(ii[:, 0:128], [[1, 128]], 0, -1, w=["tf1"])
    c.ts(c.sb["identf"][:, :], ii[:, 0:128], 0.0, OP.is_equal, r=["tf1"], w=["identf"])
    for j in range(4):
        c.iota(ii[:, :], [[1, SW]], -128 * j, -1, w=["tf1"])
        mk = c.sb["Msb"][:, j * SW:(j + 1) * SW]
        tf = c.sb["tf0"]
        c.ts(tf[:, :], ii[:, :], 0.0, OP.is_ge, r=["tf1"], w=["tf0"])
        mkeys = [f"Msb:{i}" for i in range(j * SW // 1120, ((j + 1) * SW - 1) // 1120 + 1)]
        c.ts(mk, tf[:, :], 1.0, OP.subtract, -NEGBIG, OP.mult, r=["tf0"], pw=mkeys)


FIN_ROWS = {0: ("rw1", "rw2"), 1: ("rw3", "rw4")}


def _rows_finish(c, s, rms, slot):
    base = 0 if slot == 0 else 64
    skey = f"psR{slot}"
    psR = c.sb["psR"]
    mk, rk = FIN_ROWS[slot]
    rm, rr = c.sb[mk], c.sb[rk]
    if rms:
        c.act(rr[:, :], psR[base + 32:base + 33, :], AF.Ln, bias=c.sb["epscol"][0:1, :],
              scale=1.0 / D, r=[skey, "ones"], w=[rk])
        c.act(rr[:, :], rr[:, :], AF.Exp, scale=-0.5, r=[rk], w=[rk])
        c.memset(rm[:, :], 0.0, w=[mk])
    else:
        c.ts(rm[:, :], psR[base:base + 1, :], -1.0 / D, OP.mult, r=[skey], w=[mk])
        c.tt(rr[:, :], rm[:, :], rm[:, :], OP.mult, r=[mk], w=[rk])
        c.ts(psR[base + 32:base + 33, :], psR[base + 32:base + 33, :], 1.0 / D,
             OP.mult, pw=[skey])
        c.tt(psR[base + 32:base + 33, :], psR[base + 32:base + 33, :],
             rr[:, :], OP.subtract, r=[rk], pw=[skey])
        c.act(rr[:, :], psR[base + 32:base + 33, :], AF.Ln,
              bias=c.sb["epscol"][0:1, :], r=[skey, "ones"], w=[rk])
        c.act(rr[:, :], rr[:, :], AF.Exp, scale=-0.5, r=[rk], w=[rk])


def _rows_store(c, rows_name, s, slot):
    rows = c.d(rows_name)
    cols = slice(s * SW, (s + 1) * SW)
    mk, rk = FIN_ROWS[slot]
    c.dma(rows[0:1, cols], c.sb[mk][:, :], eng="scalar", r=[mk],
          pw=[f"D:{rows_name}:{s}"])
    c.dma(rows[1:2, cols], c.sb[rk][:, :], eng="scalar", r=[rk],
          pw=[f"D:{rows_name}:{s}"])


def collect_chunk(c, src_ap, src_key, p, first, last, slot, rms=False, claim=False):
    base = 0 if slot == 0 else 64
    skey = f"psR{slot}"
    ob = c.sb["ones_bf"]
    x2 = c.sb[f"x2bf{p}"]
    if src_ap.dtype != BF16:
        xb = c.sb[f"xbf{p}"]
        c.copy(xb[:, :], src_ap, r=[src_key], w=[f"xbf{p}"])
        src_ap, src_key = xb[:, :], f"xbf{p}"
    wk = dict(w=[skey]) if claim else dict(pw=[skey])
    if not rms:
        c.mm(c.sb["psR"][base:base + 1, :], ob[:, :], src_ap,
             start=first, stop=last, r=[src_key, "ones"], **wk)
        wk = dict(pw=[skey])
    c.act(x2[:, :], src_ap, AF.Square, r=[src_key], w=[f"x2bf{p}"])
    c.mm(c.sb["psR"][base + 32:base + 33, :], ob[:, :], x2[:, :],
         start=first, stop=last, r=[f"x2bf{p}", "ones"],
         tp=(0, 96) if base == 64 else None, **wk)


def e_stats(c, src_name, rows_name, rms=False, slot=0):
    src = c.d(src_name)
    for s in range(NS):
        cols = slice(s * SW, (s + 1) * SW)
        for ic in range(ND):
            p = ic % 2
            xb = c.sb[f"xbf{p}"]
            c.dma(xb[:, :], src[ic * 128:(ic + 1) * 128, cols], eng="gpsimd",
                  r=[sk(src_name, ic, s)], w=[f"xbf{p}"])
            collect_chunk(c, xb[:, :], f"xbf{p}", p, ic == 0, ic == ND - 1, slot, rms,
                          claim=(s == 0 and ic == 0))
        _rows_finish(c, s, rms, slot)
        _rows_store(c, rows_name, s, slot)


BC_ROWS = {"A": ("rlA1", "rlA2"), "B": ("qrw1", "qrw2")}


def bcast_rows(c, rows_name, s, pair):
    rows = c.d(rows_name)
    cols = slice(s * SW, (s + 1) * SW)
    k1, k2 = BC_ROWS[pair]
    r1, r2 = c.sb[k1], c.sb[k2]
    c.dma(r1[:, :], rows[0:1, cols], eng="scalar", r=[f"D:{rows_name}:{s}"], w=[k1])
    c.dma(r2[:, :], rows[1:2, cols], eng="scalar", r=[f"D:{rows_name}:{s}"], w=[k2])
    orf = c.sb["onesrow_f"]
    bm, br = c.sb["bcm" + pair], c.sb["bcr" + pair]
    pb = "psB0" if pair == "A" else "psB1"
    c.mm(c.sb[pb][:, :], orf[0:1, :], r1[:, :], r=[k1, "ones"], w=[pb])
    c.copy(bm[:, :], c.sb[pb][:, :], r=[pb], w=["bcm" + pair])
    c.mm(c.sb[pb][:, :], orf[0:1, :], r2[:, :], r=[k2, "ones"], w=[pb])
    c.copy(br[:, :], c.sb[pb][:, :], r=[pb], w=["bcr" + pair])


def load_pcols(c, bufname, dram_name, scale=None, n=ND, off=0):
    p = c.d(dram_name)
    buf = c.sb[bufname]
    for ic in range(n):
        col = buf[:, ic:ic + 1]
        src = p.rearrange("(c p) -> c p", p=128)[off + ic:off + ic + 1, :].transpose([1, 0])
        c.dma(col, src, eng="scalar", r=["D:" + dram_name], pw=[bufname])
    if scale is not None:
        c.ts(buf[:, 0:n], buf[:, 0:n], scale, OP.mult, r=[bufname], w=[bufname])


def e_apply(c, src_name, rows_name, out_name, out_dt=BF16, gamma=None, beta=None,
            rmsw=None, resid_name=None, resid_dt=F32, collect_rows=None, slot=0,
            pair="A", pbuf="pcol"):
    src, out = c.d(src_name), c.d(out_name)
    resid = c.d(resid_name) if resid_name else None
    if gamma is not None:
        load_pcols(c, pbuf, gamma)
        load_pcols(c, "pcol2", beta)
    if rmsw is not None:
        load_pcols(c, pbuf, rmsw)
    bm, br = c.sb["bcm" + pair], c.sb["bcr" + pair]
    bmk, brk = "bcm" + pair, "bcr" + pair
    for s in range(NS):
        cols = slice(s * SW, (s + 1) * SW)
        bcast_rows(c, rows_name, s, pair)
        for ic in range(ND):
            p = ic % 2
            rr = slice(ic * 128, (ic + 1) * 128)
            xA, tA = c.sb[f"xsbA{p}"], c.sb[f"tf{p}"]
            kx, kt = f"xsbA{p}", f"tf{p}"
            c.dma(xA[:, :], src[rr, cols], r=[sk(src_name, ic, s)], w=[kx])
            c.tt(tA[:, :], xA[:, :], bm[:, :], OP.add, r=[kx, bmk], w=[kt])
            if gamma is not None:
                c.stt(tA[:, :], tA[:, :], c.sb[pbuf][:, ic:ic + 1], br[:, :],
                      OP.mult, OP.mult, r=[kt, pbuf, brk], w=[kt])
                c.ts(tA[:, :], tA[:, :], c.sb["pcol2"][:, ic:ic + 1], OP.add,
                     r=[kt, "pcol2"], w=[kt])
            elif rmsw is not None:
                c.stt(tA[:, :], tA[:, :], c.sb[pbuf][:, ic:ic + 1], br[:, :],
                      OP.mult, OP.mult, r=[kt, pbuf, brk], w=[kt])
            else:
                c.tt(tA[:, :], tA[:, :], br[:, :], OP.mult, r=[kt, brk], w=[kt])
            if resid is not None:
                if resid_dt == F32:
                    rb, rbk = c.sb[f"xsbB{p}"], f"xsbB{p}"
                else:
                    rb, rbk = c.sb[f"evC{p}"], f"evC{p}"
                c.dma(rb[:, :], resid[rr, cols], eng="gpsimd",
                      r=[sk(resid_name, ic, s)], w=[rbk])
                c.tt(tA[:, :], tA[:, :], rb[:, :], OP.add, r=[kt, rbk], w=[kt])
            if out_dt == F32:
                c.dma(out[rr, cols], tA[:, :], eng="scalar", r=[kt],
                      w=[sk(out_name, ic, s)])
            else:
                ev = c.sb[f"evB{p}"]
                c.copy(ev[:, :], tA[:, :], r=[kt], w=[f"evB{p}"])
                c.dma(out[rr, cols], ev[:, :], eng="scalar", r=[f"evB{p}"],
                      w=[sk(out_name, ic, s)])
            if collect_rows is not None:
                collect_chunk(c, tA[:, :], kt, p, ic == 0, ic == ND - 1, slot,
                              claim=(s == 0 and ic == 0))
        if collect_rows is not None:
            _rows_finish(c, s, False, slot)
            _rows_store(c, collect_rows, s, slot)


def slot_ap(c, slot):
    buf, off = SLOTS[slot]
    return c.sb[buf].ap()[:, off:off + 8192]


def load_w_slot(c, w_name, slot, w_slice=None):
    w = c.d(w_name)
    if w_slice is not None:
        w = w_slice
    c.dma(slot_ap(c, slot).rearrange("p (c j) -> p c j", c=ND),
          w.rearrange("(c p) j -> p c j", p=128), eng="gpsimd",
          r=["D:" + w_name], w=[slot])


def e_proj(c, w_name, src_name, dst_name, slot, bias=None, scale=None, out_dt=BF16,
           src_reader=None, collect_rows=None, collect_rms=False, cslot=0):
    src, dst = c.d(src_name), c.d(dst_name)
    wap = slot_ap(c, slot)
    load_w_slot(c, w_name, slot)
    if bias is not None:
        load_pcols(c, "gpc", bias, scale=scale)
    for s in range(NS):
        sp = s % 2
        nsb = c.sb["nsbA" if sp == 0 else "nsbB"]
        nkey = "nsbA" if sp == 0 else "nsbB"
        cols = slice(s * SW, (s + 1) * SW)
        rkeys = src_reader(s) if src_reader else srd(src_name, s)
        c.dma(nsb[:, :].rearrange("p (c t) -> p c t", c=ND),
              fm(src, cols), r=rkeys, w=[nkey])
        for oc in range(ND):
            p = oc % 2
            ps = c.sb[f"psA{p}"]
            for ic in range(ND):
                c.mm(ps[:, :], wap[:, ic * 1024 + oc * 128: ic * 1024 + (oc + 1) * 128],
                     nsb[:, ic * SW:(ic + 1) * SW], start=(ic == 0), stop=(ic == ND - 1),
                     r=[slot, nkey], w=[f"psA{p}"] if ic == 0 else (),
                     pw=[f"psA{p}"] if ic > 0 else ())
            if out_dt == BF16:
                ev, ekey = c.sb[f"gb{p}"], f"gb{p}"
            else:
                ev, ekey = c.sb[f"gf{p}"], f"gf{p}"
            if bias is not None:
                c.ts(ev[:, :], ps[:, :], scale if scale is not None else 1.0, OP.mult,
                     c.sb["gpc"][:, oc:oc + 1], OP.add, r=[f"psA{p}", "gpc"], w=[ekey])
            elif scale is not None:
                c.ts(ev[:, :], ps[:, :], scale, OP.mult, r=[f"psA{p}"], w=[ekey])
            else:
                c.copy(ev[:, :], ps[:, :], r=[f"psA{p}"], w=[ekey])
            c.dma(dst[oc * 128:(oc + 1) * 128, cols], ev[:, :],
                  r=[ekey], w=[sk(dst_name, oc, s)])
            if collect_rows is not None:
                collect_chunk(c, ev[:, :], ekey, p, oc == 0, oc == ND - 1, cslot,
                              rms=collect_rms, claim=(s == 0 and oc == 0))
        if collect_rows is not None:
            _rows_finish(c, s, collect_rms, cslot)
            _rows_store(c, collect_rows, s, cslot)


def e_qk(c, w_name, src_name, dst_name, role, slot, bias=None, use_bhat=False):
    src, dst = c.d(src_name), c.d(dst_name)
    wap = slot_ap(c, slot)
    scale = 0.125 if role == "q" else None
    qext = c.d("qext") if role == "q" else None
    load_w_slot(c, w_name, slot)
    if bias is not None:
        load_pcols(c, "gpc", bias, scale=scale)
    if role == "k":
        c.memset(c.sb["khat2"][:, :], 0.0, w=["khat2"])
    ob = c.sb["ones_bf"]
    for s in range(NS):
        sp = s % 2
        nsb = c.sb["nsbA" if sp == 0 else "nsbB"]
        nkey = "nsbA" if sp == 0 else "nsbB"
        cols = slice(s * SW, (s + 1) * SW)
        c.dma(nsb[:, :].rearrange("p (c t) -> p c t", c=ND), fm(src, cols),
              r=srd(src_name, s), w=[nkey])
        for oc in range(ND):
            p = oc % 2
            ps = c.sb[f"psA{p}"]
            for ic in range(ND):
                c.mm(ps[:, :], wap[:, ic * 1024 + oc * 128: ic * 1024 + (oc + 1) * 128],
                     nsb[:, ic * SW:(ic + 1) * SW], start=(ic == 0), stop=(ic == ND - 1),
                     r=[slot, nkey], w=[f"psA{p}"] if ic == 0 else (),
                     pw=[f"psA{p}"] if ic > 0 else ())
            kb = c.sb[f"gb{p}"]
            if bias is not None:
                c.ts(kb[:, :], ps[:, :], scale if scale else 1.0, OP.mult,
                     c.sb["gpc"][:, oc:oc + 1], OP.add, r=[f"psA{p}", "gpc"], w=[f"gb{p}"])
            elif scale is not None:
                c.ts(kb[:, :], ps[:, :], scale, OP.mult, r=[f"psA{p}"], w=[f"gb{p}"])
            else:
                c.copy(kb[:, :], ps[:, :], r=[f"psA{p}"], w=[f"gb{p}"])
            c.dma(dst[oc * 128:(oc + 1) * 128, cols], kb[:, :], r=[f"gb{p}"],
                  w=[sk(dst_name, oc, s)])
            k2 = c.sb["PtA" if p == 0 else "PtB"]
            k2key = "PtA" if p == 0 else "PtB"
            c.act(k2[:, :], kb[:, :], AF.Square, r=[f"gb{p}"], w=[k2key])
            for g in range(2):
                hh = 2 * oc + g
                psr = c.sb["psS0" if g == 0 else "psS1"]
                pk = "psS0" if g == 0 else "psS1"
                c.mm(psr[0:1, :], ob[64 * g:64 * g + 64, :], k2[64 * g:64 * g + 64, :],
                     r=[k2key, "ones"], w=[pk])
                if role == "k":
                    c.rmax(c.sb["qrw2"][0:1, 0:1], psr[0:1, :], r=[pk], w=["qrw2"])
                    c.tt(c.sb["khat2"][0:1, hh:hh + 1], c.sb["khat2"][0:1, hh:hh + 1],
                         c.sb["qrw2"][0:1, 0:1], OP.max, r=["khat2", "qrw2"], w=["khat2"])
                else:
                    r1 = c.sb["qrw1"]
                    c.ts(r1[:, :], psr[0:1, :], c.sb["khat2"][0:1, hh:hh + 1], OP.mult,
                         r=[pk, "khat2"], w=["qrw1"])
                    c.act(r1[:, :], r1[:, :], AF.Sqrt, r=["qrw1"], w=["qrw1"])
                    if use_bhat:
                        c.ts(r1[:, :], r1[:, :], c.sb["bhat"][0:1, hh:hh + 1], OP.add,
                             r=["qrw1", "bhat"], w=["qrw1"])
                    c.ts(r1[:, :], r1[:, :], -1.0, OP.mult, CEXP, OP.add,
                         r=["qrw1"], w=["qrw1"])
                    c.dma(qext[hh:hh + 1, cols], r1[:, :], eng="gpsimd", r=["qrw1"],
                          w=[f"D:qext:{hh}:{s}"])


def e_v(c, w_name, src_name, slot, bias=None):
    src, vtok = c.d(src_name), c.d("vtok")
    wap = slot_ap(c, slot)
    load_w_slot(c, w_name, slot)
    if bias is not None:
        b = c.d(bias)
        row = bass_rust.AP(tensor=b.tensor, offset=0, ap=[[D, 1], [1, D]])
        c.dma(c.sb["bvrow"][0:1, :], row, eng="gpsimd", r=["D:" + bias], w=["bvrow"])
    orb = c.sb["onesrow_bf"]
    for nm in ("vsb2A", "vsb2B"):
        vv = c.sb[nm].ap().rearrange("p (h m) -> p h m", m=65)
        c.memset(vv[:, :, 64:65], 1.0, w=[nm + ":ones"])
    for s in range(NS):
        sp = s % 2
        nsb = c.sb["nsbA" if sp == 0 else "nsbB"]
        nkey = "nsbA" if sp == 0 else "nsbB"
        cols = slice(s * SW, (s + 1) * SW)
        c.dma(nsb[:, :].rearrange("p (c t) -> p c t", c=ND), fm(src, cols),
              r=srd(src_name, s), w=[nkey])
        for tt in range(SW // 128):
            vp = tt % 2
            vsb2 = c.sb["vsb2A" if vp == 0 else "vsb2B"]
            vkey = "vsb2A" if vp == 0 else "vsb2B"
            vv = vsb2.ap().rearrange("p (h m) -> p h m", m=65)
            for jh in range(2):
                ps = c.sb[f"psA{jh}"]
                for ic in range(ND):
                    c.mm(ps[:, :], nsb[:, ic * SW + tt * 128: ic * SW + (tt + 1) * 128],
                         wap[:, ic * 1024 + jh * 512: ic * 1024 + (jh + 1) * 512],
                         start=(ic == 0), stop=(ic == ND - 1) and bias is None,
                         r=[nkey, slot], w=[f"psA{jh}"] if ic == 0 else (),
                         pw=[f"psA{jh}"] if ic > 0 else ())
                if bias is not None:
                    c.mm(ps[:, :], orb[0:1, :],
                         c.sb["bvrow"][0:1, jh * 512:(jh + 1) * 512],
                         start=False, stop=True,
                         r=["ones", "bvrow"], pw=[f"psA{jh}"])
                dstv = vv[:, jh * 8:(jh + 1) * 8, 0:64]
                srcv = ps.ap().rearrange("p (hh m) -> p hh m", m=64)
                c.copy(dstv, srcv, r=[f"psA{jh}"], pw=[vkey])
            t0 = s * SW + tt * 128
            c.dma(vtok[t0:t0 + 128, :], vsb2[:, :], r=[vkey, vkey + ":ones"],
                  w=[f"D:vtok:{s * 4 + tt}"])


def e_attn(c, use_bias=False):
    qT, kT, vtok, qext, oT = c.d("qT"), c.d("kT"), c.d("vtok"), c.d("qext"), c.d("oT")
    masks = c.sb["Msb"]
    MCH = 1120

    def msb_keys(lo, hi):
        return [f"Msb:{i}" for i in range(lo // MCH, (hi - 1) // MCH + 1)]
    if use_bias:
        stripes = c.d("L1_bias_stripes")
        Msb = c.sb["Msb"]

    def q_load(h, s):
        cols = slice(s * SW, (s + 1) * SW)
        sp = s % 2
        qsb = c.sb["qsbA" if sp == 0 else "qsbB"]
        qk = "qsbA" if sp == 0 else "qsbB"
        qz = "qzA" if sp == 0 else "qzB"
        c.dma(qsb[0:64, :], qT[64 * h:64 * h + 64, cols],
              r=[sk("qT", h // 2, s)], pw=[qk + ":q"])
        c.dma(qsb[64:65, :], qext[h:h + 1, cols],
              r=[f"D:qext:{h}:{s}", qz], pw=[qk + ":x"])

    tiles = [(h, s) for h in range(H) for s in range(NS)]
    pend = None
    SB = ["psS0", "psS1", "psA0", "psA1"]
    PT = ["PtA", "PtB", "gc0", "gc1"]

    def flush_epilogue():
        nonlocal pend
        if pend is None:
            return
        eh, es, ek = pend
        cols = slice(es * SW, (es + 1) * SW)
        c.mm(c.sb["psB0"][0:64, :], c.sb["onesrow_f"][0:1, 0:64], c.sb["qrw1"][:, :],
             r=["qrw1", "ones"], w=["psB0"])
        c.copy(c.sb["rbc"][0:64, :], c.sb["psB0"][0:64, :], r=["psB0"], w=["rbc"])
        ev = c.sb[ek]
        c.tt(ev[0:64, :], c.sb["psPV"][0:64, :], c.sb["rbc"][0:64, :], OP.mult,
             r=["psPV", "rbc"], w=[ek])
        c.dma(oT[64 * eh:64 * eh + 64, cols], ev[0:64, :], eng="scalar", r=[ek],
              w=[f"D:oT:{eh}:{es}"])
        pend = None

    for idx, (h, s) in enumerate(tiles):
        hp = h % 2
        ksbh = c.sb["ksbhA" if hp == 0 else "ksbhB"]
        vsbh = c.sb["vsbhA" if hp == 0 else "vsbhB"]
        kk = "ksbhA" if hp == 0 else "ksbhB"
        vk = "vsbhA" if hp == 0 else "vsbhB"
        konk = "konesA" if hp == 0 else "konesB"
        if s == 0:
            c.dma(ksbh[0:64, :], kT[64 * h:64 * h + 64, :], eng="gpsimd",
                  r=[sk("kT", h // 2, s2) for s2 in range(NS)], w=[kk])
            c.dma(vsbh[:, :].rearrange("p (k m) -> p k m", m=65),
                  vtok.rearrange("(k p) m -> p k m", p=128)[:, :, 65 * h:65 * h + 65],
                  eng="gpsimd", r=[f"D:vtok:{t}" for t in range(NT)], w=[vk])
            if use_bias:
                for i in range(4):
                    c.dma(Msb[:, i * MCH:(i + 1) * MCH],
                          stripes[h, :, i * MCH:(i + 1) * MCH], eng="gpsimd",
                          r=["D:L1_bias_stripes"], w=[f"Msb:{i}"])
        if idx == 0:
            q_load(h, s)
        if idx + 1 < len(tiles):
            q_load(*tiles[idx + 1])
        q0 = s * SW
        cols = slice(q0, q0 + SW)
        sp = s % 2
        qsb = c.sb["qsbA" if sp == 0 else "qsbB"]
        qk = "qsbA" if sp == 0 else "qsbB"
        qz = "qzA" if sp == 0 else "qzB"
        nk = 4 * (s + 1)

        def emit_S(k):
            bk = SB[k % 4]
            c.mm(c.sb[bk][:, :], ksbh[0:128, 128 * k:128 * (k + 1)], qsb[0:128, :],
                 start=True, stop=True,
                 r=[kk, konk, qk + ":q", qk + ":x", qz], w=[bk])
        for k in range(min(4, nk)):
            emit_S(k)
        flush_epilogue()
        for k in range(nk):
            bk = SB[k % 4]
            ps = c.sb[bk]
            if use_bias:
                y0 = q0 - 128 * k + 384
                c.tt(ps[:, :], ps[:, :], Msb[:, y0:y0 + SW], OP.add,
                     r=[bk] + msb_keys(y0, y0 + SW), w=[bk])
            elif k >= 4 * s:
                j = k - 4 * s
                c.tt(ps[:, :], ps[:, :], masks[:, j * SW:(j + 1) * SW], OP.add,
                     r=[bk] + msb_keys(j * SW, (j + 1) * SW), w=[bk])
            pk = PT[k % 4]
            Pt = c.sb[pk]
            c.act(Pt[:, :], ps[:, :], AF.Exp, r=[bk], w=[pk])
            if k + 4 < nk:
                emit_S(k + 4)
            c.mm(c.sb["psPV"][0:65, :], vsbh[:, 65 * k:65 * (k + 1)], Pt[:, :],
                 start=(k == 0), stop=(k == nk - 1),
                 r=[vk, pk], w=["psPV"] if k == 0 else (),
                 pw=["psPV"] if k > 0 else ())
        c.recip(c.sb["qrw1"][:, :], c.sb["psPV"][64:65, :], r=["psPV"], w=["qrw1"])
        pend = (h, s, "gb0" if sp == 0 else "gb1")
    flush_epilogue()


def oT_reader(s):
    return [f"D:oT:{h}:{s}" for h in range(H)]


def e_router(c, src_name):
    n, rw, wrowT = c.d(src_name), c.d("F0_router"), c.d("wrowT")
    rsb = c.sb["gsb"]
    c.dma(rsb[:, 0:64].rearrange("p (c j) -> p c j", c=ND),
          rw.rearrange("(c p) j -> p c j", p=128), eng="gpsimd",
          r=["D:F0_router"], w=["gsb"])
    lg = c.sb["rtf"]
    pc = c.sb["rpc"]
    for s in range(NS):
        sp = s % 2
        nsb = c.sb["nsbA" if sp == 0 else "nsbB"]
        nkey = "nsbA" if sp == 0 else "nsbB"
        cols = slice(s * SW, (s + 1) * SW)
        c.dma(nsb[:, :].rearrange("p (c t) -> p c t", c=ND), fm(n, cols),
              r=srd(src_name, s), w=[nkey])
        for tt in range(SW // 128):
            tp = tt % 2
            ps = c.sb[f"psA{tp}"]
            for ic in range(ND):
                c.mm(ps[:, 0:8], nsb[:, ic * SW + tt * 128: ic * SW + (tt + 1) * 128],
                     rsb[:, ic * 8:(ic + 1) * 8], start=(ic == 0), stop=(ic == ND - 1),
                     r=[nkey, "gsb"], w=[f"psA{tp}"] if ic == 0 else (),
                     pw=[f"psA{tp}"] if ic > 0 else ())
            c.rmax(pc[:, 0:1], ps[:, 0:8], negate=True, r=[f"psA{tp}"], w=["rpc"])
            c.act(lg[:, 0:8], ps[:, 0:8], AF.Exp, bias=pc[:, 0:1],
                  r=[f"psA{tp}", "rpc"], w=["rtf"])
            c.rsum(pc[:, 1:2], lg[:, 0:8], r=["rtf"], pw=["rpc"])
            c.recip(pc[:, 2:3], pc[:, 1:2], r=["rpc"], pw=["rpc"])
            c.ts(lg[:, 0:8], lg[:, 0:8], pc[:, 2:3], OP.mult, r=["rtf", "rpc"], w=["rtf"])
            c.rmax(pc[:, 3:4], lg[:, 0:8], r=["rtf"], pw=["rpc"])
            c.ts(lg[:, 8:16], lg[:, 0:8], pc[:, 3:4], OP.is_equal, r=["rtf", "rpc"], w=["rtf:m1"])
            c.ts(lg[:, 16:24], lg[:, 8:16], -1.0, OP.mult, 1.0, OP.add, r=["rtf:m1"], w=["rtf:nm"])
            c.tt(lg[:, 16:24], lg[:, 0:8], lg[:, 16:24], OP.mult, r=["rtf", "rtf:nm"], w=["rtf:nm"])
            c.rmax(pc[:, 4:5], lg[:, 16:24], r=["rtf:nm"], pw=["rpc"])
            c.ts(lg[:, 16:24], lg[:, 16:24], pc[:, 4:5], OP.is_equal, r=["rtf:nm", "rpc"], w=["rtf:nm"])
            c.ts(lg[:, 8:16], lg[:, 8:16], pc[:, 3:4], OP.mult, r=["rtf:m1", "rpc"], w=["rtf:m1"])
            c.ts(lg[:, 16:24], lg[:, 16:24], pc[:, 4:5], OP.mult, r=["rtf:nm", "rpc"], w=["rtf:nm"])
            c.tt(lg[:, 8:16], lg[:, 8:16], lg[:, 16:24], OP.add, r=["rtf:m1", "rtf:nm"], w=["rtf:m1"])
            c.transpose(c.sb["psPV"][0:8, 0:128], lg[:, 8:16], c.sb["identf"][:, :],
                        r=["rtf:m1", "identf"], w=["psPV"])
            c.copy(c.sb["rtt"][0:8, 0:128], c.sb["psPV"][0:8, 0:128], r=["psPV"], w=["rtt"])
            t0 = s * SW + tt * 128
            c.dma(wrowT[0:8, t0:t0 + 128], c.sb["rtt"][0:8, 0:128], eng="scalar",
                  r=["rtt"], pw=[f"D:wrowT:{s}"])


def e_moe(c, src_name, collect_rows=None, cslot=0):
    n, moe32, wrowT = c.d(src_name), c.d("moe32"), c.d("wrowT")
    ew1, ew2, ew3 = c.d("F0_ew1"), c.d("F0_ew2"), c.d("F0_ew3")
    for e_i in range(E):
        s1slot = "A0" if e_i % 2 == 0 else "B1"
        s3slot = "A1" if e_i % 2 == 0 else "C0"
        s2slot = "B0" if e_i % 2 == 0 else "A0"
        load_w_slot(c, "F0_ew1", s1slot, w_slice=ew1[e_i])
        load_w_slot(c, "F0_ew3", s3slot, w_slice=ew3[e_i])
        load_w_slot(c, "F0_ew2", s2slot, w_slice=ew2[e_i])
        w1b, w3b, w2b = slot_ap(c, s1slot), slot_ap(c, s3slot), slot_ap(c, s2slot)
        for s in range(NS):
            sp = s % 2
            nsb = c.sb["nsbA" if sp == 0 else "nsbB"]
            nkey = "nsbA" if sp == 0 else "nsbB"
            cols = slice(s * SW, (s + 1) * SW)
            c.dma(nsb[:, :].rearrange("p (c t) -> p c t", c=ND), fm(n, cols),
                  eng="scalar", r=srd(src_name, s), w=[nkey])
            c.dma(c.sb["qrw2"][:, :], wrowT[e_i:e_i + 1, cols], eng="scalar",
                  r=[f"D:wrowT:{s}"], w=["qrw2"])
            c.mm(c.sb["psB0"][:, :], c.sb["onesrow_f"][0:1, :], c.sb["qrw2"][:, :],
                 r=["qrw2", "ones"], w=["psB0"])
            c.copy(c.sb["rbc"][:, :], c.sb["psB0"][:, :], r=["psB0"], w=["rbc"])
            for eh in range(ND):
                p = eh % 2
                psu = c.sb[f"psA{p}"]
                for ic in range(ND):
                    c.mm(psu[:, :], w1b[:, ic * 1024 + eh * 128: ic * 1024 + (eh + 1) * 128],
                         nsb[:, ic * SW:(ic + 1) * SW], start=(ic == 0), stop=(ic == ND - 1),
                         r=[s1slot, nkey], w=[f"psA{p}"] if ic == 0 else (),
                         pw=[f"psA{p}"] if ic > 0 else ())
                s1 = c.sb[f"gb{p}"]
                c.act(s1[:, :], psu[:, :], AF.Silu, r=[f"psA{p}"], w=[f"gb{p}"])
                psv = c.sb[f"psB{p}"]
                for ic in range(ND):
                    c.mm(psv[:, :], w3b[:, ic * 1024 + eh * 128: ic * 1024 + (eh + 1) * 128],
                         nsb[:, ic * SW:(ic + 1) * SW], start=(ic == 0), stop=(ic == ND - 1),
                         r=[s3slot, nkey], w=[f"psB{p}"] if ic == 0 else (),
                         pw=[f"psB{p}"] if ic > 0 else ())
                gslice = c.sb["gsb"][:, eh * SW:(eh + 1) * SW]
                c.tt(gslice, s1[:, :], psv[:, :], OP.mult,
                     r=[f"gb{p}", f"psB{p}"], pw=["gsb"])
                c.tt(gslice, gslice, c.sb["rbc"][:, :], OP.mult,
                     r=["gsb", "rbc"], pw=["gsb"])
            for dc in range(ND):
                p = dc % 2
                psu = c.sb[f"psA{p}"]
                for eh in range(ND):
                    c.mm(psu[:, :], w2b[:, eh * 1024 + dc * 128: eh * 1024 + (dc + 1) * 128],
                         c.sb["gsb"][:, eh * SW:(eh + 1) * SW], start=(eh == 0),
                         stop=(eh == ND - 1), r=[s2slot, "gsb"],
                         w=[f"psA{p}"] if eh == 0 else (), pw=[f"psA{p}"] if eh > 0 else ())
                ev = c.sb[f"gf{p}"]
                mk = sk("moe32", dc, s)
                if e_i == 0:
                    c.copy(ev[:, :], psu[:, :], r=[f"psA{p}"], w=[f"gf{p}"])
                else:
                    xb = c.sb[f"xsbB{p}"]
                    c.dma(xb[:, :], moe32[dc * 128:(dc + 1) * 128, cols],
                          r=[mk], w=[f"xsbB{p}"])
                    c.tt(ev[:, :], psu[:, :], xb[:, :], OP.add,
                         r=[f"psA{p}", f"xsbB{p}"], w=[f"gf{p}"])
                c.dma(moe32[dc * 128:(dc + 1) * 128, cols], ev[:, :],
                      r=[f"gf{p}"], w=[mk])
                if collect_rows is not None and e_i == E - 1:
                    collect_chunk(c, ev[:, :], f"gf{p}", p, dc == 0, dc == ND - 1,
                                  cslot, rms=True, claim=(s == 0 and dc == 0))
            if collect_rows is not None and e_i == E - 1:
                _rows_finish(c, s, True, cslot)
                _rows_store(c, collect_rows, s, cslot)


def e_glu(c):
    h, w1, w3, g = c.d("h"), c.d("F0_w1"), c.d("F0_w3"), c.d("g")
    slot_rot = [("A0", "A1"), ("B0", "B1"), ("C0", "A0"), ("A1", "B0")]
    for it, og in enumerate(range(0, FF, 1024)):
        s1slot, s3slot = slot_rot[it]
        load_w_slot(c, "F0_w1", s1slot, w_slice=w1[:, og:og + 1024])
        load_w_slot(c, "F0_w3", s3slot, w_slice=w3[:, og:og + 1024])
        wap1, wap3 = slot_ap(c, s1slot), slot_ap(c, s3slot)
        for s in range(NS):
            sp = s % 2
            nsb = c.sb["nsbA" if sp == 0 else "nsbB"]
            nkey = "nsbA" if sp == 0 else "nsbB"
            cols = slice(s * SW, (s + 1) * SW)
            c.dma(nsb[:, :].rearrange("p (c t) -> p c t", c=ND), fm(h, cols),
                  r=srd("h", s), w=[nkey])
            for j in range(8):
                p = j % 2
                psu, psv = c.sb[f"psA{p}"], c.sb[f"psB{p}"]
                for ic in range(ND):
                    c.mm(psu[:, :], wap1[:, ic * 1024 + j * 128: ic * 1024 + (j + 1) * 128],
                         nsb[:, ic * SW:(ic + 1) * SW], start=(ic == 0), stop=(ic == ND - 1),
                         r=[s1slot, nkey], w=[f"psA{p}"] if ic == 0 else (),
                         pw=[f"psA{p}"] if ic > 0 else ())
                s1 = c.sb[f"gb{p}"]
                c.act(s1[:, :], psu[:, :], AF.Silu, r=[f"psA{p}"], w=[f"gb{p}"])
                for ic in range(ND):
                    c.mm(psv[:, :], wap3[:, ic * 1024 + j * 128: ic * 1024 + (j + 1) * 128],
                         nsb[:, ic * SW:(ic + 1) * SW], start=(ic == 0), stop=(ic == ND - 1),
                         r=[s3slot, nkey], w=[f"psB{p}"] if ic == 0 else (),
                         pw=[f"psB{p}"] if ic > 0 else ())
                ev = c.sb[f"gc{p}"]
                c.tt(ev[:, :], s1[:, :], psv[:, :], OP.mult,
                     r=[f"gb{p}", f"psB{p}"], w=[f"gc{p}"])
                jg = og // 128 + j
                c.dma(g[og + j * 128: og + (j + 1) * 128, cols], ev[:, :],
                      r=[f"gc{p}"], w=[f"D:g:{jg}:{s}"])


def e_projK32(c, w_name, src_name, dst_name, bias=None, collect_rows=None,
              collect_rms=False, cslot=0):
    w, src, dst = c.d(w_name), c.d(src_name), c.d(dst_name)
    ssb = c.sb["ssb32"]
    if bias is not None:
        load_pcols(c, "gpc", bias)
    c.dma(c.sb["wsb"][:, :].rearrange("p (c j) -> p c j", c=16),
          w[0:2048, :].rearrange("(c p) j -> p c j", p=128), eng="gpsimd",
          r=["D:" + w_name], w=["A0", "A1"])
    c.dma(c.sb["wsb2"][:, :].rearrange("p (c j) -> p c j", c=16),
          w[2048:4096, :].rearrange("(c p) j -> p c j", p=128), eng="gpsimd",
          r=["D:" + w_name], w=["B0", "B1"])
    for s in range(NS):
        cols = slice(s * SW, (s + 1) * SW)
        c.dma(ssb[:, :].rearrange("p (c t) -> p c t", c=16),
              fm(src[0:2048, :], cols),
              r=[f"D:{src_name}:{jg}:{s}" for jg in range(16)], w=["C0"])
        c.dma(c.sb["nsbA"][:, :].rearrange("p (c t) -> p c t", c=8),
              fm(src[2048:3072, :], cols),
              r=[f"D:{src_name}:{jg}:{s}" for jg in range(16, 24)], w=["nsbA"])
        c.dma(c.sb["nsbB"][:, :].rearrange("p (c t) -> p c t", c=8),
              fm(src[3072:4096, :], cols),
              r=[f"D:{src_name}:{jg}:{s}" for jg in range(24, 32)], w=["nsbB"])
        for s_ in [s]:
            pass
        for oc in range(ND):
            p = oc % 2
            ps = c.sb[f"psA{p}"]
            for ic in range(32):
                wt = (c.sb["wsb"] if ic < 16 else c.sb["wsb2"])[
                    :, (ic % 16) * 1024 + oc * 128:(ic % 16) * 1024 + (oc + 1) * 128]
                rslot = ("A0" if ic < 8 else "A1") if ic < 16 else ("B0" if ic < 24 else "B1")
                if ic < 16:
                    act = ssb[:, ic * SW:(ic + 1) * SW]
                    akey = "C0"
                elif ic < 24:
                    act = c.sb["nsbA"][:, (ic - 16) * SW:(ic - 15) * SW]
                    akey = "nsbA"
                else:
                    act = c.sb["nsbB"][:, (ic - 24) * SW:(ic - 23) * SW]
                    akey = "nsbB"
                c.mm(ps[:, :], wt, act,
                     start=(ic == 0), stop=(ic == 31),
                     r=[rslot, akey],
                     w=[f"psA{p}"] if ic == 0 else (), pw=[f"psA{p}"] if ic > 0 else ())
            ev = c.sb[f"gf{p}"]
            if bias is not None:
                c.ts(ev[:, :], ps[:, :], 1.0, OP.mult, c.sb["gpc"][:, oc:oc + 1], OP.add,
                     r=[f"psA{p}", "gpc"], w=[f"gf{p}"])
            else:
                c.copy(ev[:, :], ps[:, :], r=[f"psA{p}"], w=[f"gf{p}"])
            c.dma(dst[oc * 128:(oc + 1) * 128, cols], ev[:, :], r=[f"gf{p}"],
                  w=[sk(dst_name, oc, s)])
            if collect_rows is not None:
                collect_chunk(c, ev[:, :], f"gf{p}", p, oc == 0, oc == ND - 1, cslot,
                              rms=collect_rms, claim=(s == 0 and oc == 0))
        if collect_rows is not None:
            _rows_finish(c, s, collect_rms, cslot)
            _rows_store(c, collect_rows, s, cslot)


def e_l1ffn(c, src_name):
    n, g = c.d(src_name), c.d("g")
    gateT, gate_b = c.d("G1_gateT"), c.d("G1_gate_b")
    AcatT, BcatT = c.d("G1_AcatT"), c.d("G1_BcatT")
    wiT = c.d("G1_wiT")
    small = c.sb["lsm"]
    c.dma(small[:, 0:64].rearrange("p (c j) -> p c j", c=ND),
          gateT.rearrange("(c p) j -> p c j", p=128), eng="gpsimd",
          r=["D:G1_gateT"], pw=["lsm"])
    c.dma(small[:, 64:64 + 256].rearrange("p (c j) -> p c j", c=ND),
          AcatT.rearrange("(c p) j -> p c j", p=128), eng="gpsimd",
          r=["D:G1_AcatT"], pw=["lsm"])
    gb_row = bass_rust.AP(tensor=gate_b.tensor, offset=0, ap=[[E, 1], [1, E]])
    c.dma(c.sb["gbrow"][0:1, 0:8], gb_row, eng="gpsimd", r=["D:G1_gate_b"], w=["gbrow"])
    lg = c.sb["rtf"]
    pc = c.sb["rpc"]
    rmt = c.sb["gsb"]
    orb = c.sb["onesrow_bf"]
    for s in range(NS):
        sp = s % 2
        nsb = c.sb["nsbA" if sp == 0 else "nsbB"]
        nkey = "nsbA" if sp == 0 else "nsbB"
        cols = slice(s * SW, (s + 1) * SW)
        c.dma(nsb[:, :].rearrange("p (c t) -> p c t", c=ND), fm(n, cols),
              r=srd(src_name, s), w=[nkey])
        for tt in range(SW // 128):
            tp = tt % 2
            ps = c.sb[f"psA{tp}"]
            for ic in range(ND):
                c.mm(ps[:, 0:8], nsb[:, ic * SW + tt * 128: ic * SW + (tt + 1) * 128],
                     small[:, ic * 8:(ic + 1) * 8], start=(ic == 0), stop=False,
                     r=[nkey, "lsm"], w=[f"psA{tp}"] if ic == 0 else (),
                     pw=[f"psA{tp}"] if ic > 0 else ())
            c.mm(ps[:, 0:8], orb[0:1, :], c.sb["gbrow"][0:1, 0:8],
                 start=False, stop=True, r=["ones", "gbrow"], pw=[f"psA{tp}"])
            c.copy(lg[:, 0:8], ps[:, 0:8], r=[f"psA{tp}"], w=["rtf"])
            c.rmax(pc[:, 0:1], lg[:, 0:8], r=["rtf"], w=["rpc"])
            c.ts(lg[:, 8:16], lg[:, 0:8], pc[:, 0:1], OP.is_equal,
                 r=["rtf", "rpc"], w=["rtf:m1"])
            m32 = lg.ap()[:, 16:48].rearrange("p (e r2) -> p e r2", r2=R)
            m1v = lg.ap()[:, 8:16].rearrange("p (e r2) -> p e r2", r2=1)
            for r_ in range(R):
                c.copy(m32[:, :, r_:r_ + 1], m1v, r=["rtf:m1"], pw=["rtf:nm"])
            psb = c.sb[f"psS{tp}"]
            for ic in range(ND):
                c.mm(psb[:, 0:32], nsb[:, ic * SW + tt * 128: ic * SW + (tt + 1) * 128],
                     small[:, 64 + ic * 32:64 + (ic + 1) * 32], start=(ic == 0),
                     stop=(ic == ND - 1), r=[nkey, "lsm"],
                     w=[f"psS{tp}"] if ic == 0 else (), pw=[f"psS{tp}"] if ic > 0 else ())
            c.tt(c.sb["rtt"][:, 0:32], psb[:, 0:32], lg[:, 16:48], OP.mult,
                 r=[f"psS{tp}", "rtf:nm"], w=["rtt"])
            c.transpose(c.sb["psPV"][0:32, 0:128], c.sb["rtt"][:, 0:32],
                        c.sb["identf"][:, :], r=["rtt", "identf"], w=["psPV"])
            c.copy(rmt[0:32, s * SW + tt * 128: s * SW + (tt + 1) * 128],
                   c.sb["psPV"][0:32, 0:128], r=["psPV"], pw=["gsb"])
    bc_sb = c.sb["Msb"]
    c.dma(bc_sb[0:32, 0:FF], BcatT[:, :], eng="gpsimd", r=["D:G1_BcatT"],
          w=["Msb:0", "Msb:1", "Msb:2", "Msb:3"])
    wrot = ["A0", "B0", "C0", "A1"]
    for it, og in enumerate(range(0, FF, 1024)):
        wslot = wrot[it]
        wi = c.d("G1_wiT")
        c.dma(slot_ap(c, wslot).rearrange("p (c j) -> p c j", c=ND),
              wi[:, og:og + 1024].rearrange("(c p) j -> p c j", p=128), eng="gpsimd",
              r=["D:G1_wiT"], w=[wslot])
        wap = slot_ap(c, wslot)
        load_pcols(c, "gpc", "G1_wi_b", n=8, off=og // 128)
        for s in range(NS):
            sp = s % 2
            nsb = c.sb["nsbA" if sp == 0 else "nsbB"]
            nkey = "nsbA" if sp == 0 else "nsbB"
            cols = slice(s * SW, (s + 1) * SW)
            c.dma(nsb[:, :].rearrange("p (c t) -> p c t", c=ND), fm(n, cols),
                  r=srd(src_name, s), w=[nkey])
            for j in range(8):
                p = j % 2
                ps = c.sb[f"psA{p}"]
                c.mm(ps[:, :], bc_sb[0:32, og + j * 128: og + (j + 1) * 128],
                     rmt[0:32, cols], start=True, stop=False,
                     r=["Msb:0", "Msb:1", "Msb:2", "Msb:3", "gsb"], w=[f"psA{p}"])
                for ic in range(ND):
                    c.mm(ps[:, :], wap[:, ic * 1024 + j * 128: ic * 1024 + (j + 1) * 128],
                         nsb[:, ic * SW:(ic + 1) * SW], start=False, stop=(ic == ND - 1),
                         r=[wslot, nkey], pw=[f"psA{p}"])
                ev = c.sb[f"gb{p}"]
                c.act(ev[:, :], ps[:, :], AF.Relu, bias=c.sb["gpc"][:, j:j + 1],
                      r=[f"psA{p}", "gpc"], w=[f"gb{p}"])
                jg = og // 128 + j
                c.dma(g[og + j * 128: og + (j + 1) * 128, cols], ev[:, :],
                      r=[f"gb{p}"], w=[f"D:g:{jg}:{s}"])
    e_projK32(c, "G1_woT", "g", "y32", bias="G1_wo_b", collect_rows="rowsA", cslot=1)


def e_bhat(c):
    stripes = c.d("L1_bias_stripes")
    c.memset(c.sb["bhat"][:, :], NEGBIG, w=["bhat"])
    r = c.sb["qrw1"]
    for h in range(H):
        for ch in range(5):
            w = min(SW, 2559 - ch * SW)
            c.dma(r[0:1, 0:w], stripes[h, 127:128, ch * SW:ch * SW + w],
                  eng="scalar", r=["D:L1_bias_stripes"], w=["qrw1"])
            c.rmax(c.sb["qrw2"][0:1, 0:1], r[0:1, 0:w], r=["qrw1"], w=["qrw2"])
            c.tt(c.sb["bhat"][0:1, h:h + 1], c.sb["bhat"][0:1, h:h + 1],
                 c.sb["qrw2"][0:1, 0:1], OP.max, r=["bhat", "qrw2"], w=["bhat"])


def ph_all(c):
    e_prep(c)
    c.alloc("rw1", [1, SW], F32)
    c.alloc("rw2", [1, SW], F32)
    c.alloc("rw3", [1, SW], F32)
    c.alloc("rw4", [1, SW], F32)
    e_stats(c, "xT", "rowsA", slot=0)
    e_apply(c, "xT", "rowsA", "n1", pair="A")
    e_qk(c, "L0_Wk", "n1", "kT", "k", "A0", bias="L0_bk")
    e_qk(c, "L0_Wq", "n1", "qT", "q", "A1", bias="L0_bq")
    e_v(c, "L0_Wv", "n1", "B0", bias="L0_bv")
    e_attn(c, use_bias=False)
    e_proj(c, "L0_Wo", "oT", "y32", "B1", bias="L0_bo", out_dt=F32,
           src_reader=oT_reader, collect_rows="rowsB", cslot=1)
    e_apply(c, "y32", "rowsB", "d32", out_dt=F32, gamma="L0_gamma", beta="L0_beta",
            collect_rows="rowsC", slot=0, pair="B")
    e_apply(c, "d32", "rowsC", "x32a", out_dt=F32, resid_name="xT",
            collect_rows="rowsD", slot=1, pair="A")
    e_apply(c, "x32a", "rowsD", "n2", pair="B")
    e_router(c, "n2")
    e_qk(c, "F0_wk", "n2", "kT", "k", "A0")
    e_qk(c, "F0_wq", "n2", "qT", "q", "A1")
    e_v(c, "F0_wv", "n2", "B0")
    e_attn(c, use_bias=False)
    e_proj(c, "F0_wo", "oT", "y32", "B1", out_dt=F32, src_reader=oT_reader,
           collect_rows="rowsE", collect_rms=True, cslot=0)
    e_apply(c, "y32", "rowsE", "h", out_dt=BF16, rmsw="F0_attn_nw",
            resid_name="n2", resid_dt=BF16, pair="A")
    e_glu(c)
    e_projK32(c, "F0_w2", "g", "y32", collect_rows="rowsF", collect_rms=True, cslot=1)
    e_apply(c, "y32", "rowsF", "d32", out_dt=F32, rmsw="F0_ff_nw",
            resid_name="h", resid_dt=BF16, pair="B")
    e_moe(c, "n2", collect_rows="rowsG", cslot=0)
    e_apply(c, "moe32", "rowsG", "y32", out_dt=F32, rmsw="F0_moe_nw",
            resid_name="d32", resid_dt=F32, collect_rows="rowsH", slot=1, pair="A")
    e_apply(c, "y32", "rowsH", "x32b", out_dt=F32, resid_name="x32a",
            collect_rows="rowsI", slot=0, pair="B")
    e_apply(c, "x32b", "rowsI", "n3", pair="A")
    e_bhat(c)
    e_qk(c, "L1_WKT", "n3", "kT", "k", "A0", bias="L1_bK")
    e_qk(c, "L1_WQT", "n3", "qT", "q", "A1", bias="L1_bQ", use_bhat=True)
    e_v(c, "L1_WVT", "n3", "B0", bias="L1_bV")
    e_attn(c, use_bias=True)
    e_proj(c, "L1_WOT", "oT", "y32", "B1", bias="L1_bO", out_dt=F32,
           src_reader=oT_reader, collect_rows="rowsJ", cslot=1)
    e_apply(c, "y32", "rowsJ", "x32c", out_dt=F32, resid_name="x32b",
            collect_rows="rowsK", slot=0, pair="B")
    e_apply(c, "x32c", "rowsK", "n4", pair="A")
    e_l1ffn(c, "n4")
    e_apply(c, "y32", "rowsA", "yT", out_dt=F32, resid_name="x32c", pair="B")


def _host_layout(inputs):
    g = {}
    g["xT"] = [np.ascontiguousarray(np.asarray(inputs["L0_attn_x"][i]).T) for i in range(NCORES)]
    def A(name):
        return np.ascontiguousarray(np.asarray(inputs[name], dtype=np.float32))
    shared = {
        "L0_Wq": A("L0_attn_Wq"), "L0_Wk": A("L0_attn_Wk"), "L0_Wv": A("L0_attn_Wv"),
        "L0_Wo": A("L0_attn_Wo"),
        "L0_bq": A("L0_attn_bq"), "L0_bk": A("L0_attn_bk"), "L0_bv": A("L0_attn_bv"),
        "L0_bo": A("L0_attn_bo"), "L0_gamma": A("L0_attn_gamma"), "L0_beta": A("L0_attn_beta"),
        "F0_wq": A("L0_ffn_wq"), "F0_wk": A("L0_ffn_wk"), "F0_wv": A("L0_ffn_wv"),
        "F0_wo": A("L0_ffn_wo"), "F0_w1": A("L0_ffn_w1"), "F0_w2": A("L0_ffn_w2"),
        "F0_w3": A("L0_ffn_w3"),
        "F0_attn_nw": A("L0_ffn_attn_norm_w"), "F0_ff_nw": A("L0_ffn_ff_norm_w"),
        "F0_moe_nw": A("L0_ffn_moe_norm_w"), "F0_router": A("L0_ffn_router_w"),
        "F0_ew1": A("L0_ffn_ew1"), "F0_ew2": A("L0_ffn_ew2"), "F0_ew3": A("L0_ffn_ew3"),
        "L1_WQT": np.ascontiguousarray(A("L1_attn_WQ").T),
        "L1_WKT": np.ascontiguousarray(A("L1_attn_WK").T),
        "L1_WVT": np.ascontiguousarray(A("L1_attn_WV").T),
        "L1_WOT": np.ascontiguousarray(A("L1_attn_WO").T),
        "L1_bQ": A("L1_attn_bQ"), "L1_bK": A("L1_attn_bK"), "L1_bV": A("L1_attn_bV"),
        "L1_bO": A("L1_attn_bO"),
        "G1_gateT": np.ascontiguousarray(A("L1_ffn_gate_W").T),
        "G1_gate_b": A("L1_ffn_gate_b"),
        "G1_wiT": np.ascontiguousarray(A("L1_ffn_wi_W").T),
        "G1_wi_b": A("L1_ffn_wi_b"),
        "G1_woT": np.ascontiguousarray(A("L1_ffn_wo_W").T),
        "G1_wo_b": A("L1_ffn_wo_b"),
        "G1_AcatT": np.ascontiguousarray(A("L1_ffn_lora_A").reshape(E * R, D).T),
        "G1_BcatT": np.ascontiguousarray(
            A("L1_ffn_lora_B").transpose(0, 2, 1).reshape(E * R, FF)),
    }
    tbl = A("L1_attn_rel_pos_table")
    yy = np.arange(MW)[None, :]
    pp = np.arange(128)[:, None]
    idx = np.minimum(1663 + yy - pp, 4094)
    stripes = tbl[idx, :]
    masked = (yy - 384 - pp) < 0
    stripes = np.where(masked[:, :, None], np.float32(MNEG), stripes)
    shared["L1_bias_stripes"] = np.ascontiguousarray(stripes.transpose(2, 0, 1))
    return g, shared


def kernel(**inputs):
    percore, shared = _host_layout(inputs)
    nc = bass.Bass("TRN2", target_bir_lowering=False)
    c = Ctx(nc, set(EXTERNAL_IN), set(EXTERNAL_OUT))
    try:
        ph_all(c)
        c.realize()
    finally:
        c.close()
    in_maps = []
    for i in range(NCORES):
        m = {}
        for rn in c.reads:
            a = percore["xT"][i] if rn == "xT" else shared[rn]
            m[rn] = _as_np(a, rn)
        in_maps.append(m)
    res = run_bass_kernel_spmd(nc, in_maps, core_ids=list(range(NCORES)))
    out = np.stack([np.ascontiguousarray(np.asarray(res.results[i]["yT"]).T)
                    for i in range(NCORES)])
    return out.astype(np.float32)


def _as_np(a, name):
    shape, dt = DRAM_SPECS[name]
    npdt = {F32: np.float32, BF16: NPBF, I32: np.int32}[dt]
    a = np.asarray(a)
    if a.dtype != npdt:
        a = a.astype(npdt)
    return np.ascontiguousarray(a.reshape(shape))


# version 11 (final)
# speedup vs baseline: 1.2485x; 1.0554x over previous
import numpy as np
import ml_dtypes
import concourse.bass as bass
import concourse.mybir as mybir
import bass_rust
from concourse.bass_utils import run_bass_kernel_spmd

F32 = mybir.dt.float32
BF16 = mybir.dt.bfloat16
I32 = mybir.dt.int32
AF = mybir.ActivationFunctionType
OP = mybir.AluOpType
NPBF = ml_dtypes.bfloat16

T = 4096
D = 1024
H = 16
DH = 64
FF = 4096
E = 8
R = 4
NCORES = 8
ND = D // 128
SW = 512
NS = T // SW
NT = T // 128
EPS = 1e-5
CEXP = 20.0
NEGBIG = -1e30
MNEG = -60000.0
MW = 4480

ENGINES = ["sync", "tensor", "vector", "scalar", "gpsimd"]
DMA_QUEUES = ("sync", "gpsimd", "scalar")

DRAM_SPECS = {
    "xT":    ([D, T], F32),
    "yT":    ([D, T], F32),
    "x32a":  ([D, T], F32),
    "x32b":  ([D, T], F32),
    "x32c":  ([D, T], F32),
    "n1":    ([D, T], BF16),
    "n2":    ([D, T], BF16),
    "n3":    ([D, T], BF16),
    "n4":    ([D, T], BF16),
    "h":     ([D, T], BF16),
    "qT":    ([D, T], BF16),
    "kT":    ([D, T], BF16),
    "vtok":  ([T, H * 65], BF16),
    "oT":    ([D, T], BF16),
    "y32":   ([D, T], F32),
    "d32":   ([D, T], F32),
    "moe32": ([D, T], F32),
    "g":     ([FF, T], BF16),
    "qext":  ([H, T], BF16),
    "wrowT": ([E, T], F32),
    "rowsA": ([2, T], F32), "rowsB": ([2, T], F32), "rowsC": ([2, T], F32),
    "rowsD": ([2, T], F32), "rowsE": ([2, T], F32), "rowsF": ([2, T], F32),
    "rowsG": ([2, T], F32), "rowsH": ([2, T], F32), "rowsI": ([2, T], F32),
    "rowsJ": ([2, T], F32), "rowsK": ([2, T], F32),
    "L0_Wq": ([D, D], F32), "L0_Wk": ([D, D], F32), "L0_Wv": ([D, D], F32), "L0_Wo": ([D, D], F32),
    "L0_bq": ([D], F32), "L0_bk": ([D], F32), "L0_bv": ([D], F32), "L0_bo": ([D], F32),
    "L0_gamma": ([D], F32), "L0_beta": ([D], F32),
    "F0_wq": ([D, D], F32), "F0_wk": ([D, D], F32), "F0_wv": ([D, D], F32), "F0_wo": ([D, D], F32),
    "F0_w1": ([D, FF], F32), "F0_w2": ([FF, D], F32), "F0_w3": ([D, FF], F32),
    "F0_attn_nw": ([D], F32), "F0_ff_nw": ([D], F32), "F0_moe_nw": ([D], F32),
    "F0_router": ([D, E], F32),
    "F0_ew1": ([E, D, D], F32), "F0_ew2": ([E, D, D], F32), "F0_ew3": ([E, D, D], F32),
    "L1_WQT": ([D, D], F32), "L1_WKT": ([D, D], F32), "L1_WVT": ([D, D], F32), "L1_WOT": ([D, D], F32),
    "L1_bQ": ([D], F32), "L1_bK": ([D], F32), "L1_bV": ([D], F32), "L1_bO": ([D], F32),
    "L1_bias_stripes": ([H, 128, MW], F32),
    "G1_gateT": ([D, E], F32), "G1_gate_b": ([E], F32),
    "G1_wiT": ([D, FF], F32), "G1_wi_b": ([FF], F32),
    "G1_woT": ([FF, D], F32), "G1_wo_b": ([D], F32),
    "G1_AcatT": ([D, E * R], F32), "G1_BcatT": ([E * R, FF], F32),
}

EXTERNAL_IN = {
    "xT", "L0_Wq", "L0_Wk", "L0_Wv", "L0_Wo", "L0_bq", "L0_bk", "L0_bv", "L0_bo",
    "L0_gamma", "L0_beta",
    "F0_wq", "F0_wk", "F0_wv", "F0_wo", "F0_w1", "F0_w2", "F0_w3",
    "F0_attn_nw", "F0_ff_nw", "F0_moe_nw", "F0_router", "F0_ew1", "F0_ew2", "F0_ew3",
    "L1_WQT", "L1_WKT", "L1_WVT", "L1_WOT", "L1_bQ", "L1_bK", "L1_bV", "L1_bO",
    "L1_bias_stripes",
    "G1_gateT", "G1_gate_b", "G1_wiT", "G1_wi_b", "G1_woT", "G1_wo_b",
    "G1_AcatT", "G1_BcatT",
}
EXTERNAL_OUT = {"yT"}


def sk(name, ic, s):
    return f"D:{name}:{ic}:{s}"


def srd(name, s, n=ND):
    return [sk(name, ic, s) for ic in range(n)]


class Ctx:

    NSEM_DMA = 8

    def __init__(self, nc, external_in, external_out):
        self.nc = nc
        self.ops = []
        self.sb = {}
        self.dram = {}
        self._cms = []
        self.ext_in = external_in
        self.ext_out = external_out
        self.reads = set()
        self.writes = set()
        self.last_writer = {}
        self.part_writers = {}
        self.readers_ = {}
        self.eng_last = {e: None for e in ENGINES}

    def d(self, name):
        if name not in self.dram:
            shape, dt = DRAM_SPECS[name]
            if name in self.ext_in:
                kind = "ExternalInput"
            elif name in self.ext_out:
                kind = "ExternalOutput"
            else:
                kind = "Internal"
            self.dram[name] = self.nc.dram_tensor(name, shape, dt, kind=kind).ap()
        if name in self.ext_in:
            self.reads.add(name)
        if name in self.ext_out:
            self.writes.add(name)
        return self.dram[name]

    def alloc(self, name, shape, dt, psum=False):
        mk = self.nc.psum_tensor if psum else self.nc.sbuf_tensor
        cm = mk(name, shape, dt)
        t = cm.__enter__()
        self._cms.append(cm)
        self.sb[name] = t
        return t

    def close(self):
        for cm in reversed(self._cms):
            cm.__exit__(None, None, None)

    def op(self, eng, fn, inc=1, r=(), w=(), pw=()):
        oid = len(self.ops)
        deps = set()
        for k in r:
            j = self.last_writer.get(k)
            if j is not None:
                deps.add(j)
            deps.update(self.part_writers.get(k, ()))
        for k in w:
            j = self.last_writer.get(k)
            if j is not None:
                deps.add(j)
            deps.update(self.part_writers.get(k, ()))
            deps.update(self.readers_.get(k, ()))
        for k in pw:
            j = self.last_writer.get(k)
            if j is not None:
                deps.add(j)
            deps.update(self.readers_.get(k, ()))
        for k in r:
            self.readers_.setdefault(k, set()).add(oid)
        for k in w:
            self.last_writer[k] = oid
            self.part_writers[k] = []
            self.readers_[k] = set()
        for k in pw:
            self.part_writers.setdefault(k, []).append(oid)
        deps.discard(oid)
        self.ops.append((eng, inc, fn, sorted(deps)))
        self.eng_last[eng] = oid

    def dma(self, out, in_, eng="sync", r=(), w=(), pw=(), **kw):
        self.op(eng, lambda e, o=out, i=in_, k=kw: e.dma_start(out=o, in_=i, **k),
                inc=16, r=r, w=w, pw=pw)

    def mm(self, out, lhsT, rhs, start=True, stop=True, r=(), w=(), pw=(), tp=None):
        self.op("tensor", lambda e, o=out, l=lhsT, rr=rhs, s1=start, s2=stop, t=tp:
                e.matmul(o, l, rr, start=s1, stop=s2, tile_position=t), r=r, w=w, pw=pw)

    def transpose(self, out, in_, ident, r=(), w=(), pw=()):
        self.op("tensor", lambda e, o=out, i=in_, d=ident: e.transpose(o, i, d), r=r, w=w, pw=pw)

    def act(self, out, in_, func, bias=None, scale=None, r=(), w=(), pw=()):
        kw = {}
        if bias is not None:
            kw["bias"] = bias
        if scale is not None:
            kw["scale"] = scale
        self.op("scalar", lambda e, o=out, i=in_, f=func, k=kw:
                e.activation(out=o, in_=i, func=f, **k), r=r, w=w, pw=pw)

    def tt(self, out, a, b, o, r=(), w=(), pw=()):
        self.op("vector", lambda e, oo=out, aa=a, bb=b, op_=o:
                e.tensor_tensor(out=oo, in0=aa, in1=bb, op=op_), r=r, w=w, pw=pw)

    def ts(self, out, in0, s1, op0, s2=None, op1=None, r=(), w=(), pw=()):
        kw = dict(scalar2=s2)
        if op1 is not None:
            kw["op1"] = op1
        self.op("vector", lambda e, o=out, i=in0, a=s1, k=kw, p0=op0:
                e.tensor_scalar(out=o, in0=i, scalar1=a, op0=p0, **k), r=r, w=w, pw=pw)

    def stt(self, out, in0, scalar, in1, op0, op1, r=(), w=(), pw=()):
        self.op("vector", lambda e, o=out, i0=in0, s=scalar, i1=in1, p0=op0, p1=op1:
                e.scalar_tensor_tensor(out=o, in0=i0, scalar=s, in1=i1, op0=p0, op1=p1),
                r=r, w=w, pw=pw)

    def copy(self, out, in_, r=(), w=(), pw=()):
        self.op("vector", lambda e, o=out, i=in_: e.tensor_copy(out=o, in_=i), r=r, w=w, pw=pw)

    def memset(self, ap, v, r=(), w=(), pw=()):
        self.op("vector", lambda e, a=ap, vv=v: e.memset(a, vv), r=r, w=w, pw=pw)

    def rmax(self, out, in_, negate=False, r=(), w=(), pw=()):
        self.op("vector", lambda e, o=out, i=in_, n=negate:
                e.reduce_max(o, i, axis=mybir.AxisListType.X, negate=n), r=r, w=w, pw=pw)

    def rsum(self, out, in_, r=(), w=(), pw=()):
        self.op("vector", lambda e, o=out, i=in_:
                e.reduce_sum(o, i, axis=mybir.AxisListType.X), r=r, w=w, pw=pw)

    def recip(self, out, in_, r=(), w=(), pw=()):
        self.op("vector", lambda e, o=out, i=in_: e.reciprocal(out=o, in_=i), r=r, w=w, pw=pw)

    def iota(self, out, pattern, base, chmul, r=(), w=(), pw=()):
        self.op("gpsimd", lambda e, o=out, p=pattern, b=base, cm=chmul:
                e.iota(o, pattern=p, base=b, channel_multiplier=cm,
                       allow_small_or_imprecise_dtypes=True), r=r, w=w, pw=pw)

    def realize(self):
        nc = self.nc
        n_ops = len(self.ops)
        opinfo = [None] * n_ops
        sem_names = ["c_" + e for e in ENGINES]
        for q in DMA_QUEUES:
            for i in range(self.NSEM_DMA):
                sem_names.append(f"d_{q}_{i}")
        cum = {sn: 0 for sn in sem_names}
        rr = {q: 0 for q in DMA_QUEUES}
        presem = [None] * n_ops
        for oid, (eng, inc, fn, deps) in enumerate(self.ops):
            if inc == 16 and eng in DMA_QUEUES:
                i = rr[eng]
                rr[eng] = (i + 1) % self.NSEM_DMA
                sn = f"d_{eng}_{i}"
                presem[oid] = (sn, cum[sn])
            else:
                sn = "c_" + eng
            cum[sn] += inc
            opinfo[oid] = (sn, cum[sn])
        totals = dict(cum)
        sem_cms = {sn: nc.semaphore("S" + sn) for sn in sem_names}
        sems = {sn: sem_cms[sn].__enter__() for sn in sem_names}
        blk_cm = nc.Block()
        block = blk_cm.__enter__()
        try:
            def mksec(name):
                def body(eng):
                    hw = {sn: 0 for sn in sem_names}
                    for oid, (oeng, inc, fn, deps) in enumerate(self.ops):
                        if oeng != name:
                            continue
                        need = {}
                        for j in deps:
                            js, jcum = opinfo[j]
                            if js == "c_tensor" and name == "tensor":
                                continue
                            if jcum > need.get(js, 0):
                                need[js] = jcum
                        if presem[oid] is not None:
                            sn, cb = presem[oid]
                            if cb > need.get(sn, 0):
                                need[sn] = cb
                        for js, v in sorted(need.items()):
                            if v > hw[js]:
                                eng.wait_ge(sems[js], v)
                                hw[js] = v
                        inst = fn(eng)
                        my_sn = opinfo[oid][0]
                        inst.then_inc(sems[my_sn], inc)
                    if name in ("sync", "gpsimd"):
                        for sn in sem_names:
                            if totals[sn] > 0:
                                eng.wait_ge(sems[sn], totals[sn])
                return body
            for name in ENGINES:
                getattr(block, name)(mksec(name))
        finally:
            blk_cm.__exit__(None, None, None)
            for sn in sem_names:
                sem_cms[sn].__exit__(None, None, None)


def fm(ap_dram, cols, p=128):
    return ap_dram[:, cols].rearrange("(c p) t -> p c t", p=p)


SLOTS = {
    "A0": ("wsb", 0), "A1": ("wsb", 8192),
    "B0": ("wsb2", 0), "B1": ("wsb2", 8192),
    "C0": ("ssb32", 0),
}


def e_prep(c):
    if "wsb" in c.sb:
        return
    c.alloc("wsb", [128, 16384], BF16)
    c.alloc("wsb2", [128, 16384], BF16)
    c.alloc("ssb32", [128, 8192], BF16)
    c.alloc("nsbA", [128, 4096], BF16)
    c.alloc("nsbB", [128, 4096], BF16)
    c.alloc("gsb", [128, 4096], BF16)
    c.alloc("ksbhA", [128, 4096], BF16)
    c.alloc("ksbhB", [128, 4096], BF16)
    c.alloc("vsbhA", [128, NT * 65], BF16)
    c.alloc("vsbhB", [128, NT * 65], BF16)
    c.alloc("Msb", [128, MW], BF16)
    c.alloc("qsbA", [128, SW], BF16)
    c.alloc("qsbB", [128, SW], BF16)
    c.alloc("PtA", [128, SW], BF16)
    c.alloc("PtB", [128, SW], BF16)
    c.alloc("vsb2A", [128, 1040], BF16)
    c.alloc("vsb2B", [128, 1040], BF16)
    c.alloc("xsbA0", [128, SW], F32)
    c.alloc("xsbA1", [128, SW], F32)
    c.alloc("xsbB0", [128, SW], F32)
    c.alloc("xsbB1", [128, SW], F32)
    c.alloc("tf0", [128, SW], F32)
    c.alloc("tf1", [128, SW], F32)
    c.alloc("evB0", [128, SW], BF16)
    c.alloc("evB1", [128, SW], BF16)
    c.alloc("evC0", [128, SW], BF16)
    c.alloc("evC1", [128, SW], BF16)
    c.alloc("xbf0", [128, SW], BF16)
    c.alloc("xbf1", [128, SW], BF16)
    c.alloc("x2bf0", [128, SW], BF16)
    c.alloc("x2bf1", [128, SW], BF16)
    c.alloc("bcmA", [128, SW], F32)
    c.alloc("bcrA", [128, SW], F32)
    c.alloc("bcmB", [128, SW], F32)
    c.alloc("bcrB", [128, SW], F32)
    c.alloc("pcol", [128, 16], F32)
    c.alloc("pcol2", [128, 16], F32)
    c.alloc("gb0", [128, SW], BF16)
    c.alloc("gb1", [128, SW], BF16)
    c.alloc("gc0", [128, SW], BF16)
    c.alloc("gc1", [128, SW], BF16)
    c.alloc("gf0", [128, SW], F32)
    c.alloc("gf1", [128, SW], F32)
    c.alloc("gpc", [128, 16], F32)
    c.alloc("qrw1", [1, SW], F32)
    c.alloc("qrw2", [1, SW], F32)
    c.alloc("rlA1", [1, SW], F32)
    c.alloc("rlA2", [1, SW], F32)
    c.alloc("rbc", [128, SW], F32)
    c.alloc("rtf", [128, 64], F32)
    c.alloc("rtt", [128, 128], F32)
    c.alloc("rpc", [128, 8], F32)
    c.alloc("lsm", [128, 320], BF16)
    c.alloc("bvrow", [1, 1024], BF16)
    c.alloc("gbrow", [1, 8], BF16)
    c.alloc("ones_bf", [128, 1], BF16)
    c.alloc("ones_f", [128, 1], F32)
    c.alloc("onesrow_bf", [1, 128], BF16)
    c.alloc("onesrow_f", [1, 128], F32)
    c.alloc("epscol", [1, 1], F32)
    c.alloc("khat2", [1, H], F32)
    c.alloc("bhat", [1, H], F32)
    c.alloc("identf", [128, 128], F32)
    c.alloc("psA0", [128, SW], F32, psum=True)
    c.alloc("psA1", [128, SW], F32, psum=True)
    c.alloc("psS0", [128, SW], F32, psum=True)
    c.alloc("psS1", [128, SW], F32, psum=True)
    c.alloc("psPV", [128, SW], F32, psum=True)
    c.alloc("psB0", [128, SW], F32, psum=True)
    c.alloc("psB1", [128, SW], F32, psum=True)
    c.alloc("psR", [128, SW], F32, psum=True)

    c.memset(c.sb["ones_bf"][:, :], 1.0, w=["ones"])
    c.memset(c.sb["ones_f"][:, :], 1.0, pw=["ones"])
    c.memset(c.sb["onesrow_bf"][:, :], 1.0, pw=["ones"])
    c.memset(c.sb["onesrow_f"][:, :], 1.0, pw=["ones"])
    c.memset(c.sb["epscol"][:, :], EPS, pw=["ones"])
    c.memset(c.sb["ksbhA"][64:128, :], 0.0, w=["konesA"])
    c.memset(c.sb["ksbhB"][64:128, :], 0.0, w=["konesB"])
    c.memset(c.sb["ksbhA"][64:65, :], 1.0, pw=["konesA"])
    c.memset(c.sb["ksbhB"][64:65, :], 1.0, pw=["konesB"])
    c.memset(c.sb["qsbA"][64:128, :], 0.0, w=["qzA"])
    c.memset(c.sb["qsbB"][64:128, :], 0.0, w=["qzB"])
    ii = c.sb["tf1"]
    c.iota(ii[:, 0:128], [[1, 128]], 0, -1, w=["tf1"])
    c.ts(c.sb["identf"][:, :], ii[:, 0:128], 0.0, OP.is_equal, r=["tf1"], w=["identf"])
    for j in range(4):
        c.iota(ii[:, :], [[1, SW]], -128 * j, -1, w=["tf1"])
        mk = c.sb["Msb"][:, j * SW:(j + 1) * SW]
        tf = c.sb["tf0"]
        c.ts(tf[:, :], ii[:, :], 0.0, OP.is_ge, r=["tf1"], w=["tf0"])
        mkeys = [f"Msb:{i}" for i in range(j * SW // 1120, ((j + 1) * SW - 1) // 1120 + 1)]
        c.ts(mk, tf[:, :], 1.0, OP.subtract, -NEGBIG, OP.mult, r=["tf0"], pw=mkeys)


FIN_ROWS = {0: ("rw1", "rw2"), 1: ("rw3", "rw4")}


def _rows_finish(c, s, rms, slot):
    base = 0 if slot == 0 else 64
    skey = f"psR{slot}"
    psR = c.sb["psR"]
    mk, rk = FIN_ROWS[slot]
    rm, rr = c.sb[mk], c.sb[rk]
    if rms:
        c.act(rr[:, :], psR[base + 32:base + 33, :], AF.Ln, bias=c.sb["epscol"][0:1, :],
              scale=1.0 / D, r=[skey, "ones"], w=[rk])
        c.act(rr[:, :], rr[:, :], AF.Exp, scale=-0.5, r=[rk], w=[rk])
        c.memset(rm[:, :], 0.0, w=[mk])
    else:
        c.ts(rm[:, :], psR[base:base + 1, :], -1.0 / D, OP.mult, r=[skey], w=[mk])
        c.tt(rr[:, :], rm[:, :], rm[:, :], OP.mult, r=[mk], w=[rk])
        c.ts(psR[base + 32:base + 33, :], psR[base + 32:base + 33, :], 1.0 / D,
             OP.mult, pw=[skey])
        c.tt(psR[base + 32:base + 33, :], psR[base + 32:base + 33, :],
             rr[:, :], OP.subtract, r=[rk], pw=[skey])
        c.act(rr[:, :], psR[base + 32:base + 33, :], AF.Ln,
              bias=c.sb["epscol"][0:1, :], r=[skey, "ones"], w=[rk])
        c.act(rr[:, :], rr[:, :], AF.Exp, scale=-0.5, r=[rk], w=[rk])


def _rows_store(c, rows_name, s, slot):
    rows = c.d(rows_name)
    cols = slice(s * SW, (s + 1) * SW)
    mk, rk = FIN_ROWS[slot]
    c.dma(rows[0:1, cols], c.sb[mk][:, :], eng="scalar", r=[mk],
          pw=[f"D:{rows_name}:{s}"])
    c.dma(rows[1:2, cols], c.sb[rk][:, :], eng="scalar", r=[rk],
          pw=[f"D:{rows_name}:{s}"])


def collect_chunk(c, src_ap, src_key, p, first, last, slot, rms=False, claim=False):
    base = 0 if slot == 0 else 64
    skey = f"psR{slot}"
    ob = c.sb["ones_bf"]
    x2 = c.sb[f"x2bf{p}"]
    if src_ap.dtype != BF16:
        xb = c.sb[f"xbf{p}"]
        c.copy(xb[:, :], src_ap, r=[src_key], w=[f"xbf{p}"])
        src_ap, src_key = xb[:, :], f"xbf{p}"
    wk = dict(w=[skey]) if claim else dict(pw=[skey])
    if not rms:
        c.mm(c.sb["psR"][base:base + 1, :], ob[:, :], src_ap,
             start=first, stop=last, r=[src_key, "ones"], **wk)
        wk = dict(pw=[skey])
    c.act(x2[:, :], src_ap, AF.Square, r=[src_key], w=[f"x2bf{p}"])
    c.mm(c.sb["psR"][base + 32:base + 33, :], ob[:, :], x2[:, :],
         start=first, stop=last, r=[f"x2bf{p}", "ones"],
         tp=(0, 96) if base == 64 else None, **wk)


def e_stats(c, src_name, rows_name, rms=False, slot=0):
    src = c.d(src_name)
    for s in range(NS):
        cols = slice(s * SW, (s + 1) * SW)
        for ic in range(ND):
            p = ic % 2
            xb = c.sb[f"xbf{p}"]
            c.dma(xb[:, :], src[ic * 128:(ic + 1) * 128, cols], eng="gpsimd",
                  r=[sk(src_name, ic, s)], w=[f"xbf{p}"])
            collect_chunk(c, xb[:, :], f"xbf{p}", p, ic == 0, ic == ND - 1, slot, rms,
                          claim=(s == 0 and ic == 0))
        _rows_finish(c, s, rms, slot)
        _rows_store(c, rows_name, s, slot)


BC_ROWS = {"A": ("rlA1", "rlA2"), "B": ("qrw1", "qrw2")}


def bcast_rows(c, rows_name, s, pair):
    rows = c.d(rows_name)
    cols = slice(s * SW, (s + 1) * SW)
    k1, k2 = BC_ROWS[pair]
    r1, r2 = c.sb[k1], c.sb[k2]
    c.dma(r1[:, :], rows[0:1, cols], eng="scalar", r=[f"D:{rows_name}:{s}"], w=[k1])
    c.dma(r2[:, :], rows[1:2, cols], eng="scalar", r=[f"D:{rows_name}:{s}"], w=[k2])
    orf = c.sb["onesrow_f"]
    bm, br = c.sb["bcm" + pair], c.sb["bcr" + pair]
    pb = "psB0" if pair == "A" else "psB1"
    c.mm(c.sb[pb][:, :], orf[0:1, :], r1[:, :], r=[k1, "ones"], w=[pb])
    c.copy(bm[:, :], c.sb[pb][:, :], r=[pb], w=["bcm" + pair])
    c.mm(c.sb[pb][:, :], orf[0:1, :], r2[:, :], r=[k2, "ones"], w=[pb])
    c.copy(br[:, :], c.sb[pb][:, :], r=[pb], w=["bcr" + pair])


def load_pcols(c, bufname, dram_name, scale=None, n=ND, off=0):
    p = c.d(dram_name)
    buf = c.sb[bufname]
    for ic in range(n):
        col = buf[:, ic:ic + 1]
        src = p.rearrange("(c p) -> c p", p=128)[off + ic:off + ic + 1, :].transpose([1, 0])
        c.dma(col, src, eng="scalar", r=["D:" + dram_name], pw=[bufname])
    if scale is not None:
        c.ts(buf[:, 0:n], buf[:, 0:n], scale, OP.mult, r=[bufname], w=[bufname])


def e_apply(c, src_name, rows_name, out_name, out_dt=BF16, gamma=None, beta=None,
            rmsw=None, resid_name=None, resid_dt=F32, collect_rows=None, slot=0,
            pair="A", pbuf="pcol"):
    src, out = c.d(src_name), c.d(out_name)
    resid = c.d(resid_name) if resid_name else None
    if gamma is not None:
        load_pcols(c, pbuf, gamma)
        load_pcols(c, "pcol2", beta)
    if rmsw is not None:
        load_pcols(c, pbuf, rmsw)
    bm, br = c.sb["bcm" + pair], c.sb["bcr" + pair]
    bmk, brk = "bcm" + pair, "bcr" + pair
    for s in range(NS):
        cols = slice(s * SW, (s + 1) * SW)
        bcast_rows(c, rows_name, s, pair)
        for ic in range(ND):
            p = ic % 2
            rr = slice(ic * 128, (ic + 1) * 128)
            xA, tA = c.sb[f"xsbA{p}"], c.sb[f"tf{p}"]
            kx, kt = f"xsbA{p}", f"tf{p}"
            c.dma(xA[:, :], src[rr, cols], r=[sk(src_name, ic, s)], w=[kx])
            c.tt(tA[:, :], xA[:, :], bm[:, :], OP.add, r=[kx, bmk], w=[kt])
            if gamma is not None:
                c.stt(tA[:, :], tA[:, :], c.sb[pbuf][:, ic:ic + 1], br[:, :],
                      OP.mult, OP.mult, r=[kt, pbuf, brk], w=[kt])
                c.ts(tA[:, :], tA[:, :], c.sb["pcol2"][:, ic:ic + 1], OP.add,
                     r=[kt, "pcol2"], w=[kt])
            elif rmsw is not None:
                c.stt(tA[:, :], tA[:, :], c.sb[pbuf][:, ic:ic + 1], br[:, :],
                      OP.mult, OP.mult, r=[kt, pbuf, brk], w=[kt])
            else:
                c.tt(tA[:, :], tA[:, :], br[:, :], OP.mult, r=[kt, brk], w=[kt])
            if resid is not None:
                if resid_dt == F32:
                    rb, rbk = c.sb[f"xsbB{p}"], f"xsbB{p}"
                else:
                    rb, rbk = c.sb[f"evC{p}"], f"evC{p}"
                c.dma(rb[:, :], resid[rr, cols], eng="gpsimd",
                      r=[sk(resid_name, ic, s)], w=[rbk])
                c.tt(tA[:, :], tA[:, :], rb[:, :], OP.add, r=[kt, rbk], w=[kt])
            if out_dt == F32:
                c.dma(out[rr, cols], tA[:, :], eng="scalar", r=[kt],
                      w=[sk(out_name, ic, s)])
            else:
                ev = c.sb[f"evB{p}"]
                c.copy(ev[:, :], tA[:, :], r=[kt], w=[f"evB{p}"])
                c.dma(out[rr, cols], ev[:, :], eng="scalar", r=[f"evB{p}"],
                      w=[sk(out_name, ic, s)])
            if collect_rows is not None:
                collect_chunk(c, tA[:, :], kt, p, ic == 0, ic == ND - 1, slot,
                              claim=(s == 0 and ic == 0))
        if collect_rows is not None:
            _rows_finish(c, s, False, slot)
            _rows_store(c, collect_rows, s, slot)


def slot_ap(c, slot):
    buf, off = SLOTS[slot]
    return c.sb[buf].ap()[:, off:off + 8192]


def load_w_slot(c, w_name, slot, w_slice=None):
    w = c.d(w_name)
    if w_slice is not None:
        w = w_slice
    c.dma(slot_ap(c, slot).rearrange("p (c j) -> p c j", c=ND),
          w.rearrange("(c p) j -> p c j", p=128), eng="gpsimd",
          r=["D:" + w_name], w=[slot])


def e_proj(c, w_name, src_name, dst_name, slot, bias=None, scale=None, out_dt=BF16,
           src_reader=None, collect_rows=None, collect_rms=False, cslot=0):
    src, dst = c.d(src_name), c.d(dst_name)
    wap = slot_ap(c, slot)
    load_w_slot(c, w_name, slot)
    if bias is not None:
        load_pcols(c, "gpc", bias, scale=scale)
    for s in range(NS):
        sp = s % 2
        nsb = c.sb["nsbA" if sp == 0 else "nsbB"]
        nkey = "nsbA" if sp == 0 else "nsbB"
        cols = slice(s * SW, (s + 1) * SW)
        rkeys = src_reader(s) if src_reader else srd(src_name, s)
        c.dma(nsb[:, :].rearrange("p (c t) -> p c t", c=ND),
              fm(src, cols), r=rkeys, w=[nkey])
        for oc in range(ND):
            p = oc % 2
            ps = c.sb[f"psA{p}"]
            for ic in range(ND):
                c.mm(ps[:, :], wap[:, ic * 1024 + oc * 128: ic * 1024 + (oc + 1) * 128],
                     nsb[:, ic * SW:(ic + 1) * SW], start=(ic == 0), stop=(ic == ND - 1),
                     r=[slot, nkey], w=[f"psA{p}"] if ic == 0 else (),
                     pw=[f"psA{p}"] if ic > 0 else ())
            if out_dt == BF16:
                ev, ekey = c.sb[f"gb{p}"], f"gb{p}"
            else:
                ev, ekey = c.sb[f"gf{p}"], f"gf{p}"
            if bias is not None:
                c.ts(ev[:, :], ps[:, :], scale if scale is not None else 1.0, OP.mult,
                     c.sb["gpc"][:, oc:oc + 1], OP.add, r=[f"psA{p}", "gpc"], w=[ekey])
            elif scale is not None:
                c.ts(ev[:, :], ps[:, :], scale, OP.mult, r=[f"psA{p}"], w=[ekey])
            else:
                c.copy(ev[:, :], ps[:, :], r=[f"psA{p}"], w=[ekey])
            c.dma(dst[oc * 128:(oc + 1) * 128, cols], ev[:, :], eng="scalar",
                  r=[ekey], w=[sk(dst_name, oc, s)])
            if collect_rows is not None:
                collect_chunk(c, ev[:, :], ekey, p, oc == 0, oc == ND - 1, cslot,
                              rms=collect_rms, claim=(s == 0 and oc == 0))
        if collect_rows is not None:
            _rows_finish(c, s, collect_rms, cslot)
            _rows_store(c, collect_rows, s, cslot)


def e_qk(c, w_name, src_name, dst_name, role, slot, bias=None, use_bhat=False):
    src, dst = c.d(src_name), c.d(dst_name)
    wap = slot_ap(c, slot)
    scale = 0.125 if role == "q" else None
    qext = c.d("qext") if role == "q" else None
    load_w_slot(c, w_name, slot)
    if bias is not None:
        load_pcols(c, "gpc", bias, scale=scale)
    if role == "k":
        c.memset(c.sb["khat2"][:, :], 0.0, w=["khat2"])
    ob = c.sb["ones_bf"]
    for s in range(NS):
        sp = s % 2
        nsb = c.sb["nsbA" if sp == 0 else "nsbB"]
        nkey = "nsbA" if sp == 0 else "nsbB"
        cols = slice(s * SW, (s + 1) * SW)
        c.dma(nsb[:, :].rearrange("p (c t) -> p c t", c=ND), fm(src, cols),
              r=srd(src_name, s), w=[nkey])
        for oc in range(ND):
            p = oc % 2
            ps = c.sb[f"psA{p}"]
            for ic in range(ND):
                c.mm(ps[:, :], wap[:, ic * 1024 + oc * 128: ic * 1024 + (oc + 1) * 128],
                     nsb[:, ic * SW:(ic + 1) * SW], start=(ic == 0), stop=(ic == ND - 1),
                     r=[slot, nkey], w=[f"psA{p}"] if ic == 0 else (),
                     pw=[f"psA{p}"] if ic > 0 else ())
            kb = c.sb[f"gb{p}"]
            if bias is not None:
                c.ts(kb[:, :], ps[:, :], scale if scale else 1.0, OP.mult,
                     c.sb["gpc"][:, oc:oc + 1], OP.add, r=[f"psA{p}", "gpc"], w=[f"gb{p}"])
            elif scale is not None:
                c.ts(kb[:, :], ps[:, :], scale, OP.mult, r=[f"psA{p}"], w=[f"gb{p}"])
            else:
                c.copy(kb[:, :], ps[:, :], r=[f"psA{p}"], w=[f"gb{p}"])
            c.dma(dst[oc * 128:(oc + 1) * 128, cols], kb[:, :], eng="scalar",
                  r=[f"gb{p}"], w=[sk(dst_name, oc, s)])
            k2 = c.sb["PtA" if p == 0 else "PtB"]
            k2key = "PtA" if p == 0 else "PtB"
            c.act(k2[:, :], kb[:, :], AF.Square, r=[f"gb{p}"], w=[k2key])
            for g in range(2):
                hh = 2 * oc + g
                psr = c.sb["psS0" if g == 0 else "psS1"]
                pk = "psS0" if g == 0 else "psS1"
                c.mm(psr[0:1, :], ob[64 * g:64 * g + 64, :], k2[64 * g:64 * g + 64, :],
                     r=[k2key, "ones"], w=[pk])
                if role == "k":
                    c.rmax(c.sb["qrw2"][0:1, 0:1], psr[0:1, :], r=[pk], w=["qrw2"])
                    c.tt(c.sb["khat2"][0:1, hh:hh + 1], c.sb["khat2"][0:1, hh:hh + 1],
                         c.sb["qrw2"][0:1, 0:1], OP.max, r=["khat2", "qrw2"], w=["khat2"])
                else:
                    r1 = c.sb["qrw1"]
                    c.ts(r1[:, :], psr[0:1, :], c.sb["khat2"][0:1, hh:hh + 1], OP.mult,
                         r=[pk, "khat2"], w=["qrw1"])
                    c.act(r1[:, :], r1[:, :], AF.Sqrt, r=["qrw1"], w=["qrw1"])
                    if use_bhat:
                        c.ts(r1[:, :], r1[:, :], c.sb["bhat"][0:1, hh:hh + 1], OP.add,
                             r=["qrw1", "bhat"], w=["qrw1"])
                    c.ts(r1[:, :], r1[:, :], -1.0, OP.mult, CEXP, OP.add,
                         r=["qrw1"], w=["qrw1"])
                    c.dma(qext[hh:hh + 1, cols], r1[:, :], eng="gpsimd", r=["qrw1"],
                          w=[f"D:qext:{hh}:{s}"])


def e_v(c, w_name, src_name, slot, bias=None):
    src, vtok = c.d(src_name), c.d("vtok")
    wap = slot_ap(c, slot)
    load_w_slot(c, w_name, slot)
    if bias is not None:
        b = c.d(bias)
        row = bass_rust.AP(tensor=b.tensor, offset=0, ap=[[D, 1], [1, D]])
        c.dma(c.sb["bvrow"][0:1, :], row, eng="gpsimd", r=["D:" + bias], w=["bvrow"])
    orb = c.sb["onesrow_bf"]
    for nm in ("vsb2A", "vsb2B"):
        vv = c.sb[nm].ap().rearrange("p (h m) -> p h m", m=65)
        c.memset(vv[:, :, 64:65], 1.0, w=[nm + ":ones"])
    for s in range(NS):
        sp = s % 2
        nsb = c.sb["nsbA" if sp == 0 else "nsbB"]
        nkey = "nsbA" if sp == 0 else "nsbB"
        cols = slice(s * SW, (s + 1) * SW)
        c.dma(nsb[:, :].rearrange("p (c t) -> p c t", c=ND), fm(src, cols),
              r=srd(src_name, s), w=[nkey])
        for tt in range(SW // 128):
            vp = tt % 2
            vsb2 = c.sb["vsb2A" if vp == 0 else "vsb2B"]
            vkey = "vsb2A" if vp == 0 else "vsb2B"
            vv = vsb2.ap().rearrange("p (h m) -> p h m", m=65)
            for jh in range(2):
                ps = c.sb[f"psA{jh}"]
                for ic in range(ND):
                    c.mm(ps[:, :], nsb[:, ic * SW + tt * 128: ic * SW + (tt + 1) * 128],
                         wap[:, ic * 1024 + jh * 512: ic * 1024 + (jh + 1) * 512],
                         start=(ic == 0), stop=(ic == ND - 1) and bias is None,
                         r=[nkey, slot], w=[f"psA{jh}"] if ic == 0 else (),
                         pw=[f"psA{jh}"] if ic > 0 else ())
                if bias is not None:
                    c.mm(ps[:, :], orb[0:1, :],
                         c.sb["bvrow"][0:1, jh * 512:(jh + 1) * 512],
                         start=False, stop=True,
                         r=["ones", "bvrow"], pw=[f"psA{jh}"])
                dstv = vv[:, jh * 8:(jh + 1) * 8, 0:64]
                srcv = ps.ap().rearrange("p (hh m) -> p hh m", m=64)
                c.copy(dstv, srcv, r=[f"psA{jh}"], pw=[vkey])
            t0 = s * SW + tt * 128
            c.dma(vtok[t0:t0 + 128, :], vsb2[:, :], eng="scalar",
                  r=[vkey, vkey + ":ones"], w=[f"D:vtok:{s * 4 + tt}"])


def e_attn(c, use_bias=False):
    qT, kT, vtok, qext, oT = c.d("qT"), c.d("kT"), c.d("vtok"), c.d("qext"), c.d("oT")
    masks = c.sb["Msb"]
    MCH = 1120

    def msb_keys(lo, hi):
        return [f"Msb:{i}" for i in range(lo // MCH, (hi - 1) // MCH + 1)]
    if use_bias:
        stripes = c.d("L1_bias_stripes")
        Msb = c.sb["Msb"]

    def q_load(h, s):
        cols = slice(s * SW, (s + 1) * SW)
        sp = s % 2
        qsb = c.sb["qsbA" if sp == 0 else "qsbB"]
        qk = "qsbA" if sp == 0 else "qsbB"
        qz = "qzA" if sp == 0 else "qzB"
        c.dma(qsb[0:64, :], qT[64 * h:64 * h + 64, cols],
              r=[sk("qT", h // 2, s)], pw=[qk + ":q"])
        c.dma(qsb[64:65, :], qext[h:h + 1, cols],
              r=[f"D:qext:{h}:{s}", qz], pw=[qk + ":x"])

    tiles = [(h, s) for h in range(H) for s in range(NS)]
    pend = None
    SB = ["psS0", "psS1", "psA0", "psA1"]
    PT = ["PtA", "PtB", "gc0", "gc1"]

    def flush_epilogue():
        nonlocal pend
        if pend is None:
            return
        eh, es, ek = pend
        cols = slice(es * SW, (es + 1) * SW)
        c.mm(c.sb["psB0"][0:64, :], c.sb["onesrow_f"][0:1, 0:64], c.sb["qrw1"][:, :],
             r=["qrw1", "ones"], w=["psB0"])
        c.copy(c.sb["rbc"][0:64, :], c.sb["psB0"][0:64, :], r=["psB0"], w=["rbc"])
        ev = c.sb[ek]
        c.tt(ev[0:64, :], c.sb["psPV"][0:64, :], c.sb["rbc"][0:64, :], OP.mult,
             r=["psPV", "rbc"], w=[ek])
        c.dma(oT[64 * eh:64 * eh + 64, cols], ev[0:64, :], eng="scalar", r=[ek],
              w=[f"D:oT:{eh}:{es}"])
        pend = None

    for idx, (h, s) in enumerate(tiles):
        hp = h % 2
        ksbh = c.sb["ksbhA" if hp == 0 else "ksbhB"]
        vsbh = c.sb["vsbhA" if hp == 0 else "vsbhB"]
        kk = "ksbhA" if hp == 0 else "ksbhB"
        vk = "vsbhA" if hp == 0 else "vsbhB"
        konk = "konesA" if hp == 0 else "konesB"
        if s == 0:
            c.dma(ksbh[0:64, :], kT[64 * h:64 * h + 64, :], eng="gpsimd",
                  r=[sk("kT", h // 2, s2) for s2 in range(NS)], w=[kk])
            c.dma(vsbh[:, :].rearrange("p (k m) -> p k m", m=65),
                  vtok.rearrange("(k p) m -> p k m", p=128)[:, :, 65 * h:65 * h + 65],
                  eng="gpsimd", r=[f"D:vtok:{t}" for t in range(NT)], w=[vk])
            if use_bias:
                for i in range(4):
                    c.dma(Msb[:, i * MCH:(i + 1) * MCH],
                          stripes[h, :, i * MCH:(i + 1) * MCH], eng="gpsimd",
                          r=["D:L1_bias_stripes"], w=[f"Msb:{i}"])
        if idx == 0:
            q_load(h, s)
        if idx + 1 < len(tiles):
            q_load(*tiles[idx + 1])
        q0 = s * SW
        cols = slice(q0, q0 + SW)
        sp = s % 2
        qsb = c.sb["qsbA" if sp == 0 else "qsbB"]
        qk = "qsbA" if sp == 0 else "qsbB"
        qz = "qzA" if sp == 0 else "qzB"
        nk = 4 * (s + 1)

        def emit_S(k):
            bk = SB[k % 4]
            c.mm(c.sb[bk][:, :], ksbh[0:128, 128 * k:128 * (k + 1)], qsb[0:128, :],
                 start=True, stop=True,
                 r=[kk, konk, qk + ":q", qk + ":x", qz], w=[bk])
        for k in range(min(4, nk)):
            emit_S(k)
        flush_epilogue()
        for k in range(nk):
            bk = SB[k % 4]
            ps = c.sb[bk]
            if use_bias:
                y0 = q0 - 128 * k + 384
                c.tt(ps[:, :], ps[:, :], Msb[:, y0:y0 + SW], OP.add,
                     r=[bk] + msb_keys(y0, y0 + SW), w=[bk])
            elif k >= 4 * s:
                j = k - 4 * s
                c.tt(ps[:, :], ps[:, :], masks[:, j * SW:(j + 1) * SW], OP.add,
                     r=[bk] + msb_keys(j * SW, (j + 1) * SW), w=[bk])
            pk = PT[k % 4]
            Pt = c.sb[pk]
            c.act(Pt[:, :], ps[:, :], AF.Exp, r=[bk], w=[pk])
            if k + 4 < nk:
                emit_S(k + 4)
            c.mm(c.sb["psPV"][0:65, :], vsbh[:, 65 * k:65 * (k + 1)], Pt[:, :],
                 start=(k == 0), stop=(k == nk - 1),
                 r=[vk, pk], w=["psPV"] if k == 0 else (),
                 pw=["psPV"] if k > 0 else ())
        c.recip(c.sb["qrw1"][:, :], c.sb["psPV"][64:65, :], r=["psPV"], w=["qrw1"])
        pend = (h, s, "gb0" if sp == 0 else "gb1")
    flush_epilogue()


def oT_reader(s):
    return [f"D:oT:{h}:{s}" for h in range(H)]


def e_router(c, src_name):
    n, rw, wrowT = c.d(src_name), c.d("F0_router"), c.d("wrowT")
    rsb = c.sb["gsb"]
    c.dma(rsb[:, 0:64].rearrange("p (c j) -> p c j", c=ND),
          rw.rearrange("(c p) j -> p c j", p=128), eng="gpsimd",
          r=["D:F0_router"], w=["gsb"])
    lg = c.sb["rtf"]
    pc = c.sb["rpc"]
    for s in range(NS):
        sp = s % 2
        nsb = c.sb["nsbA" if sp == 0 else "nsbB"]
        nkey = "nsbA" if sp == 0 else "nsbB"
        cols = slice(s * SW, (s + 1) * SW)
        c.dma(nsb[:, :].rearrange("p (c t) -> p c t", c=ND), fm(n, cols),
              r=srd(src_name, s), w=[nkey])
        for tt in range(SW // 128):
            tp = tt % 2
            ps = c.sb[f"psA{tp}"]
            for ic in range(ND):
                c.mm(ps[:, 0:8], nsb[:, ic * SW + tt * 128: ic * SW + (tt + 1) * 128],
                     rsb[:, ic * 8:(ic + 1) * 8], start=(ic == 0), stop=(ic == ND - 1),
                     r=[nkey, "gsb"], w=[f"psA{tp}"] if ic == 0 else (),
                     pw=[f"psA{tp}"] if ic > 0 else ())
            c.rmax(pc[:, 0:1], ps[:, 0:8], negate=True, r=[f"psA{tp}"], w=["rpc"])
            c.act(lg[:, 0:8], ps[:, 0:8], AF.Exp, bias=pc[:, 0:1],
                  r=[f"psA{tp}", "rpc"], w=["rtf"])
            c.rsum(pc[:, 1:2], lg[:, 0:8], r=["rtf"], pw=["rpc"])
            c.recip(pc[:, 2:3], pc[:, 1:2], r=["rpc"], pw=["rpc"])
            c.ts(lg[:, 0:8], lg[:, 0:8], pc[:, 2:3], OP.mult, r=["rtf", "rpc"], w=["rtf"])
            c.rmax(pc[:, 3:4], lg[:, 0:8], r=["rtf"], pw=["rpc"])
            c.ts(lg[:, 8:16], lg[:, 0:8], pc[:, 3:4], OP.is_equal, r=["rtf", "rpc"], w=["rtf:m1"])
            c.ts(lg[:, 16:24], lg[:, 8:16], -1.0, OP.mult, 1.0, OP.add, r=["rtf:m1"], w=["rtf:nm"])
            c.tt(lg[:, 16:24], lg[:, 0:8], lg[:, 16:24], OP.mult, r=["rtf", "rtf:nm"], w=["rtf:nm"])
            c.rmax(pc[:, 4:5], lg[:, 16:24], r=["rtf:nm"], pw=["rpc"])
            c.ts(lg[:, 16:24], lg[:, 16:24], pc[:, 4:5], OP.is_equal, r=["rtf:nm", "rpc"], w=["rtf:nm"])
            c.ts(lg[:, 8:16], lg[:, 8:16], pc[:, 3:4], OP.mult, r=["rtf:m1", "rpc"], w=["rtf:m1"])
            c.ts(lg[:, 16:24], lg[:, 16:24], pc[:, 4:5], OP.mult, r=["rtf:nm", "rpc"], w=["rtf:nm"])
            c.tt(lg[:, 8:16], lg[:, 8:16], lg[:, 16:24], OP.add, r=["rtf:m1", "rtf:nm"], w=["rtf:m1"])
            c.transpose(c.sb["psPV"][0:8, 0:128], lg[:, 8:16], c.sb["identf"][:, :],
                        r=["rtf:m1", "identf"], w=["psPV"])
            c.copy(c.sb["rtt"][0:8, 0:128], c.sb["psPV"][0:8, 0:128], r=["psPV"], w=["rtt"])
            t0 = s * SW + tt * 128
            c.dma(wrowT[0:8, t0:t0 + 128], c.sb["rtt"][0:8, 0:128], eng="scalar",
                  r=["rtt"], pw=[f"D:wrowT:{s}"])


def e_moe(c, src_name, collect_rows=None, cslot=0):
    n, moe32, wrowT = c.d(src_name), c.d("moe32"), c.d("wrowT")
    ew1, ew2, ew3 = c.d("F0_ew1"), c.d("F0_ew2"), c.d("F0_ew3")
    for e_i in range(E):
        s1slot = "A0" if e_i % 2 == 0 else "B1"
        s3slot = "A1" if e_i % 2 == 0 else "C0"
        s2slot = "B0" if e_i % 2 == 0 else "A0"
        load_w_slot(c, "F0_ew1", s1slot, w_slice=ew1[e_i])
        load_w_slot(c, "F0_ew3", s3slot, w_slice=ew3[e_i])
        load_w_slot(c, "F0_ew2", s2slot, w_slice=ew2[e_i])
        w1b, w3b, w2b = slot_ap(c, s1slot), slot_ap(c, s3slot), slot_ap(c, s2slot)
        for s in range(NS):
            sp = s % 2
            nsb = c.sb["nsbA" if sp == 0 else "nsbB"]
            nkey = "nsbA" if sp == 0 else "nsbB"
            cols = slice(s * SW, (s + 1) * SW)
            c.dma(nsb[:, :].rearrange("p (c t) -> p c t", c=ND), fm(n, cols),
                  eng="scalar", r=srd(src_name, s), w=[nkey])
            c.dma(c.sb["qrw2"][:, :], wrowT[e_i:e_i + 1, cols], eng="scalar",
                  r=[f"D:wrowT:{s}"], w=["qrw2"])
            c.mm(c.sb["psB0"][:, :], c.sb["onesrow_f"][0:1, :], c.sb["qrw2"][:, :],
                 r=["qrw2", "ones"], w=["psB0"])
            c.copy(c.sb["rbc"][:, :], c.sb["psB0"][:, :], r=["psB0"], w=["rbc"])
            for eh in range(ND):
                p = eh % 2
                psu = c.sb[f"psA{p}"]
                for ic in range(ND):
                    c.mm(psu[:, :], w1b[:, ic * 1024 + eh * 128: ic * 1024 + (eh + 1) * 128],
                         nsb[:, ic * SW:(ic + 1) * SW], start=(ic == 0), stop=(ic == ND - 1),
                         r=[s1slot, nkey], w=[f"psA{p}"] if ic == 0 else (),
                         pw=[f"psA{p}"] if ic > 0 else ())
                s1 = c.sb[f"gb{p}"]
                c.act(s1[:, :], psu[:, :], AF.Silu, r=[f"psA{p}"], w=[f"gb{p}"])
                psv = c.sb[f"psB{p}"]
                for ic in range(ND):
                    c.mm(psv[:, :], w3b[:, ic * 1024 + eh * 128: ic * 1024 + (eh + 1) * 128],
                         nsb[:, ic * SW:(ic + 1) * SW], start=(ic == 0), stop=(ic == ND - 1),
                         r=[s3slot, nkey], w=[f"psB{p}"] if ic == 0 else (),
                         pw=[f"psB{p}"] if ic > 0 else ())
                gslice = c.sb["gsb"][:, eh * SW:(eh + 1) * SW]
                c.tt(gslice, s1[:, :], psv[:, :], OP.mult,
                     r=[f"gb{p}", f"psB{p}"], pw=["gsb"])
                c.tt(gslice, gslice, c.sb["rbc"][:, :], OP.mult,
                     r=["gsb", "rbc"], pw=["gsb"])
            for dc in range(ND):
                p = dc % 2
                psu = c.sb[f"psA{p}"]
                for eh in range(ND):
                    c.mm(psu[:, :], w2b[:, eh * 1024 + dc * 128: eh * 1024 + (dc + 1) * 128],
                         c.sb["gsb"][:, eh * SW:(eh + 1) * SW], start=(eh == 0),
                         stop=(eh == ND - 1), r=[s2slot, "gsb"],
                         w=[f"psA{p}"] if eh == 0 else (), pw=[f"psA{p}"] if eh > 0 else ())
                ev = c.sb[f"gf{p}"]
                mk = sk("moe32", dc, s)
                if e_i == 0:
                    c.copy(ev[:, :], psu[:, :], r=[f"psA{p}"], w=[f"gf{p}"])
                else:
                    xb = c.sb[f"xsbB{p}"]
                    c.dma(xb[:, :], moe32[dc * 128:(dc + 1) * 128, cols],
                          r=[mk], w=[f"xsbB{p}"])
                    c.tt(ev[:, :], psu[:, :], xb[:, :], OP.add,
                         r=[f"psA{p}", f"xsbB{p}"], w=[f"gf{p}"])
                c.dma(moe32[dc * 128:(dc + 1) * 128, cols], ev[:, :], eng="gpsimd",
                      r=[f"gf{p}"], w=[mk])
                if collect_rows is not None and e_i == E - 1:
                    collect_chunk(c, ev[:, :], f"gf{p}", p, dc == 0, dc == ND - 1,
                                  cslot, rms=True, claim=(s == 0 and dc == 0))
            if collect_rows is not None and e_i == E - 1:
                _rows_finish(c, s, True, cslot)
                _rows_store(c, collect_rows, s, cslot)


def e_glu(c):
    h, w1, w3, g = c.d("h"), c.d("F0_w1"), c.d("F0_w3"), c.d("g")
    slot_rot = [("A0", "A1"), ("B0", "B1"), ("C0", "A0"), ("A1", "B0")]
    for it, og in enumerate(range(0, FF, 1024)):
        s1slot, s3slot = slot_rot[it]
        load_w_slot(c, "F0_w1", s1slot, w_slice=w1[:, og:og + 1024])
        load_w_slot(c, "F0_w3", s3slot, w_slice=w3[:, og:og + 1024])
        wap1, wap3 = slot_ap(c, s1slot), slot_ap(c, s3slot)
        for s in range(NS):
            sp = s % 2
            nsb = c.sb["nsbA" if sp == 0 else "nsbB"]
            nkey = "nsbA" if sp == 0 else "nsbB"
            cols = slice(s * SW, (s + 1) * SW)
            c.dma(nsb[:, :].rearrange("p (c t) -> p c t", c=ND), fm(h, cols),
                  r=srd("h", s), w=[nkey])
            for j in range(8):
                p = j % 2
                psu, psv = c.sb[f"psA{p}"], c.sb[f"psB{p}"]
                for ic in range(ND):
                    c.mm(psu[:, :], wap1[:, ic * 1024 + j * 128: ic * 1024 + (j + 1) * 128],
                         nsb[:, ic * SW:(ic + 1) * SW], start=(ic == 0), stop=(ic == ND - 1),
                         r=[s1slot, nkey], w=[f"psA{p}"] if ic == 0 else (),
                         pw=[f"psA{p}"] if ic > 0 else ())
                s1 = c.sb[f"gb{p}"]
                c.act(s1[:, :], psu[:, :], AF.Silu, r=[f"psA{p}"], w=[f"gb{p}"])
                for ic in range(ND):
                    c.mm(psv[:, :], wap3[:, ic * 1024 + j * 128: ic * 1024 + (j + 1) * 128],
                         nsb[:, ic * SW:(ic + 1) * SW], start=(ic == 0), stop=(ic == ND - 1),
                         r=[s3slot, nkey], w=[f"psB{p}"] if ic == 0 else (),
                         pw=[f"psB{p}"] if ic > 0 else ())
                ev = c.sb[f"gc{p}"]
                c.tt(ev[:, :], s1[:, :], psv[:, :], OP.mult,
                     r=[f"gb{p}", f"psB{p}"], w=[f"gc{p}"])
                jg = og // 128 + j
                c.dma(g[og + j * 128: og + (j + 1) * 128, cols], ev[:, :], eng="scalar",
                      r=[f"gc{p}"], w=[f"D:g:{jg}:{s}"])


def e_projK32(c, w_name, src_name, dst_name, bias=None, collect_rows=None,
              collect_rms=False, cslot=0):
    w, src, dst = c.d(w_name), c.d(src_name), c.d(dst_name)
    ssb = c.sb["ssb32"]
    if bias is not None:
        load_pcols(c, "gpc", bias)
    c.dma(c.sb["wsb"][:, :].rearrange("p (c j) -> p c j", c=16),
          w[0:2048, :].rearrange("(c p) j -> p c j", p=128), eng="gpsimd",
          r=["D:" + w_name], w=["A0", "A1"])
    c.dma(c.sb["wsb2"][:, :].rearrange("p (c j) -> p c j", c=16),
          w[2048:4096, :].rearrange("(c p) j -> p c j", p=128), eng="gpsimd",
          r=["D:" + w_name], w=["B0", "B1"])
    for s in range(NS):
        cols = slice(s * SW, (s + 1) * SW)
        c.dma(ssb[:, :].rearrange("p (c t) -> p c t", c=16),
              fm(src[0:2048, :], cols),
              r=[f"D:{src_name}:{jg}:{s}" for jg in range(16)], w=["C0"])
        c.dma(c.sb["nsbA"][:, :].rearrange("p (c t) -> p c t", c=8),
              fm(src[2048:3072, :], cols), eng="gpsimd",
              r=[f"D:{src_name}:{jg}:{s}" for jg in range(16, 24)], w=["nsbA"])
        c.dma(c.sb["nsbB"][:, :].rearrange("p (c t) -> p c t", c=8),
              fm(src[3072:4096, :], cols),
              r=[f"D:{src_name}:{jg}:{s}" for jg in range(24, 32)], w=["nsbB"])
        for s_ in [s]:
            pass
        for oc in range(ND):
            p = oc % 2
            ps = c.sb[f"psA{p}"]
            for ic in range(32):
                wt = (c.sb["wsb"] if ic < 16 else c.sb["wsb2"])[
                    :, (ic % 16) * 1024 + oc * 128:(ic % 16) * 1024 + (oc + 1) * 128]
                rslot = ("A0" if ic < 8 else "A1") if ic < 16 else ("B0" if ic < 24 else "B1")
                if ic < 16:
                    act = ssb[:, ic * SW:(ic + 1) * SW]
                    akey = "C0"
                elif ic < 24:
                    act = c.sb["nsbA"][:, (ic - 16) * SW:(ic - 15) * SW]
                    akey = "nsbA"
                else:
                    act = c.sb["nsbB"][:, (ic - 24) * SW:(ic - 23) * SW]
                    akey = "nsbB"
                c.mm(ps[:, :], wt, act,
                     start=(ic == 0), stop=(ic == 31),
                     r=[rslot, akey],
                     w=[f"psA{p}"] if ic == 0 else (), pw=[f"psA{p}"] if ic > 0 else ())
            ev = c.sb[f"gf{p}"]
            if bias is not None:
                c.ts(ev[:, :], ps[:, :], 1.0, OP.mult, c.sb["gpc"][:, oc:oc + 1], OP.add,
                     r=[f"psA{p}", "gpc"], w=[f"gf{p}"])
            else:
                c.copy(ev[:, :], ps[:, :], r=[f"psA{p}"], w=[f"gf{p}"])
            c.dma(dst[oc * 128:(oc + 1) * 128, cols], ev[:, :], eng="scalar",
                  r=[f"gf{p}"], w=[sk(dst_name, oc, s)])
            if collect_rows is not None:
                collect_chunk(c, ev[:, :], f"gf{p}", p, oc == 0, oc == ND - 1, cslot,
                              rms=collect_rms, claim=(s == 0 and oc == 0))
        if collect_rows is not None:
            _rows_finish(c, s, collect_rms, cslot)
            _rows_store(c, collect_rows, s, cslot)


def e_l1ffn(c, src_name):
    n, g = c.d(src_name), c.d("g")
    gateT, gate_b = c.d("G1_gateT"), c.d("G1_gate_b")
    AcatT, BcatT = c.d("G1_AcatT"), c.d("G1_BcatT")
    wiT = c.d("G1_wiT")
    small = c.sb["lsm"]
    c.dma(small[:, 0:64].rearrange("p (c j) -> p c j", c=ND),
          gateT.rearrange("(c p) j -> p c j", p=128), eng="gpsimd",
          r=["D:G1_gateT"], pw=["lsm"])
    c.dma(small[:, 64:64 + 256].rearrange("p (c j) -> p c j", c=ND),
          AcatT.rearrange("(c p) j -> p c j", p=128), eng="gpsimd",
          r=["D:G1_AcatT"], pw=["lsm"])
    gb_row = bass_rust.AP(tensor=gate_b.tensor, offset=0, ap=[[E, 1], [1, E]])
    c.dma(c.sb["gbrow"][0:1, 0:8], gb_row, eng="gpsimd", r=["D:G1_gate_b"], w=["gbrow"])
    lg = c.sb["rtf"]
    pc = c.sb["rpc"]
    rmt = c.sb["gsb"]
    orb = c.sb["onesrow_bf"]
    for s in range(NS):
        sp = s % 2
        nsb = c.sb["nsbA" if sp == 0 else "nsbB"]
        nkey = "nsbA" if sp == 0 else "nsbB"
        cols = slice(s * SW, (s + 1) * SW)
        c.dma(nsb[:, :].rearrange("p (c t) -> p c t", c=ND), fm(n, cols),
              r=srd(src_name, s), w=[nkey])
        for tt in range(SW // 128):
            tp = tt % 2
            ps = c.sb[f"psA{tp}"]
            for ic in range(ND):
                c.mm(ps[:, 0:8], nsb[:, ic * SW + tt * 128: ic * SW + (tt + 1) * 128],
                     small[:, ic * 8:(ic + 1) * 8], start=(ic == 0), stop=False,
                     r=[nkey, "lsm"], w=[f"psA{tp}"] if ic == 0 else (),
                     pw=[f"psA{tp}"] if ic > 0 else ())
            c.mm(ps[:, 0:8], orb[0:1, :], c.sb["gbrow"][0:1, 0:8],
                 start=False, stop=True, r=["ones", "gbrow"], pw=[f"psA{tp}"])
            c.copy(lg[:, 0:8], ps[:, 0:8], r=[f"psA{tp}"], w=["rtf"])
            c.rmax(pc[:, 0:1], lg[:, 0:8], r=["rtf"], w=["rpc"])
            c.ts(lg[:, 8:16], lg[:, 0:8], pc[:, 0:1], OP.is_equal,
                 r=["rtf", "rpc"], w=["rtf:m1"])
            m32 = lg.ap()[:, 16:48].rearrange("p (e r2) -> p e r2", r2=R)
            m1v = lg.ap()[:, 8:16].rearrange("p (e r2) -> p e r2", r2=1)
            for r_ in range(R):
                c.copy(m32[:, :, r_:r_ + 1], m1v, r=["rtf:m1"], pw=["rtf:nm"])
            psb = c.sb[f"psS{tp}"]
            for ic in range(ND):
                c.mm(psb[:, 0:32], nsb[:, ic * SW + tt * 128: ic * SW + (tt + 1) * 128],
                     small[:, 64 + ic * 32:64 + (ic + 1) * 32], start=(ic == 0),
                     stop=(ic == ND - 1), r=[nkey, "lsm"],
                     w=[f"psS{tp}"] if ic == 0 else (), pw=[f"psS{tp}"] if ic > 0 else ())
            c.tt(c.sb["rtt"][:, 0:32], psb[:, 0:32], lg[:, 16:48], OP.mult,
                 r=[f"psS{tp}", "rtf:nm"], w=["rtt"])
            c.transpose(c.sb["psPV"][0:32, 0:128], c.sb["rtt"][:, 0:32],
                        c.sb["identf"][:, :], r=["rtt", "identf"], w=["psPV"])
            c.copy(rmt[0:32, s * SW + tt * 128: s * SW + (tt + 1) * 128],
                   c.sb["psPV"][0:32, 0:128], r=["psPV"], pw=["gsb"])
    bc_sb = c.sb["Msb"]
    c.dma(bc_sb[0:32, 0:FF], BcatT[:, :], eng="gpsimd", r=["D:G1_BcatT"],
          w=["Msb:0", "Msb:1", "Msb:2", "Msb:3"])
    wrot = ["A0", "B0", "C0", "A1"]
    for it, og in enumerate(range(0, FF, 1024)):
        wslot = wrot[it]
        wi = c.d("G1_wiT")
        c.dma(slot_ap(c, wslot).rearrange("p (c j) -> p c j", c=ND),
              wi[:, og:og + 1024].rearrange("(c p) j -> p c j", p=128), eng="gpsimd",
              r=["D:G1_wiT"], w=[wslot])
        wap = slot_ap(c, wslot)
        load_pcols(c, "gpc", "G1_wi_b", n=8, off=og // 128)
        for s in range(NS):
            sp = s % 2
            nsb = c.sb["nsbA" if sp == 0 else "nsbB"]
            nkey = "nsbA" if sp == 0 else "nsbB"
            cols = slice(s * SW, (s + 1) * SW)
            c.dma(nsb[:, :].rearrange("p (c t) -> p c t", c=ND), fm(n, cols),
                  r=srd(src_name, s), w=[nkey])
            for j in range(8):
                p = j % 2
                ps = c.sb[f"psA{p}"]
                c.mm(ps[:, :], bc_sb[0:32, og + j * 128: og + (j + 1) * 128],
                     rmt[0:32, cols], start=True, stop=False,
                     r=["Msb:0", "Msb:1", "Msb:2", "Msb:3", "gsb"], w=[f"psA{p}"])
                for ic in range(ND):
                    c.mm(ps[:, :], wap[:, ic * 1024 + j * 128: ic * 1024 + (j + 1) * 128],
                         nsb[:, ic * SW:(ic + 1) * SW], start=False, stop=(ic == ND - 1),
                         r=[wslot, nkey], pw=[f"psA{p}"])
                ev = c.sb[f"gb{p}"]
                c.act(ev[:, :], ps[:, :], AF.Relu, bias=c.sb["gpc"][:, j:j + 1],
                      r=[f"psA{p}", "gpc"], w=[f"gb{p}"])
                jg = og // 128 + j
                c.dma(g[og + j * 128: og + (j + 1) * 128, cols], ev[:, :], eng="scalar",
                      r=[f"gb{p}"], w=[f"D:g:{jg}:{s}"])
    e_projK32(c, "G1_woT", "g", "y32", bias="G1_wo_b", collect_rows="rowsA", cslot=1)


def e_bhat(c):
    stripes = c.d("L1_bias_stripes")
    c.memset(c.sb["bhat"][:, :], NEGBIG, w=["bhat"])
    r = c.sb["qrw1"]
    for h in range(H):
        for ch in range(5):
            w = min(SW, 2559 - ch * SW)
            c.dma(r[0:1, 0:w], stripes[h, 127:128, ch * SW:ch * SW + w],
                  eng="scalar", r=["D:L1_bias_stripes"], w=["qrw1"])
            c.rmax(c.sb["qrw2"][0:1, 0:1], r[0:1, 0:w], r=["qrw1"], w=["qrw2"])
            c.tt(c.sb["bhat"][0:1, h:h + 1], c.sb["bhat"][0:1, h:h + 1],
                 c.sb["qrw2"][0:1, 0:1], OP.max, r=["bhat", "qrw2"], w=["bhat"])


def ph_all(c):
    e_prep(c)
    c.alloc("rw1", [1, SW], F32)
    c.alloc("rw2", [1, SW], F32)
    c.alloc("rw3", [1, SW], F32)
    c.alloc("rw4", [1, SW], F32)
    e_stats(c, "xT", "rowsA", slot=0)
    e_apply(c, "xT", "rowsA", "n1", pair="A")
    e_qk(c, "L0_Wk", "n1", "kT", "k", "A0", bias="L0_bk")
    e_qk(c, "L0_Wq", "n1", "qT", "q", "A1", bias="L0_bq")
    e_v(c, "L0_Wv", "n1", "B0", bias="L0_bv")
    e_attn(c, use_bias=False)
    e_proj(c, "L0_Wo", "oT", "y32", "B1", bias="L0_bo", out_dt=F32,
           src_reader=oT_reader, collect_rows="rowsB", cslot=1)
    e_apply(c, "y32", "rowsB", "d32", out_dt=F32, gamma="L0_gamma", beta="L0_beta",
            collect_rows="rowsC", slot=0, pair="B")
    e_apply(c, "d32", "rowsC", "x32a", out_dt=F32, resid_name="xT",
            collect_rows="rowsD", slot=1, pair="A")
    e_apply(c, "x32a", "rowsD", "n2", pair="B")
    e_router(c, "n2")
    e_qk(c, "F0_wk", "n2", "kT", "k", "A0")
    e_qk(c, "F0_wq", "n2", "qT", "q", "A1")
    e_v(c, "F0_wv", "n2", "B0")
    e_attn(c, use_bias=False)
    e_proj(c, "F0_wo", "oT", "y32", "B1", out_dt=F32, src_reader=oT_reader,
           collect_rows="rowsE", collect_rms=True, cslot=0)
    e_apply(c, "y32", "rowsE", "h", out_dt=BF16, rmsw="F0_attn_nw",
            resid_name="n2", resid_dt=BF16, pair="A")
    e_glu(c)
    e_projK32(c, "F0_w2", "g", "y32", collect_rows="rowsF", collect_rms=True, cslot=1)
    e_apply(c, "y32", "rowsF", "d32", out_dt=F32, rmsw="F0_ff_nw",
            resid_name="h", resid_dt=BF16, pair="B")
    e_moe(c, "n2", collect_rows="rowsG", cslot=0)
    e_apply(c, "moe32", "rowsG", "y32", out_dt=F32, rmsw="F0_moe_nw",
            resid_name="d32", resid_dt=F32, collect_rows="rowsH", slot=1, pair="A")
    e_apply(c, "y32", "rowsH", "x32b", out_dt=F32, resid_name="x32a",
            collect_rows="rowsI", slot=0, pair="B")
    e_apply(c, "x32b", "rowsI", "n3", pair="A")
    e_bhat(c)
    e_qk(c, "L1_WKT", "n3", "kT", "k", "A0", bias="L1_bK")
    e_qk(c, "L1_WQT", "n3", "qT", "q", "A1", bias="L1_bQ", use_bhat=True)
    e_v(c, "L1_WVT", "n3", "B0", bias="L1_bV")
    e_attn(c, use_bias=True)
    e_proj(c, "L1_WOT", "oT", "y32", "B1", bias="L1_bO", out_dt=F32,
           src_reader=oT_reader, collect_rows="rowsJ", cslot=1)
    e_apply(c, "y32", "rowsJ", "x32c", out_dt=F32, resid_name="x32b",
            collect_rows="rowsK", slot=0, pair="B")
    e_apply(c, "x32c", "rowsK", "n4", pair="A")
    e_l1ffn(c, "n4")
    e_apply(c, "y32", "rowsA", "yT", out_dt=F32, resid_name="x32c", pair="B")


def _host_layout(inputs):
    g = {}
    g["xT"] = [np.ascontiguousarray(np.asarray(inputs["L0_attn_x"][i]).T) for i in range(NCORES)]
    def A(name):
        return np.ascontiguousarray(np.asarray(inputs[name], dtype=np.float32))
    shared = {
        "L0_Wq": A("L0_attn_Wq"), "L0_Wk": A("L0_attn_Wk"), "L0_Wv": A("L0_attn_Wv"),
        "L0_Wo": A("L0_attn_Wo"),
        "L0_bq": A("L0_attn_bq"), "L0_bk": A("L0_attn_bk"), "L0_bv": A("L0_attn_bv"),
        "L0_bo": A("L0_attn_bo"), "L0_gamma": A("L0_attn_gamma"), "L0_beta": A("L0_attn_beta"),
        "F0_wq": A("L0_ffn_wq"), "F0_wk": A("L0_ffn_wk"), "F0_wv": A("L0_ffn_wv"),
        "F0_wo": A("L0_ffn_wo"), "F0_w1": A("L0_ffn_w1"), "F0_w2": A("L0_ffn_w2"),
        "F0_w3": A("L0_ffn_w3"),
        "F0_attn_nw": A("L0_ffn_attn_norm_w"), "F0_ff_nw": A("L0_ffn_ff_norm_w"),
        "F0_moe_nw": A("L0_ffn_moe_norm_w"), "F0_router": A("L0_ffn_router_w"),
        "F0_ew1": A("L0_ffn_ew1"), "F0_ew2": A("L0_ffn_ew2"), "F0_ew3": A("L0_ffn_ew3"),
        "L1_WQT": np.ascontiguousarray(A("L1_attn_WQ").T),
        "L1_WKT": np.ascontiguousarray(A("L1_attn_WK").T),
        "L1_WVT": np.ascontiguousarray(A("L1_attn_WV").T),
        "L1_WOT": np.ascontiguousarray(A("L1_attn_WO").T),
        "L1_bQ": A("L1_attn_bQ"), "L1_bK": A("L1_attn_bK"), "L1_bV": A("L1_attn_bV"),
        "L1_bO": A("L1_attn_bO"),
        "G1_gateT": np.ascontiguousarray(A("L1_ffn_gate_W").T),
        "G1_gate_b": A("L1_ffn_gate_b"),
        "G1_wiT": np.ascontiguousarray(A("L1_ffn_wi_W").T),
        "G1_wi_b": A("L1_ffn_wi_b"),
        "G1_woT": np.ascontiguousarray(A("L1_ffn_wo_W").T),
        "G1_wo_b": A("L1_ffn_wo_b"),
        "G1_AcatT": np.ascontiguousarray(A("L1_ffn_lora_A").reshape(E * R, D).T),
        "G1_BcatT": np.ascontiguousarray(
            A("L1_ffn_lora_B").transpose(0, 2, 1).reshape(E * R, FF)),
    }
    tbl = A("L1_attn_rel_pos_table")
    yy = np.arange(MW)[None, :]
    pp = np.arange(128)[:, None]
    idx = np.minimum(1663 + yy - pp, 4094)
    stripes = tbl[idx, :]
    masked = (yy - 384 - pp) < 0
    stripes = np.where(masked[:, :, None], np.float32(MNEG), stripes)
    shared["L1_bias_stripes"] = np.ascontiguousarray(stripes.transpose(2, 0, 1))
    return g, shared


def kernel(**inputs):
    percore, shared = _host_layout(inputs)
    nc = bass.Bass("TRN2", target_bir_lowering=False)
    c = Ctx(nc, set(EXTERNAL_IN), set(EXTERNAL_OUT))
    try:
        ph_all(c)
        c.realize()
    finally:
        c.close()
    in_maps = []
    for i in range(NCORES):
        m = {}
        for rn in c.reads:
            a = percore["xT"][i] if rn == "xT" else shared[rn]
            m[rn] = _as_np(a, rn)
        in_maps.append(m)
    res = run_bass_kernel_spmd(nc, in_maps, core_ids=list(range(NCORES)))
    out = np.stack([np.ascontiguousarray(np.asarray(res.results[i]["yT"]).T)
                    for i in range(NCORES)])
    return out.astype(np.float32)


def _as_np(a, name):
    shape, dt = DRAM_SPECS[name]
    npdt = {F32: np.float32, BF16: NPBF, I32: np.int32}[dt]
    a = np.asarray(a)
    if a.dtype != npdt:
        a = a.astype(npdt)
    return np.ascontiguousarray(a.reshape(shape))
